# version 16
# baseline (speedup 1.0000x reference)
import math
import os
import numpy as np
import ml_dtypes
import concourse.bass as bass
import concourse.mybir as mybir
from concourse.alu_op_type import AluOpType as OP
from concourse.bass_utils import run_bass_kernel_spmd
from contextlib import ExitStack

F32 = mybir.dt.float32
BF16 = mybir.dt.bfloat16
I32 = mybir.dt.int32
AF = mybir.ActivationFunctionType
AX = mybir.AxisListType

D = 1024
NT = 4096
SEQ = 2048
NH = 16
FD = 16
HD = 64
CH = 256
NCHUNK = NT // CH
I_FFN = 2816
RMS_EPS = 1e-5
LN_EPS = 1e-5
QMAX = 127.0
S32 = math.sqrt(32.0)

ENGS = ("tensor", "vector", "scalar", "sync", "gpsimd")


class Prog:
    def __init__(self, nc):
        self.nc = nc
        self.items = {e: [] for e in ENGS}
        self.cnt = {e: 0 for e in ENGS}
        self.sem = {}
        self.last_wait = {}

    def alloc_sems(self, es):
        for e in ENGS:
            self.sem[e] = es.enter_context(self.nc.semaphore("sem_" + e))
        self._es = es
        self.streams = {}

    def stream(self, name, queue, depth):
        sems = [self._es.enter_context(self.nc.semaphore(f"dma_{name}_{i}"))
                for i in range(depth)]
        self.streams[name] = [queue, depth, sems, 0]
        return name

    def dma(self, name, fn):
        queue, depth, sems, k = self.streams[name]
        sem = sems[k % depth]
        self.streams[name][3] = k + 1
        self.items[queue].append((lambda h, f=fn, s=sem: f(h).then_inc(s, 16), 0))
        return (name, k)

    def wait_dma(self, eng, token):
        if token is None:
            return
        name, k = token
        queue, depth, sems, _ = self.streams[name]
        sem = sems[k % depth]
        val = 16 * (k // depth + 1)
        key = (eng, name, k % depth)
        if self.last_wait.get(key, -1) >= val:
            return
        self.last_wait[key] = val
        self.items[eng].append((lambda h, s=sem, v=val: h.wait_ge(s, v), 0))

    def wait_stream_total(self, eng, name):
        queue, depth, sems, k = self.streams[name]
        for i in range(depth):
            n_on = (k - i + depth - 1) // depth
            if n_on > 0:
                self.items[eng].append(
                    (lambda h, s=sems[i], v=16 * n_on: h.wait_ge(s, v), 0))

    def do(self, eng, fn, inc):
        self.cnt[eng] += inc
        self.items[eng].append((fn, inc))
        return self.cnt[eng]

    def raw(self, eng, fn):
        self.items[eng].append((fn, 0))

    def wait(self, eng, on_eng, val):
        if val is None or val <= 0:
            return
        assert val <= self.cnt[on_eng], f"{eng} waits {on_eng}>={val} cnt={self.cnt[on_eng]}"
        key = (eng, on_eng)
        if self.last_wait.get(key, -1) >= val:
            return
        self.last_wait[key] = val
        sem = self.sem[on_eng]
        self.items[eng].append((lambda h, s=sem, v=val: h.wait_ge(s, v), 0))

    def barrier(self):
        for name, (queue, depth, sems, k) in self.streams.items():
            if k > 0:
                self.wait_stream_total(queue, name)
        for e in ("sync", "gpsimd"):
            self.do(e, lambda h: h.nop(), 1)
        snap = dict(self.cnt)
        for e in ENGS:
            for o in ENGS:
                if e != o:
                    self.wait(e, o, snap[o])

    def emit_block(self, blk):
        for e in ENGS:
            def mk(engname):
                def _(h):
                    sem = self.sem[engname]
                    for fn, inc in self.items[engname]:
                        inst = fn(h)
                        if inc:
                            inst.then_inc(sem, inc)
                return _
            getattr(blk, e)(mk(e))


def build(nc):
    PHASELIM = int(os.environ.get("K_PHASELIM", "9"))
    es = ExitStack()
    p = Prog(nc)
    p.alloc_sems(es)
    sem_cc = es.enter_context(nc.semaphore("sem_cc"))
    cc_state = [0]

    def T(fn): return p.do("tensor", fn, 1)
    def V(fn): return p.do("vector", fn, 1)
    def S(fn): return p.do("scalar", fn, 1)
    def W(eng, on, val): p.wait(eng, on, val)
    DQ = p.dma
    WD = p.wait_dma
    WT = p.wait_stream_total
    for nm, q, d in [("consts", "gpsimd", 1), ("bounce", "gpsimd", 1),
                     ("spill_xn", "gpsimd", 2), ("spill_uh", "gpsimd", 2),
                     ("spill_s", "gpsimd", 2), ("wdsp", "gpsimd", 1),
                     ("attnw", "gpsimd", 4),
                     ("wqsp_d", "gpsimd", 2), ("wqsp_g", "gpsimd", 2), ("wqsp_u", "gpsimd", 2),
                     ("wnat", "sync", 2), ("wtr", "sync", 2),
                     ("spill_xf", "gpsimd", 4), ("xfl", "sync", 4),
                     ("xA", "sync", 2), ("xnch", "sync", 2),
                     ("wcolg", "sync", 2), ("wcolu", "sync", 2),
                     ("wdl", "sync", 1), ("uhl", "sync", 4),
                     ("cls", "sync", 2), ("out", "sync", 2)]:
        p.stream(nm, q, d)

    def din(name, shape, dt=F32):
        return nc.dram_tensor(name, shape, dt, kind="ExternalInput").ap()

    x_h = din("x_h", [D, NT])
    wqT_h = din("wqT_h", [D, 256]); wkT_h = din("wkT_h", [D, 256])
    wvT_h = din("wvT_h", [D, D]); woT_h = din("woT_h", [D, D])
    gw_h = din("gw_h", [1, HD])
    clipx_h = din("clipx_h", [1, 1]); clipd_h = din("clipd_h", [1, 1])
    wg_nat_h = din("wg_nat_h", [I_FFN, D]); wu_nat_h = din("wu_nat_h", [I_FFN, D])
    wd_nat_h = din("wd_nat_h", [D, I_FFN])
    wgT_h = din("wgT_h", [D, I_FFN]); wuT_h = din("wuT_h", [D, I_FFN])
    wdT_h = din("wdT_h", [I_FFN, D])
    triu_h = din("triu_h", [128, 128], BF16)
    corrA_h = din("corrA_h", [128, 256], BF16)
    corrB_h = din("corrB_h", [128, 128], BF16)
    EA0_h = din("EA0_h", [128, 256], BF16); EA1_h = din("EA1_h", [128, 256], BF16)
    EB0_h = din("EB0_h", [128, 256], BF16); EB1_h = din("EB1_h", [128, 256], BF16)
    s1mask_h = din("s1mask_h", [128, 256])
    bo_h = din("bo_h", [128, 256], BF16)
    sel_h = din("sel_h", [32, 1024])
    emaske_h = din("emaske_h", [128, 1]); emasko_h = din("emasko_h", [128, 1])
    onescol_h = din("onescol_h", [128, 1]); onescolb_h = din("onescolb_h", [128, 1], BF16)
    ones1r_h = din("ones1r_h", [1, 128])
    ident_h = din("ident_h", [128, 128], BF16)
    cvals_h = din("cvals_h", [128, 8])

    o_out = nc.dram_tensor("o_out", [D, NT], F32, kind="ExternalOutput").ap()

    xn_sc = nc.dram_tensor("xn_sc", [D, NT], BF16).ap()
    xf_sc = nc.dram_tensor("xf_sc", [D, NT], F32).ap()
    uh_sc = nc.dram_tensor("uh_sc", [I_FFN, NT], F32).ap()
    s_sc = nc.dram_tensor("s_sc", [D, NT], BF16).ap()
    wg_sc = nc.dram_tensor("wg_sc", [128, 8 * I_FFN], BF16).ap()
    wu_sc = nc.dram_tensor("wu_sc", [128, 8 * I_FFN], BF16).ap()
    wd_sc = nc.dram_tensor("wd_sc", [128, 22 * 1024], BF16).ap()
    mr_sc = nc.dram_tensor("mr_sc", [1, 4096], F32).ap()
    cc_t = [(nc.dram_tensor("cc1_in", [1, 64], F32), nc.dram_tensor("cc1_out", [1, 64], F32)),
            (nc.dram_tensor("cc2_in", [1, 64], F32), nc.dram_tensor("cc2_out", [1, 64], F32))]

    def sbp(name, shape, dt):
        return es.enter_context(nc.sbuf_tensor(name, shape, dt))

    triu = sbp("triu", [128, 128], BF16)
    corrA = sbp("corrA", [128, 256], BF16)
    corrB = sbp("corrB", [128, 128], BF16)
    EA = [sbp("EA0", [128, 256], BF16), sbp("EA1", [128, 256], BF16)]
    EB = [sbp("EB0", [128, 256], BF16), sbp("EB1", [128, 256], BF16)]
    s1mask = sbp("s1mask", [128, 256], F32)
    bo = sbp("bo", [128, 256], BF16)
    sel = sbp("sel", [32, 1024], F32)
    emaske = sbp("emaske", [128, 1], F32)
    emasko = sbp("emasko", [128, 1], F32)
    onescol = sbp("onescol", [128, 1], F32)
    onescolb = sbp("onescolb", [128, 1], BF16)
    ones1r = sbp("ones1r", [1, 128], F32)
    ident = sbp("ident", [128, 128], BF16)
    gwcol = sbp("gwcol", [128, 1], F32)
    clips = sbp("clips", [1, 2], F32)
    maxst = sbp("maxst", [128, 64], F32)
    maxst2 = sbp("maxst2", [128, 192], F32)
    gmcol = sbp("gmcol", [128, 1], F32)
    rmg = sbp("rmg", [128, 22], F32)
    rmu = sbp("rmu", [128, 22], F32)
    rmd = sbp("rmd", [128, 8], F32)
    scg = sbp("scg", [128, 22], F32)
    scu = sbp("scu", [128, 22], F32)
    scd = sbp("scd", [128, 8], F32)
    invsx_col = sbp("invsx_col", [128, 1], F32)
    invsd_col = sbp("invsd_col", [128, 1], F32)
    smallrow = sbp("smallrow", [1, 128], F32)
    qth2 = sbp("qth2", [128, 2 * 512], F32)
    qi32 = sbp("qi32", [128, 2 * 512], I32)
    qgt2 = sbp("qgt2", [128, 2 * 512], F32)
    cvals = sbp("cvals", [128, 8], F32)
    _CONSTVALS = [LN_EPS, 0.5, 4.0 / S32, 1.0 / S32, -0.5, 1.0]
    for _j, _v in enumerate(_CONSTVALS):
        nc.const_aps.register(F32, float(_v), cvals[:, _j:_j + 1])

    PS = [es.enter_context(nc.psum_tensor(f"ps{i}", [128, 512], F32)) for i in range(8)]
    ps_guard = {i: [] for i in range(8)}

    def use_ps(i):
        for eng, cnt in ps_guard[i]:
            W("tensor", eng, cnt)
        ps_guard[i] = []

    def guard_ps(i, eng, cnt):
        ps_guard[i].append((eng, cnt))

    for dst, src in [(triu, triu_h), (corrA, corrA_h), (corrB, corrB_h),
                     (EA[0], EA0_h), (EA[1], EA1_h), (EB[0], EB0_h), (EB[1], EB1_h),
                     (s1mask, s1mask_h), (bo, bo_h), (sel, sel_h),
                     (emaske, emaske_h), (emasko, emasko_h),
                     (onescol, onescol_h), (onescolb, onescolb_h), (ones1r, ones1r_h),
                     (ident, ident_h)]:
        DQ("consts", lambda h, d=dst, s=src: h.dma_start(d[:, :], s))
    DQ("consts", lambda h: h.dma_start(gwcol[0:64, 0:1], gw_h.rearrange("o (d u) -> (o d) u", u=1)))
    DQ("consts", lambda h: h.dma_start(gwcol[64:128, 0:1], gw_h.rearrange("o (d u) -> (o d) u", u=1)))
    DQ("consts", lambda h: h.dma_start(clips[0:1, 0:1], clipx_h))
    DQ("consts", lambda h: h.dma_start(clips[0:1, 1:2], clipd_h))
    DQ("consts", lambda h: h.dma_start(cvals[:, :], cvals_h))

    def global_scale(stash, ncols, cc_idx, clip_slot, gmax_slot, sig_slot, s_slot, inv_slot,
                     inv_col, rm_list):
        c_f = V(lambda h, n=ncols: h.reduce_max(gmcol[:, 0:1], stash[:, 0:n], axis=AX.X))
        W("gpsimd", "vector", c_f)
        t_o = DQ("bounce", lambda h: h.dma_start(
            mr_sc[0:1, 0:128].rearrange("o (a b) -> (o b) a", a=1), gmcol[:, 0:1],
            allow_slow_non_contiguous=True))
        WD("gpsimd", t_o)
        t_i = DQ("bounce", lambda h: h.dma_start(smallrow[0:1, :], mr_sc[0:1, 0:128]))
        WD("vector", t_i)
        c_r = V(lambda h, sl=gmax_slot: h.reduce_max(smallrow[0:1, sl:sl + 1], smallrow[0:1, 0:128],
                                                     axis=AX.X))
        W("gpsimd", "vector", c_r)
        tin, tout = cc_t[cc_idx]
        t_b = DQ("bounce", lambda h, t=tin: h.dma_start(t.ap()[0:1, 0:64],
                                                        smallrow[0:1, 0:64]))
        WD("gpsimd", t_b)
        cc_state[0] += 1
        tgt = cc_state[0]
        p.raw("gpsimd", lambda h, ti=tin, to=tout: h.collective_compute(
            "AllReduce", OP.max, replica_groups=[list(range(8))],
            ins=[ti.ap().opt()], outs=[to.ap().opt()]).then_inc(sem_cc, 1))
        p.raw("gpsimd", lambda h, t=tgt: h.wait_ge(sem_cc, t))
        t_gi = DQ("bounce", lambda h, sl=gmax_slot, t=tout: h.dma_start(
            smallrow[0:1, sl:sl + 1], t.ap()[0:1, sl:sl + 1]))
        WD("scalar", t_gi)
        WD("vector", t_gi)
        c_sg = S(lambda h, a=sig_slot, cs=clip_slot: h.activation(
            smallrow[0:1, a:a + 1], clips[0:1, cs:cs + 1], AF.Sigmoid))
        W("vector", "scalar", c_sg)
        c1 = V(lambda h, a=sig_slot, g=gmax_slot, s=s_slot: h.tensor_tensor(
            smallrow[0:1, s:s + 1], smallrow[0:1, g:g + 1], smallrow[0:1, a:a + 1], OP.mult))
        W("vector", "vector", c1)
        c2 = V(lambda h, s=s_slot: h.tensor_scalar(
            smallrow[0:1, s:s + 1], smallrow[0:1, s:s + 1], 1.0 / QMAX, None, OP.mult))
        W("vector", "vector", c2)
        c3 = V(lambda h, s=s_slot, iv=inv_slot: h.reciprocal(
            smallrow[0:1, iv:iv + 1], smallrow[0:1, s:s + 1]))
        use_ps(6)
        W("tensor", "vector", c3)
        c_mm = T(lambda h, s=s_slot: h.matmul(PS[6][:, 0:2], ones1r[0:1, :],
                                              smallrow[0:1, s:s + 2], start=True, stop=True))
        W("vector", "tensor", c_mm)
        c_e = V(lambda h, ic=inv_col: h.tensor_copy(ic[:, 0:1], PS[6][:, 1:2]))
        last = c_e
        for (sc_dst, rm_src) in rm_list:
            W("vector", "vector", last)
            last = V(lambda h, d=sc_dst, r=rm_src: h.tensor_scalar(
                d[:, :], r[:, :], PS[6][:, 0:1], 1.0 / QMAX, OP.mult, OP.mult))
        guard_ps(6, "vector", last)
        return last

    wq_res = {}
    wq_prev = {}
    wq_state = [None, False]

    def pump(budget):
        it = wq_state[0]
        while budget > 0 and not wq_state[1]:
            try:
                budget -= next(it)
            except StopIteration:
                wq_state[1] = True

    def wq_gen(nat_h, wT_h, n_rows, n_k, rmcols, w_out_sc, spst, tag,
               wnat, wtr, invb, invrow, qthw, qr0w, qgtw, wqow):
        nrt = n_rows // 128
        inner = 128 * n_k
        spq = p.streams[spst][0]
        red_marks = {}
        for rt in range(nrt):
            b = rt % 2
            if rt >= 2:
                W("sync", "vector", red_marks[rt - 2])
            elif wq_prev.get("red"):
                W("sync", "vector", wq_prev["red"])
            t_ld = DQ("wnat", lambda h, rt=rt, b=b: h.dma_start(
                wnat[:, b * inner:(b + 1) * inner], nat_h[rt * 128:(rt + 1) * 128, :]))
            WD("vector", t_ld)
            red_marks[rt] = V(lambda h, rt=rt, b=b: h.reduce_max(
                rmcols[:, rt:rt + 1], wnat[:, b * inner:(b + 1) * inner],
                axis=AX.X, apply_absolute_value=True))
            yield 3.2 if n_k == 22 else 1.4
        W("gpsimd", "vector", red_marks[nrt - 1])
        if wq_prev.get("mm"):
            W("gpsimd", "tensor", wq_prev["mm"])
        t_o = DQ("bounce", lambda h, nrt=nrt: h.dma_start(
            mr_sc[0:1, 0:nrt * 128].rearrange("o (a b) -> (o b) a", a=nrt),
            rmcols[:, 0:nrt], allow_slow_non_contiguous=True))
        WD("gpsimd", t_o)
        t_i = DQ("bounce", lambda h, nrt=nrt, n_rows=n_rows: h.dma_start(
            invrow[0:1, 0:n_rows], mr_sc[0:1, 0:nrt * 128]))
        yield 4.0
        WD("vector", t_i)
        c_rec = V(lambda h, n_rows=n_rows: h.reciprocal(invrow[0:1, 0:n_rows],
                                                        invrow[0:1, 0:n_rows]))
        W("vector", "vector", c_rec)
        c_sc = V(lambda h, n_rows=n_rows: h.tensor_scalar(
            invrow[0:1, 0:n_rows], invrow[0:1, 0:n_rows], QMAX, None, OP.mult))
        nb = (n_rows + 511) // 512
        WT("tensor", "consts")
        W("tensor", "vector", c_sc)
        c_ev = None
        c_mm = None
        for bb in range(nb):
            lo, hi = bb * 512, min(n_rows, (bb + 1) * 512)
            use_ps(6)
            c_mm = T(lambda h, lo=lo, hi=hi: h.matmul(
                PS[6][:, 0:hi - lo], ones1r[0:1, :], invrow[0:1, lo:hi], start=True, stop=True))
            W("vector", "tensor", c_mm)
            c_ev = V(lambda h, lo=lo, hi=hi: h.tensor_copy(invb[:, lo:hi], PS[6][:, 0:hi - lo]))
            guard_ps(6, "vector", c_ev)
        c_invb = c_ev
        yield 2.0
        sp_marks = {}
        q_marks = {}
        for k in range(n_k):
            b = k % 2
            if k >= 2:
                W("sync", "vector", q_marks[k - 2])
            elif wq_prev.get("chain"):
                W("sync", "vector", wq_prev["chain"])
            t_ld = DQ("wtr", lambda h, k=k, b=b, n_rows=n_rows: h.dma_start(
                wtr[:, b * n_rows:(b + 1) * n_rows], wT_h[k * 128:(k + 1) * 128, :]))
            WD("vector", t_ld)
            W("vector", "vector", c_invb)
            if k >= 2:
                WD("vector", sp_marks[k - 2])
                W("vector", "vector", q_marks[k - 2])
            elif wq_prev.get("sp2"):
                WD("vector", wq_prev["sp2"][0])
                WD("vector", wq_prev["sp2"][1])
            c1 = V(lambda h, b=b, n_rows=n_rows: h.tensor_tensor(
                qthw[:, 0:n_rows], wtr[:, b * n_rows:(b + 1) * n_rows],
                invb[:, 0:n_rows], OP.mult))
            W("vector", "vector", c1)
            c2 = V(lambda h, n_rows=n_rows: h.tensor_scalar(
                qthw[:, 0:n_rows], qthw[:, 0:n_rows], 0.5, 127.9, OP.add, OP.min))
            W("vector", "vector", c2)
            c4 = V(lambda h, n_rows=n_rows: h.tensor_scalar(
                qr0w[:, 0:n_rows], qthw[:, 0:n_rows], -127.9, None, OP.max))
            W("vector", "vector", c4)
            c5 = V(lambda h, n_rows=n_rows: h.tensor_tensor(
                qgtw[:, 0:n_rows], qr0w[:, 0:n_rows], qthw[:, 0:n_rows], OP.is_gt))
            W("vector", "vector", c5)
            c6 = V(lambda h, b=b, n_rows=n_rows: h.scalar_tensor_tensor(
                wqow[:, b * n_rows:(b + 1) * n_rows], qgtw[:, 0:n_rows], -1.0,
                qr0w[:, 0:n_rows], OP.mult, OP.add))
            q_marks[k] = c6
            W(spq, "vector", c6)
            sp_marks[k] = DQ(spst, lambda h, k=k, b=b, n_rows=n_rows: h.dma_start(
                w_out_sc[:, k * n_rows:(k + 1) * n_rows],
                wqow[:, b * n_rows:(b + 1) * n_rows]))
            yield 4.9 if n_k == 22 else 12.8
        wq_prev.clear()
        wq_prev.update({"red": red_marks[nrt - 1], "chain": q_marks[n_k - 1], "mm": c_mm,
                        "sp2": {0: sp_marks[n_k - 2], 1: sp_marks[n_k - 1]}})
        wq_res[tag] = (sp_marks[n_k - 1], q_marks[n_k - 1])

    if PHASELIM < 1:
        with nc.Block() as blk:
            p.emit_block(blk)
        return es

    with nc.sbuf_tensor("xbuf", [128, 2 * 8 * 512], F32) as xbuf, \
         nc.sbuf_tensor("xsq", [128, 2 * 512], F32) as xsq, \
         nc.sbuf_tensor("xnf", [128, 4 * 512], F32) as xnf, \
         nc.sbuf_tensor("xnb", [128, 8 * 512], BF16) as xnb, \
         nc.sbuf_tensor("mrows", [1, 2 * 512], F32) as mrows, \
         nc.sbuf_tensor("rrows", [1, 2 * 512], F32) as rrows, \
         nc.sbuf_tensor("t1row", [1, 512], F32) as t1row, \
         nc.sbuf_tensor("t2row", [1, 512], F32) as t2row, \
         nc.sbuf_tensor("wnat", [128, 2 * I_FFN], F32) as wnat, \
         nc.sbuf_tensor("wtr", [128, 2 * I_FFN], F32) as wtr, \
         nc.sbuf_tensor("invb", [128, I_FFN], F32) as invb, \
         nc.sbuf_tensor("invrow", [1, I_FFN], F32) as invrow, \
         nc.sbuf_tensor("qthw", [128, I_FFN], F32) as qthw, \
         nc.sbuf_tensor("qr0w", [128, I_FFN], I32) as qr0w, \
         nc.sbuf_tensor("qgtw", [128, I_FFN], F32) as qgtw, \
         nc.sbuf_tensor("wqow", [128, 2 * I_FFN], BF16) as wqow:

        def _gen_all():
            yield from wq_gen(wd_nat_h, wdT_h, D, 22, rmd, wd_sc, "wqsp_d", "d",
                              wnat, wtr, invb, invrow, qthw, qr0w, qgtw, wqow)
            yield from wq_gen(wg_nat_h, wgT_h, I_FFN, 8, rmg, wg_sc, "wqsp_g", "g",
                              wnat, wtr, invb, invrow, qthw, qr0w, qgtw, wqow)
            yield from wq_gen(wu_nat_h, wuT_h, I_FFN, 8, rmu, wu_sc, "wqsp_u", "u",
                              wnat, wtr, invb, invrow, qthw, qr0w, qgtw, wqow)
        wq_state[0] = _gen_all()

        def xtile(b, pt):
            return xbuf[:, (b * 8 + pt) * 512:(b * 8 + pt) * 512 + 512]

        xf_w = xf_sc.rearrange("(pt p) (tt c) -> p pt tt c", p=128, tt=8)

        sqmm_marks = {}
        a_done = {}
        cast_g = {}
        spxf = {}
        prev_cmq = 0
        for tt in range(8):
            b = tt % 2
            sl2 = tt % 2
            if tt >= 2:
                WD("sync", a_done[tt - 2])
            c_ld = DQ("xA", lambda h, tt=tt, b=b: h.dma_start(
                xbuf.rearrange("p (z pt c) -> p z pt c", z=2, pt=8)[:, b],
                x_h.rearrange("(pt p) (tt c) -> p tt pt c", p=128, tt=8)[:, tt]))
            WD("scalar", c_ld)
            WT("scalar", "consts")
            W("scalar", "tensor", prev_cmq)
            use_ps(4); use_ps(5)
            WD("tensor", c_ld)
            WT("tensor", "consts")
            sq_marks = []
            for pt in range(8):
                if pt >= 2:
                    W("scalar", "tensor", sqmm_marks[pt - 2])
                sq_marks.append(S(lambda h, pt=pt, b=b: h.activation(
                    xsq[:, (pt % 2) * 512:(pt % 2 + 1) * 512], xtile(b, pt), AF.Square)))
                T(lambda h, pt=pt, b=b: h.matmul(
                    PS[4][0:1, :], onescol[:, 0:1], xtile(b, pt), start=(pt == 0), stop=(pt == 7)))
                W("tensor", "scalar", sq_marks[pt])
                c_mq = T(lambda h, pt=pt: h.matmul(
                    PS[5][0:1, :], onescol[:, 0:1], xsq[:, (pt % 2) * 512:(pt % 2 + 1) * 512],
                    start=(pt == 0), stop=(pt == 7)))
                sqmm_marks[pt] = c_mq
            pump(14.0)
            W("vector", "tensor", c_mq)
            c_m = V(lambda h, sl2=sl2: h.tensor_scalar(
                mrows[0:1, sl2 * 512:(sl2 + 1) * 512], PS[4][0:1, :], 1.0 / D, None, OP.mult))
            W("vector", "vector", c_m)
            c_mq2 = V(lambda h: h.tensor_scalar(t1row[0:1, :], PS[5][0:1, :], 1.0 / D, None, OP.mult))
            guard_ps(4, "vector", c_m); guard_ps(5, "vector", c_mq2)
            W("vector", "vector", c_mq2)
            c_m2 = V(lambda h, sl2=sl2: h.tensor_tensor(
                t2row[0:1, :], mrows[0:1, sl2 * 512:(sl2 + 1) * 512],
                mrows[0:1, sl2 * 512:(sl2 + 1) * 512], OP.mult))
            W("vector", "vector", c_m2)
            c_var = V(lambda h: h.tensor_tensor(t1row[0:1, :], t1row[0:1, :], t2row[0:1, :],
                                                OP.subtract))
            W("scalar", "vector", c_var)
            c_ln = S(lambda h: h.activation(t2row[0:1, :], t1row[0:1, :], AF.Ln, bias=LN_EPS))
            W("scalar", "scalar", c_ln)
            c_rs = S(lambda h, sl2=sl2: h.activation(
                rrows[0:1, sl2 * 512:(sl2 + 1) * 512], t2row[0:1, :], AF.Exp, scale=-0.5))
            use_ps(4); use_ps(5)
            W("tensor", "vector", c_m)
            W("tensor", "scalar", c_rs)
            T(lambda h, sl2=sl2: h.matmul(PS[4][:, :], ones1r[0:1, :],
                                          mrows[0:1, sl2 * 512:(sl2 + 1) * 512],
                                          start=True, stop=True))
            c_br = T(lambda h, sl2=sl2: h.matmul(PS[5][:, :], ones1r[0:1, :],
                                                 rrows[0:1, sl2 * 512:(sl2 + 1) * 512],
                                                 start=True, stop=True))
            pump(7.0)
            W("vector", "tensor", c_br)
            if tt >= 1:
                WD("scalar", a_done[tt - 1])
            for pt in range(8):
                ti = tt * 8 + pt
                s4 = ti % 4
                if ti >= 4:
                    W("vector", "scalar", cast_g[ti - 4])
                    WD("vector", spxf[ti - 4])
                c_s = V(lambda h, s4=s4, pt=pt, b=b: h.tensor_tensor(
                    xnf[:, s4 * 512:(s4 + 1) * 512], xtile(b, pt), PS[4][:, :],
                    OP.subtract))
                W("vector", "vector", c_s)
                c_x = V(lambda h, s4=s4: h.tensor_tensor(
                    xnf[:, s4 * 512:(s4 + 1) * 512],
                    xnf[:, s4 * 512:(s4 + 1) * 512], PS[5][:, :], OP.mult))
                W("vector", "vector", c_x)
                V(lambda h, s4=s4, ti=ti: h.reduce_max(
                    maxst[:, ti:ti + 1], xnf[:, s4 * 512:(s4 + 1) * 512],
                    axis=AX.X, apply_absolute_value=True))
                W("scalar", "vector", c_x)
                cast_g[ti] = S(lambda h, s4=s4, pt=pt: h.copy(
                    xnb[:, pt * 512:(pt + 1) * 512], xnf[:, s4 * 512:(s4 + 1) * 512]))
                W("gpsimd", "vector", c_x)
                spxf[ti] = DQ("spill_xf", lambda h, s4=s4, pt=pt, tt=tt: h.dma_start(
                    xf_w[:, pt, tt], xnf[:, s4 * 512:(s4 + 1) * 512]))
            guard_ps(4, "vector", c_x); guard_ps(5, "vector", c_x)
            prev_cmq = c_mq
            W("gpsimd", "scalar", cast_g[tt * 8 + 7])
            a_done[tt] = DQ("spill_xn", lambda h, tt=tt: h.dma_start(
                xn_sc.rearrange("(pt p) (tt c) -> p tt pt c", p=128, tt=8)[:, tt],
                xnb.rearrange("p (pt c) -> p pt c", pt=8)))

        W("vector", "vector", p.cnt["vector"])
        c_f = V(lambda h: h.reduce_max(gmcol[:, 0:1], maxst[:, 0:64], axis=AX.X))
        W("gpsimd", "vector", c_f)
        t_o1 = DQ("bounce", lambda h: h.dma_start(
            mr_sc[0:1, 0:128].rearrange("o (a b) -> (o b) a", a=1), gmcol[:, 0:1],
            allow_slow_non_contiguous=True))
        WD("gpsimd", t_o1)
        t_i1 = DQ("bounce", lambda h: h.dma_start(smallrow[0:1, :], mr_sc[0:1, 0:128]))
        pump(22.0)
        WD("vector", t_i1)
        c_r = V(lambda h: h.reduce_max(smallrow[0:1, 2:3], smallrow[0:1, 0:128], axis=AX.X))
        W("gpsimd", "vector", c_r)
        t_b1 = DQ("bounce", lambda h: h.dma_start(cc_t[0][0].ap()[0:1, 0:64],
                                                  smallrow[0:1, 0:64]))
        WD("gpsimd", t_b1)
        cc_state[0] += 1
        tgt1 = cc_state[0]
        p.raw("gpsimd", lambda h: h.collective_compute(
            "AllReduce", OP.max, replica_groups=[list(range(8))],
            ins=[cc_t[0][0].ap().opt()], outs=[cc_t[0][1].ap().opt()]).then_inc(sem_cc, 1))
        pump(1e9)
        p.raw("gpsimd", lambda h, t=tgt1: h.wait_ge(sem_cc, t))
        t_gi1 = DQ("bounce", lambda h: h.dma_start(smallrow[0:1, 2:3],
                                                   cc_t[0][1].ap()[0:1, 2:3]))
        WD("scalar", t_gi1)
        WD("vector", t_gi1)
        c_sg = S(lambda h: h.activation(smallrow[0:1, 3:4], clips[0:1, 0:1], AF.Sigmoid))
        W("vector", "scalar", c_sg)
        c1x = V(lambda h: h.tensor_tensor(smallrow[0:1, 4:5], smallrow[0:1, 2:3],
                                          smallrow[0:1, 3:4], OP.mult))
        W("vector", "vector", c1x)
        c2x = V(lambda h: h.tensor_scalar(smallrow[0:1, 4:5], smallrow[0:1, 4:5],
                                          1.0 / QMAX, None, OP.mult))
        W("vector", "vector", c2x)
        c3x = V(lambda h: h.reciprocal(smallrow[0:1, 5:6], smallrow[0:1, 4:5]))
        use_ps(6)
        W("tensor", "vector", c3x)
        c_mmx = T(lambda h: h.matmul(PS[6][:, 0:2], ones1r[0:1, :], smallrow[0:1, 4:6],
                                     start=True, stop=True))
        W("vector", "tensor", c_mmx)
        c_e = V(lambda h: h.tensor_copy(invsx_col[:, 0:1], PS[6][:, 1:2]))
        W("vector", "vector", c_e)
        cf1 = V(lambda h: h.tensor_scalar(scg[:, :], rmg[:, :], PS[6][:, 0:1], 1.0 / QMAX,
                                          OP.mult, OP.mult))
        W("vector", "vector", cf1)
        c_sx = V(lambda h: h.tensor_scalar(scu[:, :], rmu[:, :], PS[6][:, 0:1], 1.0 / QMAX,
                                           OP.mult, OP.mult))
        guard_ps(6, "vector", c_sx)
        t_wdq = wq_res["d"][0]
        p.barrier()

    if PHASELIM < 4:
        with nc.Block() as blk:
            p.emit_block(blk)
        return es
    with nc.sbuf_tensor("xq", [128, 8 * NT], BF16) as xq, \
         nc.sbuf_tensor("xnl", [128, 4 * 512], F32) as xnl, \
         nc.sbuf_tensor("slbuf", [128, 2 * 512], F32) as slbuf, \
         nc.sbuf_tensor("uhbuf", [128, 2 * 512], F32) as uhbuf, \
         nc.sbuf_tensor("wgq", [128, 8 * I_FFN], BF16) as wgq, \
         nc.sbuf_tensor("wuq", [128, 8 * I_FFN], BF16) as wuq:

        xf_r = xf_sc.rearrange("(pt p) (tt c) -> p pt tt c", p=128, tt=8)
        th_m = {}
        qg_m = {}
        bq_tt_done = {}

        def emit_bq(tt):
            for pt in range(8):
                ti = tt * 8 + pt
                sl = ti % 4
                qb = ti % 2
                if ti >= 4:
                    W("sync", "scalar", th_m[ti - 4])
                t_ld = DQ("xfl", lambda h, pt=pt, tt=tt, sl=sl: h.dma_start(
                    xnl[:, sl * 512:(sl + 1) * 512], xf_r[:, pt, tt]))
                WD("scalar", t_ld)
                if ti == 0:
                    W("scalar", "vector", c_sx)
                if ti >= 2:
                    W("scalar", "vector", qg_m[ti - 2])
                c_th = S(lambda h, sl=sl, qb=qb: h.activation(
                    qth2[:, qb * 512:(qb + 1) * 512], xnl[:, sl * 512:(sl + 1) * 512],
                    AF.Identity, bias=0.5, scale=invsx_col[:, 0:1]))
                th_m[ti] = c_th
                W("vector", "scalar", c_th)
                c_cl = V(lambda h, qb=qb: h.tensor_scalar(
                    qth2[:, qb * 512:(qb + 1) * 512], qth2[:, qb * 512:(qb + 1) * 512],
                    127.9, -127.9, OP.min, OP.max))
                W("scalar", "vector", c_cl)
                c_i = S(lambda h, qb=qb: h.copy(
                    qi32[:, qb * 512:(qb + 1) * 512], qth2[:, qb * 512:(qb + 1) * 512]))
                W("vector", "scalar", c_i)
                c_g = V(lambda h, qb=qb: h.tensor_tensor(
                    qgt2[:, qb * 512:(qb + 1) * 512], qi32[:, qb * 512:(qb + 1) * 512],
                    qth2[:, qb * 512:(qb + 1) * 512], OP.is_gt))
                W("vector", "vector", c_g)
                c_q = V(lambda h, pt=pt, tt=tt, qb=qb: h.scalar_tensor_tensor(
                    xq[:, pt * NT + tt * 512:pt * NT + (tt + 1) * 512],
                    qgt2[:, qb * 512:(qb + 1) * 512], -1.0,
                    qi32[:, qb * 512:(qb + 1) * 512], OP.mult, OP.add))
                qg_m[ti] = c_q
                if pt == 7:
                    bq_tt_done[tt] = c_q

        uh_mx = {}
        b_sp = {}
        WT("sync", "wqsp_g")
        WT("sync", "wqsp_u")
        t_wgl = DQ("wcolg", lambda h: h.dma_start(wgq[:, :], wg_sc))
        t_wul = DQ("wcolu", lambda h: h.dma_start(wuq[:, :], wu_sc))
        WD("tensor", t_wgl)
        WD("tensor", t_wul)

        def emit_b(tt):
            W("tensor", "vector", bq_tt_done[tt])
            for m in range(22):
                idx = tt * 22 + m
                pg_ = idx % 2
                pu_ = 2 + idx % 2
                ub = idx % 2
                use_ps(pg_); use_ps(pu_)
                for k in range(8):
                    T(lambda h, k=k, tt=tt, m=m, pg_=pg_: h.matmul(
                        PS[pg_][:, :], wgq[:, k * I_FFN + m * 128:k * I_FFN + (m + 1) * 128],
                        xq[:, k * NT + tt * 512:k * NT + (tt + 1) * 512],
                        start=(k == 0), stop=(k == 7)))
                for k in range(8):
                    T(lambda h, k=k, tt=tt, m=m, pu_=pu_: h.matmul(
                        PS[pu_][:, :], wuq[:, k * I_FFN + m * 128:k * I_FFN + (m + 1) * 128],
                        xq[:, k * NT + tt * 512:k * NT + (tt + 1) * 512],
                        start=(k == 0), stop=(k == 7)))
                W("scalar", "tensor", p.cnt["tensor"])
                W("scalar", "vector", c_sx)
                if idx >= 2:
                    W("scalar", "vector", uh_mx[idx - 2])
                c_sl = S(lambda h, m=m, pg_=pg_, ub=ub: h.activation(
                    slbuf[:, ub * 512:(ub + 1) * 512], PS[pg_][:, :], AF.Silu,
                    scale=scg[:, m:m + 1]))
                guard_ps(pg_, "scalar", c_sl)
                W("vector", "scalar", c_sl)
                if idx >= 2:
                    WD("vector", b_sp[idx - 2])
                c_uh = V(lambda h, m=m, pu_=pu_, ub=ub: h.scalar_tensor_tensor(
                    uhbuf[:, ub * 512:(ub + 1) * 512], PS[pu_][:, :], scu[:, m:m + 1],
                    slbuf[:, ub * 512:(ub + 1) * 512], OP.mult, OP.mult))
                guard_ps(pu_, "vector", c_uh)
                W("vector", "vector", c_uh)
                uh_mx[idx] = V(lambda h, idx=idx, ub=ub: h.reduce_max(
                    maxst2[:, idx:idx + 1], uhbuf[:, ub * 512:(ub + 1) * 512],
                    axis=AX.X, apply_absolute_value=True))
                W("gpsimd", "vector", uh_mx[idx])
                b_sp[idx] = DQ("spill_uh", lambda h, m=m, tt=tt, ub=ub: h.dma_start(
                    uh_sc[m * 128:(m + 1) * 128, tt * 512:(tt + 1) * 512],
                    uhbuf[:, ub * 512:(ub + 1) * 512]))

        emit_bq(0)
        emit_bq(1)
        for tt in range(8):
            emit_b(tt)
            if tt + 2 < 8:
                emit_bq(tt + 2)

        if PHASELIM < 6:
            p.barrier()
            with nc.Block() as blk:
                p.emit_block(blk)
            return es
        W("vector", "vector", uh_mx[22 * 8 - 1])
        c_sd = global_scale(maxst2, 176, 1, 1, 6, 7, 8, 9, invsd_col, [(scd, rmd)])
        p.barrier()

    if PHASELIM < 7:
        with nc.Block() as blk:
            p.emit_block(blk)
        return es

    if PHASELIM < 7:
        with nc.Block() as blk:
            p.emit_block(blk)
        return es
    if True:
        aes = ExitStack()
        def asb(name, shape, dt):
            return aes.enter_context(nc.sbuf_tensor(name, shape, dt))
        wqT = asb("wqT", [128, 8 * 256], BF16)
        wkT = asb("wkT", [128, 8 * 256], BF16)
        wvT = asb("wvT", [128, 8 * 1024], BF16)
        woT = asb("woT", [128, 8 * 1024], BF16)
        S2f = asb("S2f", [128, NH * 128], F32)
        S2b = asb("S2b", [128, NH * 128], BF16)
        S1f = asb("S1f", [128, 256], F32)
        S1b = asb("S1b", [128, 256], BF16)
        S0c = asb("S0c", [128, 8], F32)
        xnch = asb("xnch", [128, 2 * 8 * CH], BF16)
        qf = [[asb(f"qf{z}_{i}", [128, CH], BF16) for i in range(2)] for z in range(2)]
        kev = [[asb(f"kev{z}_{i}", [128, CH], BF16) for i in range(2)] for z in range(2)]
        kod = [[asb(f"kod{z}_{i}", [128, CH], BF16) for i in range(2)] for z in range(2)]
        kfm = [asb(f"kfm_{i}", [128, CH], BF16) for i in range(2)]
        ktm = [[asb(f"ktm{z}_{i}", [128, 256], BF16) for i in range(2)] for z in range(2)]
        vtm = [[asb(f"vtm{z}_{i}", [128, 1024], BF16) for i in range(2)] for z in range(2)]
        kf2 = [[asb(f"kf2{z}_{i}", [128, 4096], BF16) for i in range(2)] for z in range(2)]
        qf2 = [asb(f"qf2_{z}", [128, NH * 512], BF16) for z in range(2)]
        qAe = asb("qAe", [128, 512], BF16)
        Am = [asb(f"Am_{z}", [128, NH * 384], BF16) for z in range(2)]
        ysq = asb("ysq", [128, 256], BF16)
        rstd = asb("rstd", [32, 256], F32)
        rstdg = asb("rstdg", [128, 256], BF16)
        yrms = asb("yrms", [128, 8 * 256], BF16)
        sstg = asb("sstg", [128, 8 * 256], BF16)

        for dst, src_ in [(wqT, wqT_h), (wkT, wkT_h), (wvT, wvT_h), (woT, woT_h)]:
            DQ("attnw", lambda h, d=dst, s=src_: h.dma_start(
                d.rearrange("p (k c) -> p k c", k=8),
                s.rearrange("(k p) c -> p k c", p=128)))

        state_ready = {"vector": 0, "scalar": 0}
        kf2_guard = [0]
        s0_prev = [0]
        xn_lastuse = {}
        post_done = {0: 0, 1: 0}
        s_spill = {}
        pre_out = {}

        def emit_pre(ci):
            z = ci % 2
            blkb = ci % 2
            xoff = blkb * 8 * CH
            if ci >= 2:
                W("sync", "tensor", xn_lastuse[ci - 2])
            c_ld = DQ("xnch", lambda h, ci=ci, b=blkb: h.dma_start(
                xnch.rearrange("p (z pt c) -> p z pt c", z=2, pt=8)[:, b],
                xn_sc.rearrange("(pt p) (ci c) -> p ci pt c", p=128, ci=NCHUNK)[:, ci]))

            def xn_ap(pt, lo=0, hi=CH, xoff=xoff):
                base = xoff + pt * CH
                return xnch[:, base + lo:base + hi]

            pd_guard = post_done[z]
            use_ps(0); use_ps(1)
            WD("tensor", c_ld)
            WT("tensor", "attnw")
            W("tensor", "tensor", pd_guard)
            for mm_ in range(2):
                for k in range(8):
                    c_qmm = T(lambda h, k=k, mm_=mm_, f=xn_ap: h.matmul(
                        PS[0][:, mm_ * 256:(mm_ + 1) * 256],
                        wqT[:, k * 256 + mm_ * 128:k * 256 + (mm_ + 1) * 128],
                        f(k), start=(k == 0), stop=(k == 7)))
            for mm_ in range(2):
                for k in range(8):
                    c_kmm = T(lambda h, k=k, mm_=mm_, f=xn_ap: h.matmul(
                        PS[1][:, mm_ * 256:(mm_ + 1) * 256],
                        wkT[:, k * 256 + mm_ * 128:k * 256 + (mm_ + 1) * 128],
                        f(k), start=(k == 0), stop=(k == 7)))
            W("scalar", "tensor", c_qmm)
            c_qe = S(lambda h, z=z: h.copy(qf[z][0][:, :], PS[0][:, 0:256]))
            c_qe = S(lambda h, z=z: h.copy(qf[z][1][:, :], PS[0][:, 256:512]))
            guard_ps(0, "scalar", c_qe)
            W("vector", "tensor", c_kmm)
            for mm_ in range(2):
                V(lambda h, mm_=mm_, z=z: h.tensor_scalar(
                    kev[z][mm_][:, :], PS[1][:, mm_ * 256:(mm_ + 1) * 256], emaske[:, 0:1],
                    None, OP.mult))
                c_kp = V(lambda h, mm_=mm_, z=z: h.tensor_scalar(
                    kod[z][mm_][:, :], PS[1][:, mm_ * 256:(mm_ + 1) * 256], emasko[:, 0:1],
                    None, OP.mult))
            W("scalar", "tensor", c_kmm)
            W("scalar", "vector", c_kp)
            c_kf = S(lambda h: h.copy(kfm[0][:, :], PS[1][:, 0:256]))
            c_kf = S(lambda h: h.copy(kfm[1][:, :], PS[1][:, 256:512]))
            guard_ps(1, "scalar", c_kf)
            guard_ps(1, "vector", c_kp)

            ps_t = PS[0].bitcast(BF16)
            c_ktm = None
            for half in range(2):
                for mm_ in range(2):
                    use_ps(0)
                    W("tensor", "scalar", c_kf)
                    c_tr = T(lambda h, half=half, mm_=mm_: h.transpose(
                        ps_t[:, 0:128], kfm[mm_][:, half * 128:(half + 1) * 128], ident[:, :]))
                    W("vector", "tensor", c_tr)
                    c_ktm = V(lambda h, half=half, mm_=mm_, z=z: h.tensor_copy(
                        ktm[z][half][:, mm_ * 128:(mm_ + 1) * 128], ps_t[:, 0:128]))
                    guard_ps(0, "vector", c_ktm)

            c_vtm = None
            for half in range(2):
                use_ps(2); use_ps(3)
                for k in range(8):
                    T(lambda h, k=k, half=half, f=xn_ap: h.matmul(
                        PS[2][:, :], f(k, half * 128, half * 128 + 128),
                        wvT[:, k * 1024:k * 1024 + 512], start=(k == 0), stop=(k == 7)))
                for k in range(8):
                    c_v2 = T(lambda h, k=k, half=half, f=xn_ap: h.matmul(
                        PS[3][:, :], f(k, half * 128, half * 128 + 128),
                        wvT[:, k * 1024 + 512:(k + 1) * 1024], start=(k == 0), stop=(k == 7)))
                W("scalar", "tensor", c_v2)
                c_ve = S(lambda h, half=half, z=z: h.copy(vtm[z][half][:, 0:512], PS[2][:, :]))
                guard_ps(2, "scalar", c_ve)
                c_vtm = S(lambda h, half=half, z=z: h.copy(vtm[z][half][:, 512:1024], PS[3][:, :]))
                guard_ps(3, "scalar", c_vtm)

            c_kf2 = None
            for half in range(2):
                W("gpsimd", "vector", c_ktm)
                W("gpsimd", "tensor", pd_guard)
                a_ap = ktm[z][half].rearrange("p (hh i j) -> p hh i j", hh=NH, j=1).broadcast_to(
                    [128, NH, FD, FD])
                b_ap = ktm[z][half].rearrange("p (hh i j) -> p hh i j", hh=NH, i=1).broadcast_to(
                    [128, NH, FD, FD])
                c_kf2 = p.do("gpsimd", lambda h, half=half, a=a_ap, bb=b_ap, z=z: h.tensor_tensor(
                    kf2[z][half].rearrange("p (hh i j) -> p hh i j", hh=NH, i=FD), a, bb,
                    OP.mult), 1)

            c_qf2 = None
            for hh in range(NH):
                par = hh % 2
                pair = hh // 2
                tile = pair // 4
                rg = 32 * (pair % 4)
                pa, pb_ = (0, 1) if hh % 2 == 0 else (2, 3)
                use_ps(pa); use_ps(pb_)
                W("tensor", "scalar", c_qe)
                for Tt in range(2):
                    c_b = None
                    T(lambda h, par=par, Tt=Tt, tile=tile, rg=rg, pa=pa, z=z: h.matmul(
                        PS[pa][:, Tt * 256:(Tt + 1) * 256],
                        EA[par][rg:rg + 32, Tt * 128:(Tt + 1) * 128],
                        qf[z][tile][rg:rg + 32, :], start=True, stop=True, tile_position=(rg, 0)))
                    c_b = T(lambda h, par=par, Tt=Tt, tile=tile, rg=rg, pb_=pb_, z=z: h.matmul(
                        PS[pb_][:, Tt * 256:(Tt + 1) * 256],
                        EB[par][rg:rg + 32, Tt * 128:(Tt + 1) * 128],
                        qf[z][tile][rg:rg + 32, :], start=True, stop=True, tile_position=(rg, 0)))
                W("scalar", "tensor", c_b)
                if hh >= 1:
                    W("scalar", "vector", c_qf2)
                c_ae = S(lambda h, pa=pa: h.copy(qAe[:, :], PS[pa][:, :]))
                guard_ps(pa, "scalar", c_ae)
                W("vector", "scalar", c_ae)
                c_qf2 = V(lambda h, hh=hh, pb_=pb_, z=z: h.tensor_tensor(
                    qf2[z][:, hh * 512:(hh + 1) * 512], qAe[:, :], PS[pb_][:, :], OP.mult))
                guard_ps(pb_, "vector", c_qf2)

            c_am = None
            for hh in range(NH):
                par = hh % 2
                pair = hh // 2
                tile = pair // 4
                rg = 32 * (pair % 4)
                kpar = kev[z] if par == 0 else kod[z]
                pt_ = 0 if hh % 2 == 0 else 1
                use_ps(pt_)
                W("tensor", "vector", c_kp)
                T(lambda h, tile=tile, rg=rg, pt_=pt_, kpar=kpar, z=z: h.matmul(
                    PS[pt_][:, 0:256], kpar[tile][rg:rg + 32, 0:128],
                    qf[z][tile][rg:rg + 32, :], start=True, stop=True, tile_position=(rg, 0)))
                c_t2 = T(lambda h, tile=tile, rg=rg, pt_=pt_, kpar=kpar, z=z: h.matmul(
                    PS[pt_][:, 256:384], kpar[tile][rg:rg + 32, 128:256],
                    qf[z][tile][rg:rg + 32, 128:256], start=True, stop=True, tile_position=(rg, 0)))
                W("scalar", "tensor", c_t2)
                W("scalar", "tensor", pd_guard)
                c_pl = S(lambda h, hh=hh, pt_=pt_, z=z: h.activation(
                    Am[z][:, hh * 384:(hh + 1) * 384], PS[pt_][:, 0:384], AF.Square,
                    bias=4.0 / S32, scale=1.0 / S32))
                guard_ps(pt_, "scalar", c_pl)
                W("vector", "scalar", c_pl)
                c_m1 = V(lambda h, hh=hh, z=z: h.scalar_tensor_tensor(
                    Am[z][:, hh * 384:hh * 384 + 128], Am[z][:, hh * 384:hh * 384 + 128],
                    0.5, triu[:, :], OP.add, OP.mult))
                W("vector", "vector", c_m1)
                c_m2 = V(lambda h, hh=hh, z=z: h.tensor_scalar(
                    Am[z][:, hh * 384 + 128:hh * 384 + 256],
                    Am[z][:, hh * 384 + 128:hh * 384 + 256], 0.5, None, OP.add))
                W("vector", "vector", c_m2)
                c_am = V(lambda h, hh=hh, z=z: h.scalar_tensor_tensor(
                    Am[z][:, hh * 384 + 256:hh * 384 + 384],
                    Am[z][:, hh * 384 + 256:hh * 384 + 384],
                    0.5, triu[:, :], OP.add, OP.mult))
            xn_lastuse[ci] = p.cnt["tensor"]
            pre_out[ci] = {"am": c_am, "qf2": c_qf2, "vtm": c_vtm, "kf2": c_kf2,
                           "ktm": c_ktm, "kp": c_kp}

        def emit_post(ci):
            z = ci % 2
            pre = pre_out[ci]
            if ci % (SEQ // CH) == 0:
                if ci > 0:
                    W("vector", "tensor", p.cnt["tensor"])
                    W("vector", "scalar", state_ready["scalar"])
                    W("vector", "vector", max(state_ready["vector"], s0_prev[0]))
                for zz in (S2f, S1f, S0c, S2b, S1b):
                    c_z = V(lambda h, zz=zz: h.memset(zz[:, :], 0.0))
                state_ready["vector"] = c_z

            W("tensor", "vector", state_ready["vector"])
            W("tensor", "scalar", state_ready["scalar"])
            W("tensor", "vector", pre["am"])
            W("tensor", "vector", pre["qf2"])
            W("tensor", "scalar", pre["vtm"])
            stats_first = True
            c_st = None
            ysb_guard = 0
            for pr in range(8):
                py = 4 if pr % 2 == 0 else 5
                use_ps(py)
                tile = pr // 4
                rg = 32 * (pr % 4)
                c_yl = None
                for hi in range(2):
                    hh = 2 * pr + hi
                    eb = hi * 64
                    sbase = 128 * (pr // 4) + 64 * hi
                    T(lambda h, tile=tile, rg=rg, eb=eb, sbase=sbase, py=py, z=z: h.matmul(
                        PS[py][eb:eb + 64, 0:256], S1b[rg:rg + 32, sbase:sbase + 64],
                        qf[z][tile][rg:rg + 32, :], start=True, stop=False,
                        tile_position=(rg, eb)))
                    T(lambda h, hh=hh, eb=eb, py=py, z=z: h.matmul(
                        PS[py][eb:eb + 64, 0:256], vtm[z][0][:, hh * 64:(hh + 1) * 64],
                        Am[z][:, hh * 384:hh * 384 + 256], start=False, stop=False))
                    T(lambda h, hh=hh, eb=eb, py=py, z=z: h.matmul(
                        PS[py][eb:eb + 64, 128:256], vtm[z][1][:, hh * 64:(hh + 1) * 64],
                        Am[z][:, hh * 384 + 256:hh * 384 + 384], start=False, stop=False))
                    for Tt in range(2):
                        c_yl = T(lambda h, hh=hh, eb=eb, Tt=Tt, py=py, z=z,
                                 lst=(Tt == 1): h.matmul(
                            PS[py][eb:eb + 64, 0:256],
                            S2b[:, (2 * hh + Tt) * 64:(2 * hh + Tt + 1) * 64],
                            qf2[z][:, (2 * hh + Tt) * 256:(2 * hh + Tt + 1) * 256],
                            start=False, stop=lst))
                W("scalar", "tensor", c_yl)
                W("scalar", "vector", state_ready["vector"])
                if not stats_first:
                    W("scalar", "tensor", c_st)
                c_ysq = S(lambda h, pr=pr, py=py: h.activation(
                    ysq[:, :], PS[py][:, 0:256], AF.Square, bias=S0c[:, pr:pr + 1]))
                W("scalar", "scalar", c_ysq)
                c_ysb = S(lambda h, pr=pr, py=py: h.activation(
                    yrms[:, pr * 256:(pr + 1) * 256], PS[py][:, 0:256], AF.Identity,
                    bias=S0c[:, pr:pr + 1]))
                guard_ps(py, "scalar", c_ysb)
                ysb_guard = c_ysb
                if stats_first:
                    use_ps(6)
                W("tensor", "scalar", c_ysq)
                c_st = T(lambda h, pr=pr, s=stats_first: h.matmul(
                    PS[6][0:32, 0:256], bo[:, pr * 32:(pr + 1) * 32], ysq[:, :],
                    start=s, stop=(pr == 7)))
                stats_first = False
            W("vector", "tensor", c_st)
            c_z2 = V(lambda h: h.tensor_scalar(rstd[:, :], PS[6][0:32, 0:256], 1.0 / HD,
                                               RMS_EPS, OP.mult, OP.add))
            guard_ps(6, "vector", c_z2)
            W("scalar", "vector", c_z2)
            c_l = S(lambda h: h.activation(rstd[:, :], rstd[:, :], AF.Ln))
            W("scalar", "scalar", c_l)
            c_e = S(lambda h: h.activation(rstd[:, :], rstd[:, :], AF.Exp, scale=-0.5))
            c_yrm = None
            for pr in range(8):
                use_ps(7)
                W("tensor", "scalar", c_e)
                W("tensor", "scalar", ysb_guard)
                c_bc = T(lambda h, pr=pr: h.matmul(
                    PS[7][:, 0:256], sel[:, pr * 128:(pr + 1) * 128], rstd[:, :],
                    start=True, stop=True))
                W("vector", "tensor", c_bc)
                if c_yrm is not None:
                    W("vector", "vector", c_yrm)
                c_rg = V(lambda h: h.tensor_scalar(rstdg[:, :], PS[7][:, 0:256],
                                                   gwcol[:, 0:1], None, OP.mult))
                guard_ps(7, "vector", c_rg)
                W("vector", "vector", c_rg)
                W("vector", "scalar", ysb_guard)
                c_yrm = V(lambda h, pr=pr: h.tensor_tensor(
                    yrms[:, pr * 256:(pr + 1) * 256], yrms[:, pr * 256:(pr + 1) * 256],
                    rstdg[:, :], OP.mult))

            c_se = None
            for m in range(8):
                po = 6 if m % 2 == 0 else 7
                use_ps(po)
                W("tensor", "vector", c_yrm)
                for k in range(8):
                    c_o = T(lambda h, m=m, k=k, po=po: h.matmul(
                        PS[po][:, 0:256], woT[:, k * 1024 + m * 128:k * 1024 + (m + 1) * 128],
                        yrms[:, k * 256:(k + 1) * 256], start=(k == 0), stop=(k == 7)))
                W("scalar", "tensor", c_o)
                if ci >= 1:
                    WD("scalar", s_spill[ci - 1])
                c_se = S(lambda h, m=m, po=po: h.copy(
                    sstg[:, m * 256:(m + 1) * 256], PS[po][:, 0:256]))
                guard_ps(po, "scalar", c_se)
            W("gpsimd", "scalar", c_se)
            s_spill[ci] = DQ("spill_s", lambda h, ci=ci: h.dma_start(
                s_sc.rearrange("(pt p) (ci c) -> p ci pt c", p=128, ci=NCHUNK)[:, ci],
                sstg.rearrange("p (pt c) -> p pt c", pt=8)))

            if ci % (SEQ // CH) == (SEQ // CH) - 1:
                post_done[z] = p.cnt["tensor"]
                return
            c_s2cp = None
            for grp in range(4):
                pg_ = 4 if grp % 2 == 0 else 5
                use_ps(pg_)
                W("tensor", "gpsimd", pre["kf2"])
                for gi in range(4):
                    hh = grp * 4 + gi
                    for Tt in range(2):
                        for half in range(2):
                            c_u = T(lambda h, hh=hh, Tt=Tt, half=half, gi=gi, pg_=pg_,
                                    z=z: h.matmul(
                                PS[pg_][:, (gi * 2 + Tt) * 64:(gi * 2 + Tt + 1) * 64],
                                kf2[z][half][:, hh * 256 + Tt * 128:hh * 256 + (Tt + 1) * 128],
                                vtm[z][half][:, hh * 64:(hh + 1) * 64],
                                start=(half == 0), stop=(half == 1)))
                W("vector", "tensor", c_u)
                c_ad = V(lambda h, grp=grp, pg_=pg_: h.tensor_tensor(
                    S2f[:, grp * 512:(grp + 1) * 512], S2f[:, grp * 512:(grp + 1) * 512],
                    PS[pg_][:, :], OP.add))
                guard_ps(pg_, "vector", c_ad)
                W("scalar", "vector", c_ad)
                c_s2cp = S(lambda h, grp=grp: h.copy(S2b[:, grp * 512:(grp + 1) * 512],
                                                     S2f[:, grp * 512:(grp + 1) * 512]))
            use_ps(6)
            W("tensor", "vector", pre["ktm"])
            for pr in range(8):
                tile = pr // 4
                rg = 32 * (pr % 4)
                for half in range(2):
                    c_s1 = T(lambda h, pr=pr, half=half, tile=tile, rg=rg, z=z: h.matmul(
                        PS[6][rg:rg + 32, tile * 128:(tile + 1) * 128],
                        ktm[z][half][:, pr * 32:(pr + 1) * 32],
                        vtm[z][half][:, pr * 128:(pr + 1) * 128],
                        start=(half == 0), stop=(half == 1), tile_position=(0, rg)))
            W("vector", "tensor", c_s1)
            c_s1a = V(lambda h: h.tensor_tensor(S1f[:, :], S1f[:, :], PS[6][:, 0:256], OP.add))
            guard_ps(6, "vector", c_s1a)
            W("vector", "vector", c_s1a)
            c_s1b = V(lambda h: h.tensor_tensor(S1b[:, :], S1f[:, :], s1mask[:, :], OP.mult))
            use_ps(7)
            for pr in range(8):
                for half in range(2):
                    c_s0 = T(lambda h, pr=pr, half=half, z=z: h.matmul(
                        PS[7][:, (pr * 2 + half):(pr * 2 + half + 1)],
                        vtm[z][half][:, pr * 128:(pr + 1) * 128], onescolb[:, 0:1],
                        start=True, stop=True))
            W("vector", "tensor", c_s0)
            W("vector", "scalar", ysb_guard)
            W("vector", "vector", s0_prev[0])
            c_s0a = V(lambda h: h.tensor_tensor(
                S0c[:, :], S0c[:, :],
                PS[7][:, 0:16].rearrange("p (a b) -> p a b", a=8)[:, :, 0:1].rearrange(
                    "p a b -> p (a b)"), OP.add))
            W("vector", "vector", c_s0a)
            c_s0b = V(lambda h: h.tensor_tensor(
                S0c[:, :], S0c[:, :],
                PS[7][:, 0:16].rearrange("p (a b) -> p a b", a=8)[:, :, 1:2].rearrange(
                    "p a b -> p (a b)"), OP.add))
            guard_ps(7, "vector", c_s0b)
            s0_prev[0] = c_s0b
            state_ready["vector"] = c_s1b if c_s0b < c_s1b else c_s0b
            state_ready["scalar"] = c_s2cp
            post_done[z] = p.cnt["tensor"]

        emit_pre(0)
        for ci in range(NCHUNK):
            if ci + 1 < NCHUNK:
                emit_pre(ci + 1)
            emit_post(ci)
        c_slast = s_spill[NCHUNK - 1]
        p.barrier()
        aes.close()

    if True:
        ces = ExitStack()
        def csb(name, shape, dt):
            return ces.enter_context(nc.sbuf_tensor(name, shape, dt))
        wdTq = csb("wdTq", [128, 22 * 1024], BF16)
        rh = csb("rh", [128, 2 * 22 * 512], BF16)
        uhl = csb("uhl", [128, 4 * 512], F32)
        gbuf = csb("gbuf", [128, 8 * 512], BF16)
        sgbuf = csb("sgbuf", [128, 512], F32)
        xb2 = csb("xb2", [128, 2 * 8 * 512], F32)
        sld = csb("sld", [128, 8 * 512], BF16)
        outb = csb("outb", [128, 2 * 8 * 512], F32)

        WD("sync", t_wdq)
        t_wdl = DQ("wdl", lambda h: h.dma_start(wdTq[:, :], wd_sc))
        WT("sync", "spill_uh")
        WT("sync", "spill_s")
        qg_v = {}
        qg_g = {}
        dn_guard = {0: 0, 1: 0}
        prev_cb = [0]
        comb_marks = {}
        dn_last = {}
        out_marks = {}
        th_marks = {}
        rq_last = {}

        def emit_quant(tt):
            rb_ = tt % 2
            rq = {}
            for k in range(22):
                ub = (tt * 22 + k) % 4
                qb = (tt * 22 + k) % 2
                gi = tt * 22 + k
                if gi >= 4:
                    W("sync", "scalar", th_marks[gi - 4])
                t_lu = DQ("uhl", lambda h, k=k, tt=tt, b=ub: h.dma_start(
                    uhl[:, b * 512:(b + 1) * 512],
                    uh_sc[k * 128:(k + 1) * 128, tt * 512:(tt + 1) * 512]))
                WD("scalar", t_lu)
                W("scalar", "vector", c_sd)
                if gi >= 2:
                    W("scalar", "vector", rq_last.get(gi - 2, 0))
                c_th = S(lambda h, b=ub, qb=qb: h.activation(
                    qth2[:, qb * 512:(qb + 1) * 512], uhl[:, b * 512:(b + 1) * 512],
                    AF.Identity, bias=0.5, scale=invsd_col[:, 0:1]))
                th_marks[gi] = c_th
                W("vector", "scalar", c_th)
                if gi >= 2:
                    W("vector", "vector", rq_last.get(gi - 2, 0))
                if k == 0 and tt >= 2:
                    W("vector", "tensor", dn_guard[rb_])
                c_cl = V(lambda h, qb=qb: h.tensor_scalar(
                    qth2[:, qb * 512:(qb + 1) * 512], qth2[:, qb * 512:(qb + 1) * 512],
                    127.9, -127.9, OP.min, OP.max))
                W("scalar", "vector", c_cl)
                c_i = S(lambda h, qb=qb: h.copy(
                    qi32[:, qb * 512:(qb + 1) * 512], qth2[:, qb * 512:(qb + 1) * 512]))
                W("vector", "scalar", c_i)
                c_g = V(lambda h, qb=qb: h.tensor_tensor(
                    qgt2[:, qb * 512:(qb + 1) * 512], qi32[:, qb * 512:(qb + 1) * 512],
                    qth2[:, qb * 512:(qb + 1) * 512], OP.is_gt))
                qg_v[gi] = c_g
                W("vector", "vector", c_g)
                rq[k] = V(lambda h, k=k, qb=qb, rb_=rb_: h.scalar_tensor_tensor(
                    rh[:, (rb_ * 22 + k) * 512:(rb_ * 22 + k + 1) * 512],
                    qgt2[:, qb * 512:(qb + 1) * 512], -1.0,
                    qi32[:, qb * 512:(qb + 1) * 512], OP.mult, OP.add))
                rq_last[gi] = rq[k]
            return rq

        def emit_down(tt, rq_of):
            ob = tt % 2
            rb_ = tt % 2
            if tt >= 2:
                W("sync", "vector", comb_marks[tt - 2])
            if tt >= 2:
                WD("vector", out_marks[tt - 2])
            t_lx = DQ("xA", lambda h, tt=tt, b=ob: h.dma_start(
                xb2.rearrange("p (z pt c) -> p z pt c", z=2, pt=8)[:, b],
                x_h.rearrange("(pt p) (tt c) -> p tt pt c", p=128, tt=8)[:, tt]))
            if tt >= 1:
                W("sync", "vector", comb_marks[tt - 1])
            t_ls = DQ("cls", lambda h, tt=tt: h.dma_start(
                sld.rearrange("p (pt c) -> p pt c", pt=8),
                s_sc.rearrange("(pt p) (tt c) -> p tt pt c", p=128, tt=8)[:, tt]))
            WD("tensor", t_wdl)
            for hm in range(8):
                use_ps(hm)
            for k in range(22):
                W("tensor", "vector", rq_of[k])
                for hm in range(8):
                    c_d = T(lambda h, hm=hm, k=k, rb_=rb_: h.matmul(
                        PS[hm][:, :], wdTq[:, k * 1024 + hm * 128:k * 1024 + (hm + 1) * 128],
                        rh[:, (rb_ * 22 + k) * 512:(rb_ * 22 + k + 1) * 512],
                        start=(k == 0), stop=(k == 21)))
            for hm in range(8):
                W("scalar", "tensor", c_d)
                if tt >= 1:
                    W("scalar", "vector", comb_marks[tt - 1])
                c_gv = S(lambda h, hm=hm: h.activation(
                    gbuf[:, hm * 512:(hm + 1) * 512], PS[hm][:, :], AF.Silu,
                    scale=scd[:, hm:hm + 1]))
                guard_ps(hm, "scalar", c_gv)
                W("vector", "scalar", c_gv)
                WD("vector", t_lx)
                WD("vector", t_ls)
                W("vector", "vector", prev_cb[0])
                c_sg = V(lambda h, hm=hm: h.tensor_tensor(
                    sgbuf[:, :], sld[:, hm * 512:(hm + 1) * 512],
                    gbuf[:, hm * 512:(hm + 1) * 512], OP.mult))
                W("vector", "vector", c_sg)
                c_cb = V(lambda h, hm=hm, b=ob: h.tensor_tensor(
                    outb[:, (b * 8 + hm) * 512:(b * 8 + hm) * 512 + 512],
                    xb2[:, (b * 8 + hm) * 512:(b * 8 + hm) * 512 + 512], sgbuf[:, :], OP.add))
                prev_cb[0] = c_cb
            dn_last[tt] = c_d
            dn_guard[rb_] = c_d
            comb_marks[tt] = c_cb
            W("sync", "vector", c_cb)
            out_marks[tt] = DQ("out", lambda h, tt=tt, b=ob: h.dma_start(
                o_out.rearrange("(pt p) (tt c) -> p tt pt c", p=128, tt=8)[:, tt],
                outb.rearrange("p (z pt c) -> p z pt c", z=2, pt=8)[:, b]))

        rq_pend = emit_quant(0)
        for tt in range(8):
            nxt = emit_quant(tt + 1) if tt + 1 < 8 else None
            emit_down(tt, rq_pend)
            rq_pend = nxt
        WT("sync", "out")

        with nc.Block() as blk:
            p.emit_block(blk)
        ces.close()
    return es


def _stage_constants():
    bf = ml_dtypes.bfloat16
    c = {}
    c["triu_h"] = np.triu(np.ones((128, 128), np.float32)).astype(bf)
    c["corrA_h"] = np.concatenate([0.5 * np.triu(np.ones((128, 128), np.float32)),
                                   0.5 * np.ones((128, 128), np.float32)], axis=1).astype(bf)
    c["corrB_h"] = (0.5 * np.triu(np.ones((128, 128), np.float32))).astype(bf)
    for par in range(2):
        ea = np.zeros((32, 256), np.float32)
        eb = np.zeros((32, 256), np.float32)
        for Tt in range(2):
            for il in range(8):
                for j in range(16):
                    col = Tt * 128 + il * 16 + j
                    i = il + 8 * Tt
                    ea[16 * par + i, col] = 0.25
                    eb[16 * par + j, col] = 0.125
        c[f"EA{par}_h"] = np.tile(ea, (4, 1)).astype(bf)
        c[f"EB{par}_h"] = np.tile(eb, (4, 1)).astype(bf)
    m = np.zeros((128, 256), np.float32)
    for a in range(4):
        for f in range(32):
            for cc_ in range(2):
                if (f < 16) == (cc_ == 0):
                    for b in range(2):
                        m[32 * a + f, 128 * b + 64 * cc_:128 * b + 64 * cc_ + 64] = 0.25
    c["s1mask_h"] = m
    bov = np.zeros((128, 256), np.float32)
    for pr in range(8):
        bov[0:64, pr * 32 + 2 * pr] = 1.0
        bov[64:128, pr * 32 + 2 * pr + 1] = 1.0
    c["bo_h"] = bov.astype(bf)
    selm = np.zeros((32, 1024), np.float32)
    for pr in range(8):
        selm[2 * pr, pr * 128:pr * 128 + 64] = 1.0
        selm[2 * pr + 1, pr * 128 + 64:pr * 128 + 128] = 1.0
    c["sel_h"] = selm
    pm = np.arange(128) % 32
    c["emaske_h"] = (pm < 16).astype(np.float32).reshape(128, 1)
    c["emasko_h"] = (pm >= 16).astype(np.float32).reshape(128, 1)
    c["onescol_h"] = np.ones((128, 1), np.float32)
    c["onescolb_h"] = np.ones((128, 1), np.float32).astype(bf)
    c["ones1r_h"] = np.ones((1, 128), np.float32)
    c["ident_h"] = np.eye(128, dtype=np.float32).astype(bf)
    cv = np.zeros((128, 8), np.float32)
    for j, v in enumerate([1e-5, 0.5, 4.0 / math.sqrt(32.0), 1.0 / math.sqrt(32.0), -0.5, 1.0]):
        cv[:, j] = v
    c["cvals_h"] = cv
    return c


def kernel(L0_attn_hidden_states, L0_attn_Wq, L0_attn_Wk, L0_attn_Wv, L0_attn_Wo,
           L0_attn_g_weight, L0_ffn_clip_x, L0_ffn_clip_down, L0_ffn_w_gate,
           L0_ffn_w_up, L0_ffn_w_down):
    x = np.asarray(L0_attn_hidden_states, np.float32)
    B, L, _ = x.shape
    x_fm = np.ascontiguousarray(x.reshape(B * L, D).T)

    shared = {
        "wqT_h": np.ascontiguousarray(np.asarray(L0_attn_Wq, np.float32).T),
        "wkT_h": np.ascontiguousarray(np.asarray(L0_attn_Wk, np.float32).T),
        "wvT_h": np.ascontiguousarray(np.asarray(L0_attn_Wv, np.float32).T),
        "woT_h": np.ascontiguousarray(np.asarray(L0_attn_Wo, np.float32).T),
        "gw_h": np.asarray(L0_attn_g_weight, np.float32).reshape(1, HD),
        "clipx_h": np.asarray(L0_ffn_clip_x, np.float32).reshape(1, 1),
        "clipd_h": np.asarray(L0_ffn_clip_down, np.float32).reshape(1, 1),
        "wg_nat_h": np.ascontiguousarray(np.asarray(L0_ffn_w_gate, np.float32)),
        "wu_nat_h": np.ascontiguousarray(np.asarray(L0_ffn_w_up, np.float32)),
        "wd_nat_h": np.ascontiguousarray(np.asarray(L0_ffn_w_down, np.float32)),
        "wgT_h": np.ascontiguousarray(np.asarray(L0_ffn_w_gate, np.float32).T),
        "wuT_h": np.ascontiguousarray(np.asarray(L0_ffn_w_up, np.float32).T),
        "wdT_h": np.ascontiguousarray(np.asarray(L0_ffn_w_down, np.float32).T),
    }
    shared.update(_stage_constants())

    nc = bass.Bass("TRN2", target_bir_lowering=False, debug=False)
    es = build(nc)

    in_maps = []
    for c in range(8):
        mm = dict(shared)
        mm["x_h"] = np.ascontiguousarray(x_fm[:, c * NT:(c + 1) * NT])
        in_maps.append(mm)
    res = run_bass_kernel_spmd(nc, in_maps, core_ids=list(range(8)))
    es.close()

    out_fm = np.concatenate([res.results[c]["o_out"] for c in range(8)], axis=1)
    out = np.ascontiguousarray(out_fm.T).reshape(B, L, D).astype(np.float32)
    return out


# version 17 (final)
# speedup vs baseline: 1.0240x; 1.0240x over previous
import math
import os
import numpy as np
import ml_dtypes
import concourse.bass as bass
import concourse.mybir as mybir
from concourse.alu_op_type import AluOpType as OP
from concourse.bass_utils import run_bass_kernel_spmd
from contextlib import ExitStack

F32 = mybir.dt.float32
BF16 = mybir.dt.bfloat16
I32 = mybir.dt.int32
AF = mybir.ActivationFunctionType
AX = mybir.AxisListType

D = 1024
NT = 4096
SEQ = 2048
NH = 16
FD = 16
HD = 64
CH = 256
NCHUNK = NT // CH
I_FFN = 2816
RMS_EPS = 1e-5
LN_EPS = 1e-5
QMAX = 127.0
S32 = math.sqrt(32.0)

ENGS = ("tensor", "vector", "scalar", "sync", "gpsimd")


class Prog:
    def __init__(self, nc):
        self.nc = nc
        self.items = {e: [] for e in ENGS}
        self.cnt = {e: 0 for e in ENGS}
        self.sem = {}
        self.last_wait = {}

    def alloc_sems(self, es):
        for e in ENGS:
            self.sem[e] = es.enter_context(self.nc.semaphore("sem_" + e))
        self._es = es
        self.streams = {}

    def stream(self, name, queue, depth):
        sems = [self._es.enter_context(self.nc.semaphore(f"dma_{name}_{i}"))
                for i in range(depth)]
        self.streams[name] = [queue, depth, sems, 0]
        return name

    def dma(self, name, fn):
        queue, depth, sems, k = self.streams[name]
        sem = sems[k % depth]
        self.streams[name][3] = k + 1
        self.items[queue].append((lambda h, f=fn, s=sem: f(h).then_inc(s, 16), 0))
        return (name, k)

    def wait_dma(self, eng, token):
        if token is None:
            return
        name, k = token
        queue, depth, sems, _ = self.streams[name]
        sem = sems[k % depth]
        val = 16 * (k // depth + 1)
        key = (eng, name, k % depth)
        if self.last_wait.get(key, -1) >= val:
            return
        self.last_wait[key] = val
        self.items[eng].append((lambda h, s=sem, v=val: h.wait_ge(s, v), 0))

    def wait_stream_total(self, eng, name):
        queue, depth, sems, k = self.streams[name]
        for i in range(depth):
            n_on = (k - i + depth - 1) // depth
            if n_on > 0:
                self.items[eng].append(
                    (lambda h, s=sems[i], v=16 * n_on: h.wait_ge(s, v), 0))

    def do(self, eng, fn, inc):
        self.cnt[eng] += inc
        self.items[eng].append((fn, inc))
        return self.cnt[eng]

    def raw(self, eng, fn):
        self.items[eng].append((fn, 0))

    def wait(self, eng, on_eng, val):
        if val is None or val <= 0:
            return
        assert val <= self.cnt[on_eng], f"{eng} waits {on_eng}>={val} cnt={self.cnt[on_eng]}"
        key = (eng, on_eng)
        if self.last_wait.get(key, -1) >= val:
            return
        self.last_wait[key] = val
        sem = self.sem[on_eng]
        self.items[eng].append((lambda h, s=sem, v=val: h.wait_ge(s, v), 0))

    def barrier(self):
        for name, (queue, depth, sems, k) in self.streams.items():
            if k > 0:
                self.wait_stream_total(queue, name)
        for e in ("sync", "gpsimd"):
            self.do(e, lambda h: h.nop(), 1)
        snap = dict(self.cnt)
        for e in ENGS:
            for o in ENGS:
                if e != o:
                    self.wait(e, o, snap[o])

    def emit_block(self, blk):
        for e in ENGS:
            def mk(engname):
                def _(h):
                    sem = self.sem[engname]
                    for fn, inc in self.items[engname]:
                        inst = fn(h)
                        if inc:
                            inst.then_inc(sem, inc)
                return _
            getattr(blk, e)(mk(e))


def build(nc):
    PHASELIM = int(os.environ.get("K_PHASELIM", "9"))
    es = ExitStack()
    p = Prog(nc)
    p.alloc_sems(es)
    sem_cc = es.enter_context(nc.semaphore("sem_cc"))
    cc_state = [0]

    def T(fn): return p.do("tensor", fn, 1)
    def V(fn): return p.do("vector", fn, 1)
    def S(fn): return p.do("scalar", fn, 1)
    def W(eng, on, val): p.wait(eng, on, val)
    DQ = p.dma
    WD = p.wait_dma
    WT = p.wait_stream_total
    for nm, q, d in [("consts", "gpsimd", 1), ("bounce", "gpsimd", 1),
                     ("spill_xn", "gpsimd", 2), ("spill_uh", "gpsimd", 2),
                     ("spill_s", "gpsimd", 2), ("wdsp", "gpsimd", 1),
                     ("attnw", "gpsimd", 4),
                     ("wqsp_d", "gpsimd", 2), ("wqsp_g", "gpsimd", 2), ("wqsp_u", "gpsimd", 2),
                     ("wnat", "sync", 2), ("wtr", "sync", 2),
                     ("spill_xf", "gpsimd", 4), ("xfl", "sync", 4),
                     ("xA", "sync", 2), ("xnch", "sync", 2),
                     ("wcolg", "sync", 2), ("wcolu", "sync", 2),
                     ("wdl", "sync", 1), ("uhl", "sync", 4),
                     ("cls", "sync", 2), ("out", "sync", 2)]:
        p.stream(nm, q, d)

    def din(name, shape, dt=F32):
        return nc.dram_tensor(name, shape, dt, kind="ExternalInput").ap()

    x_h = din("x_h", [D, NT])
    wqT_h = din("wqT_h", [D, 256]); wkT_h = din("wkT_h", [D, 256])
    wvT_h = din("wvT_h", [D, D]); woT_h = din("woT_h", [D, D])
    gw_h = din("gw_h", [1, HD])
    clipx_h = din("clipx_h", [1, 1]); clipd_h = din("clipd_h", [1, 1])
    wg_nat_h = din("wg_nat_h", [I_FFN, D]); wu_nat_h = din("wu_nat_h", [I_FFN, D])
    wd_nat_h = din("wd_nat_h", [D, I_FFN])
    wgT_h = din("wgT_h", [D, I_FFN]); wuT_h = din("wuT_h", [D, I_FFN])
    wdT_h = din("wdT_h", [I_FFN, D])
    triu_h = din("triu_h", [128, 128], BF16)
    corrA_h = din("corrA_h", [128, 256], BF16)
    corrB_h = din("corrB_h", [128, 128], BF16)
    EA0_h = din("EA0_h", [128, 256], BF16); EA1_h = din("EA1_h", [128, 256], BF16)
    EB0_h = din("EB0_h", [128, 256], BF16); EB1_h = din("EB1_h", [128, 256], BF16)
    s1mask_h = din("s1mask_h", [128, 256])
    bo_h = din("bo_h", [128, 256], BF16)
    sel_h = din("sel_h", [32, 1024], BF16)
    emaske_h = din("emaske_h", [128, 1]); emasko_h = din("emasko_h", [128, 1])
    onescol_h = din("onescol_h", [128, 1]); onescolb_h = din("onescolb_h", [128, 1], BF16)
    ones1r_h = din("ones1r_h", [1, 128])
    ident_h = din("ident_h", [128, 128], BF16)
    cvals_h = din("cvals_h", [128, 8])

    o_out = nc.dram_tensor("o_out", [D, NT], F32, kind="ExternalOutput").ap()

    xn_sc = nc.dram_tensor("xn_sc", [D, NT], BF16).ap()
    xf_sc = nc.dram_tensor("xf_sc", [D, NT], F32).ap()
    uh_sc = nc.dram_tensor("uh_sc", [I_FFN, NT], F32).ap()
    s_sc = nc.dram_tensor("s_sc", [D, NT], BF16).ap()
    wg_sc = nc.dram_tensor("wg_sc", [128, 8 * I_FFN], BF16).ap()
    wu_sc = nc.dram_tensor("wu_sc", [128, 8 * I_FFN], BF16).ap()
    wd_sc = nc.dram_tensor("wd_sc", [128, 22 * 1024], BF16).ap()
    mr_sc = nc.dram_tensor("mr_sc", [1, 4096], F32).ap()
    cc_t = [(nc.dram_tensor("cc1_in", [1, 64], F32), nc.dram_tensor("cc1_out", [1, 64], F32)),
            (nc.dram_tensor("cc2_in", [1, 64], F32), nc.dram_tensor("cc2_out", [1, 64], F32))]

    def sbp(name, shape, dt):
        return es.enter_context(nc.sbuf_tensor(name, shape, dt))

    triu = sbp("triu", [128, 128], BF16)
    corrA = sbp("corrA", [128, 256], BF16)
    corrB = sbp("corrB", [128, 128], BF16)
    EA = [sbp("EA0", [128, 256], BF16), sbp("EA1", [128, 256], BF16)]
    EB = [sbp("EB0", [128, 256], BF16), sbp("EB1", [128, 256], BF16)]
    s1mask = sbp("s1mask", [128, 256], F32)
    bo = sbp("bo", [128, 256], BF16)
    sel = sbp("sel", [32, 1024], BF16)
    emaske = sbp("emaske", [128, 1], F32)
    emasko = sbp("emasko", [128, 1], F32)
    onescol = sbp("onescol", [128, 1], F32)
    onescolb = sbp("onescolb", [128, 1], BF16)
    ones1r = sbp("ones1r", [1, 128], F32)
    ident = sbp("ident", [128, 128], BF16)
    gwcol = sbp("gwcol", [128, 1], F32)
    clips = sbp("clips", [1, 2], F32)
    maxst = sbp("maxst", [128, 64], F32)
    maxst2 = sbp("maxst2", [128, 192], F32)
    gmcol = sbp("gmcol", [128, 1], F32)
    rmg = sbp("rmg", [128, 22], F32)
    rmu = sbp("rmu", [128, 22], F32)
    rmd = sbp("rmd", [128, 8], F32)
    scg = sbp("scg", [128, 22], F32)
    scu = sbp("scu", [128, 22], F32)
    scd = sbp("scd", [128, 8], F32)
    invsx_col = sbp("invsx_col", [128, 1], F32)
    invsd_col = sbp("invsd_col", [128, 1], F32)
    smallrow = sbp("smallrow", [1, 128], F32)
    qth2 = sbp("qth2", [128, 2 * 512], F32)
    qi32 = sbp("qi32", [128, 2 * 512], I32)
    qgt2 = sbp("qgt2", [128, 2 * 512], F32)
    cvals = sbp("cvals", [128, 8], F32)
    _CONSTVALS = [LN_EPS, 0.5, 4.0 / S32, 1.0 / S32, -0.5, 1.0]
    for _j, _v in enumerate(_CONSTVALS):
        nc.const_aps.register(F32, float(_v), cvals[:, _j:_j + 1])

    PS = [es.enter_context(nc.psum_tensor(f"ps{i}", [128, 512], F32)) for i in range(8)]
    ps_guard = {i: [] for i in range(8)}

    def use_ps(i):
        for eng, cnt in ps_guard[i]:
            W("tensor", eng, cnt)
        ps_guard[i] = []

    def guard_ps(i, eng, cnt):
        ps_guard[i].append((eng, cnt))

    for dst, src in [(triu, triu_h), (corrA, corrA_h), (corrB, corrB_h),
                     (EA[0], EA0_h), (EA[1], EA1_h), (EB[0], EB0_h), (EB[1], EB1_h),
                     (s1mask, s1mask_h), (bo, bo_h), (sel, sel_h),
                     (emaske, emaske_h), (emasko, emasko_h),
                     (onescol, onescol_h), (onescolb, onescolb_h), (ones1r, ones1r_h),
                     (ident, ident_h)]:
        DQ("consts", lambda h, d=dst, s=src: h.dma_start(d[:, :], s))
    DQ("consts", lambda h: h.dma_start(gwcol[0:64, 0:1], gw_h.rearrange("o (d u) -> (o d) u", u=1)))
    DQ("consts", lambda h: h.dma_start(gwcol[64:128, 0:1], gw_h.rearrange("o (d u) -> (o d) u", u=1)))
    DQ("consts", lambda h: h.dma_start(clips[0:1, 0:1], clipx_h))
    DQ("consts", lambda h: h.dma_start(clips[0:1, 1:2], clipd_h))
    DQ("consts", lambda h: h.dma_start(cvals[:, :], cvals_h))

    def global_scale(stash, ncols, cc_idx, clip_slot, gmax_slot, sig_slot, s_slot, inv_slot,
                     inv_col, rm_list):
        c_f = V(lambda h, n=ncols: h.reduce_max(gmcol[:, 0:1], stash[:, 0:n], axis=AX.X))
        W("gpsimd", "vector", c_f)
        t_o = DQ("bounce", lambda h: h.dma_start(
            mr_sc[0:1, 0:128].rearrange("o (a b) -> (o b) a", a=1), gmcol[:, 0:1],
            allow_slow_non_contiguous=True))
        WD("gpsimd", t_o)
        t_i = DQ("bounce", lambda h: h.dma_start(smallrow[0:1, :], mr_sc[0:1, 0:128]))
        WD("vector", t_i)
        c_r = V(lambda h, sl=gmax_slot: h.reduce_max(smallrow[0:1, sl:sl + 1], smallrow[0:1, 0:128],
                                                     axis=AX.X))
        W("gpsimd", "vector", c_r)
        tin, tout = cc_t[cc_idx]
        t_b = DQ("bounce", lambda h, t=tin: h.dma_start(t.ap()[0:1, 0:64],
                                                        smallrow[0:1, 0:64]))
        WD("gpsimd", t_b)
        cc_state[0] += 1
        tgt = cc_state[0]
        p.raw("gpsimd", lambda h, ti=tin, to=tout: h.collective_compute(
            "AllReduce", OP.max, replica_groups=[list(range(8))],
            ins=[ti.ap().opt()], outs=[to.ap().opt()]).then_inc(sem_cc, 1))
        p.raw("gpsimd", lambda h, t=tgt: h.wait_ge(sem_cc, t))
        t_gi = DQ("bounce", lambda h, sl=gmax_slot, t=tout: h.dma_start(
            smallrow[0:1, sl:sl + 1], t.ap()[0:1, sl:sl + 1]))
        WD("scalar", t_gi)
        WD("vector", t_gi)
        c_sg = S(lambda h, a=sig_slot, cs=clip_slot: h.activation(
            smallrow[0:1, a:a + 1], clips[0:1, cs:cs + 1], AF.Sigmoid))
        W("vector", "scalar", c_sg)
        c1 = V(lambda h, a=sig_slot, g=gmax_slot, s=s_slot: h.tensor_tensor(
            smallrow[0:1, s:s + 1], smallrow[0:1, g:g + 1], smallrow[0:1, a:a + 1], OP.mult))
        W("vector", "vector", c1)
        c2 = V(lambda h, s=s_slot: h.tensor_scalar(
            smallrow[0:1, s:s + 1], smallrow[0:1, s:s + 1], 1.0 / QMAX, None, OP.mult))
        W("vector", "vector", c2)
        c3 = V(lambda h, s=s_slot, iv=inv_slot: h.reciprocal(
            smallrow[0:1, iv:iv + 1], smallrow[0:1, s:s + 1]))
        use_ps(6)
        W("tensor", "vector", c3)
        c_mm = T(lambda h, s=s_slot: h.matmul(PS[6][:, 0:2], ones1r[0:1, :],
                                              smallrow[0:1, s:s + 2], start=True, stop=True))
        W("vector", "tensor", c_mm)
        c_e = V(lambda h, ic=inv_col: h.tensor_copy(ic[:, 0:1], PS[6][:, 1:2]))
        last = c_e
        for (sc_dst, rm_src) in rm_list:
            W("vector", "vector", last)
            last = V(lambda h, d=sc_dst, r=rm_src: h.tensor_scalar(
                d[:, :], r[:, :], PS[6][:, 0:1], 1.0 / QMAX, OP.mult, OP.mult))
        guard_ps(6, "vector", last)
        return last

    wq_res = {}
    wq_prev = {}
    wq_state = [None, False]

    def pump(budget):
        it = wq_state[0]
        while budget > 0 and not wq_state[1]:
            try:
                budget -= next(it)
            except StopIteration:
                wq_state[1] = True

    def wq_gen(nat_h, wT_h, n_rows, n_k, rmcols, w_out_sc, spst, tag,
               wnat, wtr, invb, invrow, qthw, qr0w, qgtw, wqow):
        nrt = n_rows // 128
        inner = 128 * n_k
        spq = p.streams[spst][0]
        red_marks = {}
        for rt in range(nrt):
            b = rt % 2
            if rt >= 2:
                W("sync", "vector", red_marks[rt - 2])
            elif wq_prev.get("red"):
                W("sync", "vector", wq_prev["red"])
            t_ld = DQ("wnat", lambda h, rt=rt, b=b: h.dma_start(
                wnat[:, b * inner:(b + 1) * inner], nat_h[rt * 128:(rt + 1) * 128, :]))
            WD("vector", t_ld)
            red_marks[rt] = V(lambda h, rt=rt, b=b: h.reduce_max(
                rmcols[:, rt:rt + 1], wnat[:, b * inner:(b + 1) * inner],
                axis=AX.X, apply_absolute_value=True))
            yield 3.2 if n_k == 22 else 1.4
        W("gpsimd", "vector", red_marks[nrt - 1])
        if wq_prev.get("mm"):
            W("gpsimd", "tensor", wq_prev["mm"])
        t_o = DQ("bounce", lambda h, nrt=nrt: h.dma_start(
            mr_sc[0:1, 0:nrt * 128].rearrange("o (a b) -> (o b) a", a=nrt),
            rmcols[:, 0:nrt], allow_slow_non_contiguous=True))
        WD("gpsimd", t_o)
        t_i = DQ("bounce", lambda h, nrt=nrt, n_rows=n_rows: h.dma_start(
            invrow[0:1, 0:n_rows], mr_sc[0:1, 0:nrt * 128]))
        yield 4.0
        WD("vector", t_i)
        c_rec = V(lambda h, n_rows=n_rows: h.reciprocal(invrow[0:1, 0:n_rows],
                                                        invrow[0:1, 0:n_rows]))
        W("vector", "vector", c_rec)
        c_sc = V(lambda h, n_rows=n_rows: h.tensor_scalar(
            invrow[0:1, 0:n_rows], invrow[0:1, 0:n_rows], QMAX, None, OP.mult))
        nb = (n_rows + 511) // 512
        WT("tensor", "consts")
        W("tensor", "vector", c_sc)
        c_ev = None
        c_mm = None
        for bb in range(nb):
            lo, hi = bb * 512, min(n_rows, (bb + 1) * 512)
            use_ps(6)
            c_mm = T(lambda h, lo=lo, hi=hi: h.matmul(
                PS[6][:, 0:hi - lo], ones1r[0:1, :], invrow[0:1, lo:hi], start=True, stop=True))
            W("vector", "tensor", c_mm)
            c_ev = V(lambda h, lo=lo, hi=hi: h.tensor_copy(invb[:, lo:hi], PS[6][:, 0:hi - lo]))
            guard_ps(6, "vector", c_ev)
        c_invb = c_ev
        yield 2.0
        sp_marks = {}
        q_marks = {}
        for k in range(n_k):
            b = k % 2
            if k >= 2:
                W("sync", "vector", q_marks[k - 2])
            elif wq_prev.get("chain"):
                W("sync", "vector", wq_prev["chain"])
            t_ld = DQ("wtr", lambda h, k=k, b=b, n_rows=n_rows: h.dma_start(
                wtr[:, b * n_rows:(b + 1) * n_rows], wT_h[k * 128:(k + 1) * 128, :]))
            WD("vector", t_ld)
            W("vector", "vector", c_invb)
            if k >= 2:
                WD("vector", sp_marks[k - 2])
                W("vector", "vector", q_marks[k - 2])
            elif wq_prev.get("sp2"):
                WD("vector", wq_prev["sp2"][0])
                WD("vector", wq_prev["sp2"][1])
            c1 = V(lambda h, b=b, n_rows=n_rows: h.tensor_tensor(
                qthw[:, 0:n_rows], wtr[:, b * n_rows:(b + 1) * n_rows],
                invb[:, 0:n_rows], OP.mult))
            W("vector", "vector", c1)
            c2 = V(lambda h, n_rows=n_rows: h.tensor_scalar(
                qthw[:, 0:n_rows], qthw[:, 0:n_rows], 0.5, 127.9, OP.add, OP.min))
            W("vector", "vector", c2)
            c4 = V(lambda h, n_rows=n_rows: h.tensor_scalar(
                qr0w[:, 0:n_rows], qthw[:, 0:n_rows], -127.9, None, OP.max))
            W("vector", "vector", c4)
            c5 = V(lambda h, n_rows=n_rows: h.tensor_tensor(
                qgtw[:, 0:n_rows], qr0w[:, 0:n_rows], qthw[:, 0:n_rows], OP.is_gt))
            W("vector", "vector", c5)
            c6 = V(lambda h, b=b, n_rows=n_rows: h.scalar_tensor_tensor(
                wqow[:, b * n_rows:(b + 1) * n_rows], qgtw[:, 0:n_rows], -1.0,
                qr0w[:, 0:n_rows], OP.mult, OP.add))
            q_marks[k] = c6
            W(spq, "vector", c6)
            sp_marks[k] = DQ(spst, lambda h, k=k, b=b, n_rows=n_rows: h.dma_start(
                w_out_sc[:, k * n_rows:(k + 1) * n_rows],
                wqow[:, b * n_rows:(b + 1) * n_rows]))
            yield 4.9 if n_k == 22 else 12.8
        wq_prev.clear()
        wq_prev.update({"red": red_marks[nrt - 1], "chain": q_marks[n_k - 1], "mm": c_mm,
                        "sp2": {0: sp_marks[n_k - 2], 1: sp_marks[n_k - 1]}})
        wq_res[tag] = (sp_marks[n_k - 1], q_marks[n_k - 1])

    if PHASELIM < 1:
        with nc.Block() as blk:
            p.emit_block(blk)
        return es

    with nc.sbuf_tensor("xbuf", [128, 2 * 8 * 512], F32) as xbuf, \
         nc.sbuf_tensor("xsq", [128, 2 * 512], F32) as xsq, \
         nc.sbuf_tensor("xnf", [128, 4 * 512], F32) as xnf, \
         nc.sbuf_tensor("xnb", [128, 8 * 512], BF16) as xnb, \
         nc.sbuf_tensor("mrows", [1, 2 * 512], F32) as mrows, \
         nc.sbuf_tensor("rrows", [1, 2 * 512], F32) as rrows, \
         nc.sbuf_tensor("t1row", [1, 512], F32) as t1row, \
         nc.sbuf_tensor("t2row", [1, 512], F32) as t2row, \
         nc.sbuf_tensor("wnat", [128, 2 * I_FFN], F32) as wnat, \
         nc.sbuf_tensor("wtr", [128, 2 * I_FFN], F32) as wtr, \
         nc.sbuf_tensor("invb", [128, I_FFN], F32) as invb, \
         nc.sbuf_tensor("invrow", [1, I_FFN], F32) as invrow, \
         nc.sbuf_tensor("qthw", [128, I_FFN], F32) as qthw, \
         nc.sbuf_tensor("qr0w", [128, I_FFN], I32) as qr0w, \
         nc.sbuf_tensor("qgtw", [128, I_FFN], F32) as qgtw, \
         nc.sbuf_tensor("wqow", [128, 2 * I_FFN], BF16) as wqow:

        def _gen_all():
            yield from wq_gen(wd_nat_h, wdT_h, D, 22, rmd, wd_sc, "wqsp_d", "d",
                              wnat, wtr, invb, invrow, qthw, qr0w, qgtw, wqow)
            yield from wq_gen(wg_nat_h, wgT_h, I_FFN, 8, rmg, wg_sc, "wqsp_g", "g",
                              wnat, wtr, invb, invrow, qthw, qr0w, qgtw, wqow)
            yield from wq_gen(wu_nat_h, wuT_h, I_FFN, 8, rmu, wu_sc, "wqsp_u", "u",
                              wnat, wtr, invb, invrow, qthw, qr0w, qgtw, wqow)
        wq_state[0] = _gen_all()

        def xtile(b, pt):
            return xbuf[:, (b * 8 + pt) * 512:(b * 8 + pt) * 512 + 512]

        xf_w = xf_sc.rearrange("(pt p) (tt c) -> p pt tt c", p=128, tt=8)

        sqmm_marks = {}
        a_done = {}
        cast_g = {}
        spxf = {}
        prev_cmq = 0
        for tt in range(8):
            b = tt % 2
            sl2 = tt % 2
            if tt >= 2:
                WD("sync", a_done[tt - 2])
            c_ld = DQ("xA", lambda h, tt=tt, b=b: h.dma_start(
                xbuf.rearrange("p (z pt c) -> p z pt c", z=2, pt=8)[:, b],
                x_h.rearrange("(pt p) (tt c) -> p tt pt c", p=128, tt=8)[:, tt]))
            WD("scalar", c_ld)
            WT("scalar", "consts")
            W("scalar", "tensor", prev_cmq)
            use_ps(4); use_ps(5)
            WD("tensor", c_ld)
            WT("tensor", "consts")
            sq_marks = []
            for pt in range(8):
                if pt >= 2:
                    W("scalar", "tensor", sqmm_marks[pt - 2])
                sq_marks.append(S(lambda h, pt=pt, b=b: h.activation(
                    xsq[:, (pt % 2) * 512:(pt % 2 + 1) * 512], xtile(b, pt), AF.Square)))
                T(lambda h, pt=pt, b=b: h.matmul(
                    PS[4][0:1, :], onescol[:, 0:1], xtile(b, pt), start=(pt == 0), stop=(pt == 7)))
                W("tensor", "scalar", sq_marks[pt])
                c_mq = T(lambda h, pt=pt: h.matmul(
                    PS[5][0:1, :], onescol[:, 0:1], xsq[:, (pt % 2) * 512:(pt % 2 + 1) * 512],
                    start=(pt == 0), stop=(pt == 7)))
                sqmm_marks[pt] = c_mq
            pump(14.0)
            W("vector", "tensor", c_mq)
            c_m = V(lambda h, sl2=sl2: h.tensor_scalar(
                mrows[0:1, sl2 * 512:(sl2 + 1) * 512], PS[4][0:1, :], 1.0 / D, None, OP.mult))
            W("vector", "vector", c_m)
            c_mq2 = V(lambda h: h.tensor_scalar(t1row[0:1, :], PS[5][0:1, :], 1.0 / D, None, OP.mult))
            guard_ps(4, "vector", c_m); guard_ps(5, "vector", c_mq2)
            W("vector", "vector", c_mq2)
            c_m2 = V(lambda h, sl2=sl2: h.tensor_tensor(
                t2row[0:1, :], mrows[0:1, sl2 * 512:(sl2 + 1) * 512],
                mrows[0:1, sl2 * 512:(sl2 + 1) * 512], OP.mult))
            W("vector", "vector", c_m2)
            c_var = V(lambda h: h.tensor_tensor(t1row[0:1, :], t1row[0:1, :], t2row[0:1, :],
                                                OP.subtract))
            W("scalar", "vector", c_var)
            c_ln = S(lambda h: h.activation(t2row[0:1, :], t1row[0:1, :], AF.Ln, bias=LN_EPS))
            W("scalar", "scalar", c_ln)
            c_rs = S(lambda h, sl2=sl2: h.activation(
                rrows[0:1, sl2 * 512:(sl2 + 1) * 512], t2row[0:1, :], AF.Exp, scale=-0.5))
            use_ps(4); use_ps(5)
            W("tensor", "vector", c_m)
            W("tensor", "scalar", c_rs)
            T(lambda h, sl2=sl2: h.matmul(PS[4][:, :], ones1r[0:1, :],
                                          mrows[0:1, sl2 * 512:(sl2 + 1) * 512],
                                          start=True, stop=True))
            c_br = T(lambda h, sl2=sl2: h.matmul(PS[5][:, :], ones1r[0:1, :],
                                                 rrows[0:1, sl2 * 512:(sl2 + 1) * 512],
                                                 start=True, stop=True))
            pump(7.0)
            W("vector", "tensor", c_br)
            if tt >= 1:
                WD("scalar", a_done[tt - 1])
            for pt in range(8):
                ti = tt * 8 + pt
                s4 = ti % 4
                if ti >= 4:
                    W("vector", "scalar", cast_g[ti - 4])
                    WD("vector", spxf[ti - 4])
                c_s = V(lambda h, s4=s4, pt=pt, b=b: h.tensor_tensor(
                    xnf[:, s4 * 512:(s4 + 1) * 512], xtile(b, pt), PS[4][:, :],
                    OP.subtract))
                W("vector", "vector", c_s)
                c_x = V(lambda h, s4=s4: h.tensor_tensor(
                    xnf[:, s4 * 512:(s4 + 1) * 512],
                    xnf[:, s4 * 512:(s4 + 1) * 512], PS[5][:, :], OP.mult))
                W("vector", "vector", c_x)
                V(lambda h, s4=s4, ti=ti: h.reduce_max(
                    maxst[:, ti:ti + 1], xnf[:, s4 * 512:(s4 + 1) * 512],
                    axis=AX.X, apply_absolute_value=True))
                W("scalar", "vector", c_x)
                cast_g[ti] = S(lambda h, s4=s4, pt=pt: h.copy(
                    xnb[:, pt * 512:(pt + 1) * 512], xnf[:, s4 * 512:(s4 + 1) * 512]))
                W("gpsimd", "vector", c_x)
                spxf[ti] = DQ("spill_xf", lambda h, s4=s4, pt=pt, tt=tt: h.dma_start(
                    xf_w[:, pt, tt], xnf[:, s4 * 512:(s4 + 1) * 512]))
            guard_ps(4, "vector", c_x); guard_ps(5, "vector", c_x)
            prev_cmq = c_mq
            W("gpsimd", "scalar", cast_g[tt * 8 + 7])
            a_done[tt] = DQ("spill_xn", lambda h, tt=tt: h.dma_start(
                xn_sc.rearrange("(pt p) (tt c) -> p tt pt c", p=128, tt=8)[:, tt],
                xnb.rearrange("p (pt c) -> p pt c", pt=8)))

        W("vector", "vector", p.cnt["vector"])
        c_f = V(lambda h: h.reduce_max(gmcol[:, 0:1], maxst[:, 0:64], axis=AX.X))
        W("gpsimd", "vector", c_f)
        t_o1 = DQ("bounce", lambda h: h.dma_start(
            mr_sc[0:1, 0:128].rearrange("o (a b) -> (o b) a", a=1), gmcol[:, 0:1],
            allow_slow_non_contiguous=True))
        WD("gpsimd", t_o1)
        t_i1 = DQ("bounce", lambda h: h.dma_start(smallrow[0:1, :], mr_sc[0:1, 0:128]))
        pump(22.0)
        WD("vector", t_i1)
        c_r = V(lambda h: h.reduce_max(smallrow[0:1, 2:3], smallrow[0:1, 0:128], axis=AX.X))
        W("gpsimd", "vector", c_r)
        t_b1 = DQ("bounce", lambda h: h.dma_start(cc_t[0][0].ap()[0:1, 0:64],
                                                  smallrow[0:1, 0:64]))
        WD("gpsimd", t_b1)
        cc_state[0] += 1
        tgt1 = cc_state[0]
        p.raw("gpsimd", lambda h: h.collective_compute(
            "AllReduce", OP.max, replica_groups=[list(range(8))],
            ins=[cc_t[0][0].ap().opt()], outs=[cc_t[0][1].ap().opt()]).then_inc(sem_cc, 1))
        pump(1e9)
        p.raw("gpsimd", lambda h, t=tgt1: h.wait_ge(sem_cc, t))
        t_gi1 = DQ("bounce", lambda h: h.dma_start(smallrow[0:1, 2:3],
                                                   cc_t[0][1].ap()[0:1, 2:3]))
        WD("scalar", t_gi1)
        WD("vector", t_gi1)
        c_sg = S(lambda h: h.activation(smallrow[0:1, 3:4], clips[0:1, 0:1], AF.Sigmoid))
        W("vector", "scalar", c_sg)
        c1x = V(lambda h: h.tensor_tensor(smallrow[0:1, 4:5], smallrow[0:1, 2:3],
                                          smallrow[0:1, 3:4], OP.mult))
        W("vector", "vector", c1x)
        c2x = V(lambda h: h.tensor_scalar(smallrow[0:1, 4:5], smallrow[0:1, 4:5],
                                          1.0 / QMAX, None, OP.mult))
        W("vector", "vector", c2x)
        c3x = V(lambda h: h.reciprocal(smallrow[0:1, 5:6], smallrow[0:1, 4:5]))
        use_ps(6)
        W("tensor", "vector", c3x)
        c_mmx = T(lambda h: h.matmul(PS[6][:, 0:2], ones1r[0:1, :], smallrow[0:1, 4:6],
                                     start=True, stop=True))
        W("vector", "tensor", c_mmx)
        c_e = V(lambda h: h.tensor_copy(invsx_col[:, 0:1], PS[6][:, 1:2]))
        W("vector", "vector", c_e)
        cf1 = V(lambda h: h.tensor_scalar(scg[:, :], rmg[:, :], PS[6][:, 0:1], 1.0 / QMAX,
                                          OP.mult, OP.mult))
        W("vector", "vector", cf1)
        c_sx = V(lambda h: h.tensor_scalar(scu[:, :], rmu[:, :], PS[6][:, 0:1], 1.0 / QMAX,
                                           OP.mult, OP.mult))
        guard_ps(6, "vector", c_sx)
        t_wdq = wq_res["d"][0]
        p.barrier()

    if PHASELIM < 4:
        with nc.Block() as blk:
            p.emit_block(blk)
        return es
    with nc.sbuf_tensor("xq", [128, 8 * NT], BF16) as xq, \
         nc.sbuf_tensor("xnl", [128, 4 * 512], F32) as xnl, \
         nc.sbuf_tensor("slbuf", [128, 2 * 512], F32) as slbuf, \
         nc.sbuf_tensor("uhbuf", [128, 2 * 512], F32) as uhbuf, \
         nc.sbuf_tensor("wgq", [128, 8 * I_FFN], BF16) as wgq, \
         nc.sbuf_tensor("wuq", [128, 8 * I_FFN], BF16) as wuq:

        xf_r = xf_sc.rearrange("(pt p) (tt c) -> p pt tt c", p=128, tt=8)
        th_m = {}
        qg_m = {}
        bq_tt_done = {}

        def emit_bq(tt):
            for pt in range(8):
                ti = tt * 8 + pt
                sl = ti % 4
                qb = ti % 2
                if ti >= 4:
                    W("sync", "scalar", th_m[ti - 4])
                t_ld = DQ("xfl", lambda h, pt=pt, tt=tt, sl=sl: h.dma_start(
                    xnl[:, sl * 512:(sl + 1) * 512], xf_r[:, pt, tt]))
                WD("scalar", t_ld)
                if ti == 0:
                    W("scalar", "vector", c_sx)
                if ti >= 2:
                    W("scalar", "vector", qg_m[ti - 2])
                c_th = S(lambda h, sl=sl, qb=qb: h.activation(
                    qth2[:, qb * 512:(qb + 1) * 512], xnl[:, sl * 512:(sl + 1) * 512],
                    AF.Identity, bias=0.5, scale=invsx_col[:, 0:1]))
                th_m[ti] = c_th
                W("vector", "scalar", c_th)
                c_cl = V(lambda h, qb=qb: h.tensor_scalar(
                    qth2[:, qb * 512:(qb + 1) * 512], qth2[:, qb * 512:(qb + 1) * 512],
                    127.9, None, OP.min))
                W("vector", "vector", c_cl)
                c_i = V(lambda h, qb=qb: h.tensor_scalar(
                    qi32[:, qb * 512:(qb + 1) * 512], qth2[:, qb * 512:(qb + 1) * 512],
                    -127.9, None, OP.max))
                W("vector", "vector", c_i)
                c_g = V(lambda h, qb=qb: h.tensor_tensor(
                    qgt2[:, qb * 512:(qb + 1) * 512], qi32[:, qb * 512:(qb + 1) * 512],
                    qth2[:, qb * 512:(qb + 1) * 512], OP.is_gt))
                qg_m[ti] = c_g
                W("vector", "vector", c_g)
                c_q = V(lambda h, pt=pt, tt=tt, qb=qb: h.scalar_tensor_tensor(
                    xq[:, pt * NT + tt * 512:pt * NT + (tt + 1) * 512],
                    qgt2[:, qb * 512:(qb + 1) * 512], -1.0,
                    qi32[:, qb * 512:(qb + 1) * 512], OP.mult, OP.add))
                if pt == 7:
                    bq_tt_done[tt] = c_q

        uh_mx = {}
        b_sp = {}
        WT("sync", "wqsp_g")
        WT("sync", "wqsp_u")
        t_wgl = DQ("wcolg", lambda h: h.dma_start(wgq[:, :], wg_sc))
        t_wul = DQ("wcolu", lambda h: h.dma_start(wuq[:, :], wu_sc))
        WD("tensor", t_wgl)
        WD("tensor", t_wul)

        def emit_b(tt):
            W("tensor", "vector", bq_tt_done[tt])
            for m in range(22):
                idx = tt * 22 + m
                pg_ = idx % 2
                pu_ = 2 + idx % 2
                ub = idx % 2
                use_ps(pg_); use_ps(pu_)
                for k in range(8):
                    T(lambda h, k=k, tt=tt, m=m, pg_=pg_: h.matmul(
                        PS[pg_][:, :], wgq[:, k * I_FFN + m * 128:k * I_FFN + (m + 1) * 128],
                        xq[:, k * NT + tt * 512:k * NT + (tt + 1) * 512],
                        start=(k == 0), stop=(k == 7)))
                for k in range(8):
                    T(lambda h, k=k, tt=tt, m=m, pu_=pu_: h.matmul(
                        PS[pu_][:, :], wuq[:, k * I_FFN + m * 128:k * I_FFN + (m + 1) * 128],
                        xq[:, k * NT + tt * 512:k * NT + (tt + 1) * 512],
                        start=(k == 0), stop=(k == 7)))
                W("scalar", "tensor", p.cnt["tensor"])
                W("scalar", "vector", c_sx)
                if idx >= 2:
                    W("scalar", "vector", uh_mx[idx - 2])
                c_sl = S(lambda h, m=m, pg_=pg_, ub=ub: h.activation(
                    slbuf[:, ub * 512:(ub + 1) * 512], PS[pg_][:, :], AF.Silu,
                    scale=scg[:, m:m + 1]))
                guard_ps(pg_, "scalar", c_sl)
                W("vector", "scalar", c_sl)
                if idx >= 2:
                    WD("vector", b_sp[idx - 2])
                c_uh = V(lambda h, m=m, pu_=pu_, ub=ub: h.scalar_tensor_tensor(
                    uhbuf[:, ub * 512:(ub + 1) * 512], PS[pu_][:, :], scu[:, m:m + 1],
                    slbuf[:, ub * 512:(ub + 1) * 512], OP.mult, OP.mult))
                guard_ps(pu_, "vector", c_uh)
                W("vector", "vector", c_uh)
                uh_mx[idx] = V(lambda h, idx=idx, ub=ub: h.reduce_max(
                    maxst2[:, idx:idx + 1], uhbuf[:, ub * 512:(ub + 1) * 512],
                    axis=AX.X, apply_absolute_value=True))
                W("gpsimd", "vector", uh_mx[idx])
                b_sp[idx] = DQ("spill_uh", lambda h, m=m, tt=tt, ub=ub: h.dma_start(
                    uh_sc[m * 128:(m + 1) * 128, tt * 512:(tt + 1) * 512],
                    uhbuf[:, ub * 512:(ub + 1) * 512]))

        emit_bq(0)
        emit_bq(1)
        for tt in range(8):
            emit_b(tt)
            if tt + 2 < 8:
                emit_bq(tt + 2)

        if PHASELIM < 6:
            p.barrier()
            with nc.Block() as blk:
                p.emit_block(blk)
            return es
        W("vector", "vector", uh_mx[22 * 8 - 1])
        c_sd = global_scale(maxst2, 176, 1, 1, 6, 7, 8, 9, invsd_col, [(scd, rmd)])
        p.barrier()

    if PHASELIM < 7:
        with nc.Block() as blk:
            p.emit_block(blk)
        return es

    if PHASELIM < 7:
        with nc.Block() as blk:
            p.emit_block(blk)
        return es
    if True:
        aes = ExitStack()
        def asb(name, shape, dt):
            return aes.enter_context(nc.sbuf_tensor(name, shape, dt))
        wqT = asb("wqT", [128, 8 * 256], BF16)
        wkT = asb("wkT", [128, 8 * 256], BF16)
        wvT = asb("wvT", [128, 8 * 1024], BF16)
        woT = asb("woT", [128, 8 * 1024], BF16)
        S2f = asb("S2f", [128, NH * 128], F32)
        S2b = asb("S2b", [128, NH * 128], BF16)
        S1f = asb("S1f", [128, 256], F32)
        S1b = asb("S1b", [128, 256], BF16)
        S0c = asb("S0c", [128, 8], F32)
        xnch = asb("xnch", [128, 2 * 8 * CH], BF16)
        qf = [[asb(f"qf{z}_{i}", [128, CH], BF16) for i in range(2)] for z in range(2)]
        kev = [[asb(f"kev{z}_{i}", [128, CH], BF16) for i in range(2)] for z in range(2)]
        kod = [[asb(f"kod{z}_{i}", [128, CH], BF16) for i in range(2)] for z in range(2)]
        kfm = [asb(f"kfm_{i}", [128, CH], BF16) for i in range(2)]
        ktm = [[asb(f"ktm{z}_{i}", [128, 256], BF16) for i in range(2)] for z in range(2)]
        vtm = [[asb(f"vtm{z}_{i}", [128, 1024], BF16) for i in range(2)] for z in range(2)]
        kf2 = [[asb(f"kf2{z}_{i}", [128, 4096], BF16) for i in range(2)] for z in range(2)]
        qf2 = [asb(f"qf2_{z}", [128, NH * 512], BF16) for z in range(2)]
        qAe = asb("qAe", [128, 512], BF16)
        Am = [asb(f"Am_{z}", [128, NH * 384], BF16) for z in range(2)]
        ysq = asb("ysq", [128, 256], BF16)
        rstd = asb("rstd", [32, 256], F32)
        rstdb = asb("rstdb", [32, 256], BF16)
        rstdg = asb("rstdg", [128, 256], BF16)
        yrms = asb("yrms", [128, 8 * 256], BF16)
        sstg = asb("sstg", [128, 8 * 256], BF16)

        for dst, src_ in [(wqT, wqT_h), (wkT, wkT_h), (wvT, wvT_h), (woT, woT_h)]:
            DQ("attnw", lambda h, d=dst, s=src_: h.dma_start(
                d.rearrange("p (k c) -> p k c", k=8),
                s.rearrange("(k p) c -> p k c", p=128)))

        state_ready = {"vector": 0, "scalar": 0}
        kf2_guard = [0]
        s0_prev = [0]
        xn_lastuse = {}
        post_done = {0: 0, 1: 0}
        s_spill = {}
        pre_out = {}

        def emit_pre(ci):
            z = ci % 2
            blkb = ci % 2
            xoff = blkb * 8 * CH
            if ci >= 2:
                W("sync", "tensor", xn_lastuse[ci - 2])
            c_ld = DQ("xnch", lambda h, ci=ci, b=blkb: h.dma_start(
                xnch.rearrange("p (z pt c) -> p z pt c", z=2, pt=8)[:, b],
                xn_sc.rearrange("(pt p) (ci c) -> p ci pt c", p=128, ci=NCHUNK)[:, ci]))

            def xn_ap(pt, lo=0, hi=CH, xoff=xoff):
                base = xoff + pt * CH
                return xnch[:, base + lo:base + hi]

            pd_guard = post_done[z]
            use_ps(0); use_ps(1)
            WD("tensor", c_ld)
            WT("tensor", "attnw")
            W("tensor", "tensor", pd_guard)
            for mm_ in range(2):
                for k in range(8):
                    c_qmm = T(lambda h, k=k, mm_=mm_, f=xn_ap: h.matmul(
                        PS[0][:, mm_ * 256:(mm_ + 1) * 256],
                        wqT[:, k * 256 + mm_ * 128:k * 256 + (mm_ + 1) * 128],
                        f(k), start=(k == 0), stop=(k == 7)))
            for mm_ in range(2):
                for k in range(8):
                    c_kmm = T(lambda h, k=k, mm_=mm_, f=xn_ap: h.matmul(
                        PS[1][:, mm_ * 256:(mm_ + 1) * 256],
                        wkT[:, k * 256 + mm_ * 128:k * 256 + (mm_ + 1) * 128],
                        f(k), start=(k == 0), stop=(k == 7)))
            W("scalar", "tensor", c_qmm)
            c_qe = S(lambda h, z=z: h.copy(qf[z][0][:, :], PS[0][:, 0:256]))
            c_qe = S(lambda h, z=z: h.copy(qf[z][1][:, :], PS[0][:, 256:512]))
            guard_ps(0, "scalar", c_qe)
            W("vector", "tensor", c_kmm)
            for mm_ in range(2):
                V(lambda h, mm_=mm_, z=z: h.tensor_scalar(
                    kev[z][mm_][:, :], PS[1][:, mm_ * 256:(mm_ + 1) * 256], emaske[:, 0:1],
                    None, OP.mult))
                c_kp = V(lambda h, mm_=mm_, z=z: h.tensor_scalar(
                    kod[z][mm_][:, :], PS[1][:, mm_ * 256:(mm_ + 1) * 256], emasko[:, 0:1],
                    None, OP.mult))
            W("scalar", "tensor", c_kmm)
            W("scalar", "vector", c_kp)
            c_kf = S(lambda h: h.copy(kfm[0][:, :], PS[1][:, 0:256]))
            c_kf = S(lambda h: h.copy(kfm[1][:, :], PS[1][:, 256:512]))
            guard_ps(1, "scalar", c_kf)
            guard_ps(1, "vector", c_kp)

            ps_t = PS[0].bitcast(BF16)
            c_ktm = None
            for half in range(2):
                for mm_ in range(2):
                    use_ps(0)
                    W("tensor", "scalar", c_kf)
                    c_tr = T(lambda h, half=half, mm_=mm_: h.transpose(
                        ps_t[:, 0:128], kfm[mm_][:, half * 128:(half + 1) * 128], ident[:, :]))
                    W("vector", "tensor", c_tr)
                    c_ktm = V(lambda h, half=half, mm_=mm_, z=z: h.tensor_copy(
                        ktm[z][half][:, mm_ * 128:(mm_ + 1) * 128], ps_t[:, 0:128]))
                    guard_ps(0, "vector", c_ktm)

            c_vtm = None
            for half in range(2):
                use_ps(2); use_ps(3)
                for k in range(8):
                    T(lambda h, k=k, half=half, f=xn_ap: h.matmul(
                        PS[2][:, :], f(k, half * 128, half * 128 + 128),
                        wvT[:, k * 1024:k * 1024 + 512], start=(k == 0), stop=(k == 7)))
                for k in range(8):
                    c_v2 = T(lambda h, k=k, half=half, f=xn_ap: h.matmul(
                        PS[3][:, :], f(k, half * 128, half * 128 + 128),
                        wvT[:, k * 1024 + 512:(k + 1) * 1024], start=(k == 0), stop=(k == 7)))
                W("scalar", "tensor", c_v2)
                c_ve = S(lambda h, half=half, z=z: h.copy(vtm[z][half][:, 0:512], PS[2][:, :]))
                guard_ps(2, "scalar", c_ve)
                c_vtm = S(lambda h, half=half, z=z: h.copy(vtm[z][half][:, 512:1024], PS[3][:, :]))
                guard_ps(3, "scalar", c_vtm)

            c_kf2 = None
            for half in range(2):
                W("gpsimd", "vector", c_ktm)
                W("gpsimd", "tensor", pd_guard)
                a_ap = ktm[z][half].rearrange("p (hh i j) -> p hh i j", hh=NH, j=1).broadcast_to(
                    [128, NH, FD, FD])
                b_ap = ktm[z][half].rearrange("p (hh i j) -> p hh i j", hh=NH, i=1).broadcast_to(
                    [128, NH, FD, FD])
                c_kf2 = p.do("gpsimd", lambda h, half=half, a=a_ap, bb=b_ap, z=z: h.tensor_tensor(
                    kf2[z][half].rearrange("p (hh i j) -> p hh i j", hh=NH, i=FD), a, bb,
                    OP.mult), 1)

            c_qf2 = None
            for hh in range(NH):
                par = hh % 2
                pair = hh // 2
                tile = pair // 4
                rg = 32 * (pair % 4)
                pa, pb_ = (0, 1) if hh % 2 == 0 else (2, 3)
                use_ps(pa); use_ps(pb_)
                W("tensor", "scalar", c_qe)
                for Tt in range(2):
                    c_b = None
                    T(lambda h, par=par, Tt=Tt, tile=tile, rg=rg, pa=pa, z=z: h.matmul(
                        PS[pa][:, Tt * 256:(Tt + 1) * 256],
                        EA[par][rg:rg + 32, Tt * 128:(Tt + 1) * 128],
                        qf[z][tile][rg:rg + 32, :], start=True, stop=True, tile_position=(rg, 0)))
                    c_b = T(lambda h, par=par, Tt=Tt, tile=tile, rg=rg, pb_=pb_, z=z: h.matmul(
                        PS[pb_][:, Tt * 256:(Tt + 1) * 256],
                        EB[par][rg:rg + 32, Tt * 128:(Tt + 1) * 128],
                        qf[z][tile][rg:rg + 32, :], start=True, stop=True, tile_position=(rg, 0)))
                W("scalar", "tensor", c_b)
                if hh >= 1:
                    W("scalar", "vector", c_qf2)
                c_ae = S(lambda h, pa=pa: h.copy(qAe[:, :], PS[pa][:, :]))
                guard_ps(pa, "scalar", c_ae)
                W("vector", "scalar", c_ae)
                c_qf2 = V(lambda h, hh=hh, pb_=pb_, z=z: h.tensor_tensor(
                    qf2[z][:, hh * 512:(hh + 1) * 512], qAe[:, :], PS[pb_][:, :], OP.mult))
                guard_ps(pb_, "vector", c_qf2)

            c_am = None
            for hh in range(NH):
                par = hh % 2
                pair = hh // 2
                tile = pair // 4
                rg = 32 * (pair % 4)
                kpar = kev[z] if par == 0 else kod[z]
                pt_ = 0 if hh % 2 == 0 else 1
                use_ps(pt_)
                W("tensor", "vector", c_kp)
                T(lambda h, tile=tile, rg=rg, pt_=pt_, kpar=kpar, z=z: h.matmul(
                    PS[pt_][:, 0:256], kpar[tile][rg:rg + 32, 0:128],
                    qf[z][tile][rg:rg + 32, :], start=True, stop=True, tile_position=(rg, 0)))
                c_t2 = T(lambda h, tile=tile, rg=rg, pt_=pt_, kpar=kpar, z=z: h.matmul(
                    PS[pt_][:, 256:384], kpar[tile][rg:rg + 32, 128:256],
                    qf[z][tile][rg:rg + 32, 128:256], start=True, stop=True, tile_position=(rg, 0)))
                W("scalar", "tensor", c_t2)
                W("scalar", "tensor", pd_guard)
                c_pl = S(lambda h, hh=hh, pt_=pt_, z=z: h.activation(
                    Am[z][:, hh * 384:(hh + 1) * 384], PS[pt_][:, 0:384], AF.Square,
                    bias=4.0 / S32, scale=1.0 / S32))
                guard_ps(pt_, "scalar", c_pl)
                W("vector", "scalar", c_pl)
                c_m1 = V(lambda h, hh=hh, z=z: h.scalar_tensor_tensor(
                    Am[z][:, hh * 384:hh * 384 + 128], Am[z][:, hh * 384:hh * 384 + 128],
                    0.5, triu[:, :], OP.add, OP.mult))
                W("vector", "vector", c_m1)
                c_m2 = V(lambda h, hh=hh, z=z: h.tensor_scalar(
                    Am[z][:, hh * 384 + 128:hh * 384 + 256],
                    Am[z][:, hh * 384 + 128:hh * 384 + 256], 0.5, None, OP.add))
                W("vector", "vector", c_m2)
                c_am = V(lambda h, hh=hh, z=z: h.scalar_tensor_tensor(
                    Am[z][:, hh * 384 + 256:hh * 384 + 384],
                    Am[z][:, hh * 384 + 256:hh * 384 + 384],
                    0.5, triu[:, :], OP.add, OP.mult))
            xn_lastuse[ci] = p.cnt["tensor"]
            pre_out[ci] = {"am": c_am, "qf2": c_qf2, "vtm": c_vtm, "kf2": c_kf2,
                           "ktm": c_ktm, "kp": c_kp}

        def emit_post(ci):
            z = ci % 2
            pre = pre_out[ci]
            if ci % (SEQ // CH) == 0:
                if ci > 0:
                    W("vector", "tensor", p.cnt["tensor"])
                    W("vector", "scalar", state_ready["scalar"])
                    W("vector", "vector", max(state_ready["vector"], s0_prev[0]))
                for zz in (S2f, S1f, S0c, S2b, S1b):
                    c_z = V(lambda h, zz=zz: h.memset(zz[:, :], 0.0))
                state_ready["vector"] = c_z

            W("tensor", "vector", state_ready["vector"])
            W("tensor", "scalar", state_ready["scalar"])
            W("tensor", "vector", pre["am"])
            W("tensor", "vector", pre["qf2"])
            W("tensor", "scalar", pre["vtm"])
            stats_first = True
            c_st = None
            ysb_guard = 0
            for pr in range(8):
                py = 4 if pr % 2 == 0 else 5
                use_ps(py)
                tile = pr // 4
                rg = 32 * (pr % 4)
                c_yl = None
                for hi in range(2):
                    hh = 2 * pr + hi
                    eb = hi * 64
                    sbase = 128 * (pr // 4) + 64 * hi
                    T(lambda h, tile=tile, rg=rg, eb=eb, sbase=sbase, py=py, z=z: h.matmul(
                        PS[py][eb:eb + 64, 0:256], S1b[rg:rg + 32, sbase:sbase + 64],
                        qf[z][tile][rg:rg + 32, :], start=True, stop=False,
                        tile_position=(rg, eb)))
                    T(lambda h, hh=hh, eb=eb, py=py, z=z: h.matmul(
                        PS[py][eb:eb + 64, 0:256], vtm[z][0][:, hh * 64:(hh + 1) * 64],
                        Am[z][:, hh * 384:hh * 384 + 256], start=False, stop=False))
                    T(lambda h, hh=hh, eb=eb, py=py, z=z: h.matmul(
                        PS[py][eb:eb + 64, 128:256], vtm[z][1][:, hh * 64:(hh + 1) * 64],
                        Am[z][:, hh * 384 + 256:hh * 384 + 384], start=False, stop=False))
                    for Tt in range(2):
                        c_yl = T(lambda h, hh=hh, eb=eb, Tt=Tt, py=py, z=z,
                                 lst=(Tt == 1): h.matmul(
                            PS[py][eb:eb + 64, 0:256],
                            S2b[:, (2 * hh + Tt) * 64:(2 * hh + Tt + 1) * 64],
                            qf2[z][:, (2 * hh + Tt) * 256:(2 * hh + Tt + 1) * 256],
                            start=False, stop=lst))
                W("scalar", "tensor", c_yl)
                W("scalar", "vector", state_ready["vector"])
                if not stats_first:
                    W("scalar", "tensor", c_st)
                c_ysq = S(lambda h, pr=pr, py=py: h.activation(
                    ysq[:, :], PS[py][:, 0:256], AF.Square, bias=S0c[:, pr:pr + 1]))
                W("scalar", "scalar", c_ysq)
                c_ysb = S(lambda h, pr=pr, py=py: h.activation(
                    yrms[:, pr * 256:(pr + 1) * 256], PS[py][:, 0:256], AF.Identity,
                    bias=S0c[:, pr:pr + 1]))
                guard_ps(py, "scalar", c_ysb)
                ysb_guard = c_ysb
                if stats_first:
                    use_ps(6)
                W("tensor", "scalar", c_ysq)
                c_st = T(lambda h, pr=pr, s=stats_first: h.matmul(
                    PS[6][0:32, 0:256], bo[:, pr * 32:(pr + 1) * 32], ysq[:, :],
                    start=s, stop=(pr == 7)))
                stats_first = False
            W("vector", "tensor", c_st)
            c_z2 = V(lambda h: h.tensor_scalar(rstd[:, :], PS[6][0:32, 0:256], 1.0 / HD,
                                               RMS_EPS, OP.mult, OP.add))
            guard_ps(6, "vector", c_z2)
            W("scalar", "vector", c_z2)
            c_l = S(lambda h: h.activation(rstd[:, :], rstd[:, :], AF.Ln))
            W("scalar", "scalar", c_l)
            c_e = S(lambda h: h.activation(rstdb[:, :], rstd[:, :], AF.Exp, scale=-0.5))
            c_yrm = None
            for pr in range(8):
                use_ps(7)
                W("tensor", "scalar", c_e)
                W("tensor", "scalar", ysb_guard)
                c_bc = T(lambda h, pr=pr: h.matmul(
                    PS[7][:, 0:256], sel[:, pr * 128:(pr + 1) * 128], rstdb[:, :],
                    start=True, stop=True))
                W("vector", "tensor", c_bc)
                if c_yrm is not None:
                    W("vector", "vector", c_yrm)
                c_rg = V(lambda h: h.tensor_scalar(rstdg[:, :], PS[7][:, 0:256],
                                                   gwcol[:, 0:1], None, OP.mult))
                guard_ps(7, "vector", c_rg)
                W("vector", "vector", c_rg)
                W("vector", "scalar", ysb_guard)
                c_yrm = V(lambda h, pr=pr: h.tensor_tensor(
                    yrms[:, pr * 256:(pr + 1) * 256], yrms[:, pr * 256:(pr + 1) * 256],
                    rstdg[:, :], OP.mult))

            c_se = None
            for m in range(8):
                po = 6 if m % 2 == 0 else 7
                use_ps(po)
                W("tensor", "vector", c_yrm)
                for k in range(8):
                    c_o = T(lambda h, m=m, k=k, po=po: h.matmul(
                        PS[po][:, 0:256], woT[:, k * 1024 + m * 128:k * 1024 + (m + 1) * 128],
                        yrms[:, k * 256:(k + 1) * 256], start=(k == 0), stop=(k == 7)))
                W("scalar", "tensor", c_o)
                if ci >= 1:
                    WD("scalar", s_spill[ci - 1])
                c_se = S(lambda h, m=m, po=po: h.copy(
                    sstg[:, m * 256:(m + 1) * 256], PS[po][:, 0:256]))
                guard_ps(po, "scalar", c_se)
            W("gpsimd", "scalar", c_se)
            s_spill[ci] = DQ("spill_s", lambda h, ci=ci: h.dma_start(
                s_sc.rearrange("(pt p) (ci c) -> p ci pt c", p=128, ci=NCHUNK)[:, ci],
                sstg.rearrange("p (pt c) -> p pt c", pt=8)))

            if ci % (SEQ // CH) == (SEQ // CH) - 1:
                post_done[z] = p.cnt["tensor"]
                return
            c_s2cp = None
            for grp in range(4):
                pg_ = 4 if grp % 2 == 0 else 5
                use_ps(pg_)
                W("tensor", "gpsimd", pre["kf2"])
                for gi in range(4):
                    hh = grp * 4 + gi
                    for Tt in range(2):
                        for half in range(2):
                            c_u = T(lambda h, hh=hh, Tt=Tt, half=half, gi=gi, pg_=pg_,
                                    z=z: h.matmul(
                                PS[pg_][:, (gi * 2 + Tt) * 64:(gi * 2 + Tt + 1) * 64],
                                kf2[z][half][:, hh * 256 + Tt * 128:hh * 256 + (Tt + 1) * 128],
                                vtm[z][half][:, hh * 64:(hh + 1) * 64],
                                start=(half == 0), stop=(half == 1)))
                W("vector", "tensor", c_u)
                c_ad = V(lambda h, grp=grp, pg_=pg_: h.tensor_tensor(
                    S2f[:, grp * 512:(grp + 1) * 512], S2f[:, grp * 512:(grp + 1) * 512],
                    PS[pg_][:, :], OP.add))
                guard_ps(pg_, "vector", c_ad)
                W("scalar", "vector", c_ad)
                c_s2cp = S(lambda h, grp=grp: h.copy(S2b[:, grp * 512:(grp + 1) * 512],
                                                     S2f[:, grp * 512:(grp + 1) * 512]))
            use_ps(6)
            W("tensor", "vector", pre["ktm"])
            for pr in range(8):
                tile = pr // 4
                rg = 32 * (pr % 4)
                for half in range(2):
                    c_s1 = T(lambda h, pr=pr, half=half, tile=tile, rg=rg, z=z: h.matmul(
                        PS[6][rg:rg + 32, tile * 128:(tile + 1) * 128],
                        ktm[z][half][:, pr * 32:(pr + 1) * 32],
                        vtm[z][half][:, pr * 128:(pr + 1) * 128],
                        start=(half == 0), stop=(half == 1), tile_position=(0, rg)))
            W("vector", "tensor", c_s1)
            c_s1a = V(lambda h: h.tensor_tensor(S1f[:, :], S1f[:, :], PS[6][:, 0:256], OP.add))
            guard_ps(6, "vector", c_s1a)
            W("vector", "vector", c_s1a)
            c_s1b = V(lambda h: h.tensor_tensor(S1b[:, :], S1f[:, :], s1mask[:, :], OP.mult))
            use_ps(7)
            for pr in range(8):
                for half in range(2):
                    c_s0 = T(lambda h, pr=pr, half=half, z=z: h.matmul(
                        PS[7][:, (pr * 2 + half):(pr * 2 + half + 1)],
                        vtm[z][half][:, pr * 128:(pr + 1) * 128], onescolb[:, 0:1],
                        start=True, stop=True))
            W("vector", "tensor", c_s0)
            W("vector", "scalar", ysb_guard)
            W("vector", "vector", s0_prev[0])
            c_s0a = V(lambda h: h.tensor_tensor(
                S0c[:, :], S0c[:, :],
                PS[7][:, 0:16].rearrange("p (a b) -> p a b", a=8)[:, :, 0:1].rearrange(
                    "p a b -> p (a b)"), OP.add))
            W("vector", "vector", c_s0a)
            c_s0b = V(lambda h: h.tensor_tensor(
                S0c[:, :], S0c[:, :],
                PS[7][:, 0:16].rearrange("p (a b) -> p a b", a=8)[:, :, 1:2].rearrange(
                    "p a b -> p (a b)"), OP.add))
            guard_ps(7, "vector", c_s0b)
            s0_prev[0] = c_s0b
            state_ready["vector"] = c_s1b if c_s0b < c_s1b else c_s0b
            state_ready["scalar"] = c_s2cp
            post_done[z] = p.cnt["tensor"]

        emit_pre(0)
        for ci in range(NCHUNK):
            if ci + 1 < NCHUNK:
                emit_pre(ci + 1)
            emit_post(ci)
        c_slast = s_spill[NCHUNK - 1]
        p.barrier()
        aes.close()

    if True:
        ces = ExitStack()
        def csb(name, shape, dt):
            return ces.enter_context(nc.sbuf_tensor(name, shape, dt))
        wdTq = csb("wdTq", [128, 22 * 1024], BF16)
        rh = csb("rh", [128, 2 * 22 * 512], BF16)
        uhl = csb("uhl", [128, 4 * 512], F32)
        gbuf = csb("gbuf", [128, 8 * 512], BF16)
        sgbuf = csb("sgbuf", [128, 512], F32)
        xb2 = csb("xb2", [128, 2 * 8 * 512], F32)
        sld = csb("sld", [128, 8 * 512], BF16)
        outb = csb("outb", [128, 2 * 8 * 512], F32)

        WD("sync", t_wdq)
        t_wdl = DQ("wdl", lambda h: h.dma_start(wdTq[:, :], wd_sc))
        WT("sync", "spill_uh")
        WT("sync", "spill_s")
        qg_v = {}
        qg_g = {}
        dn_guard = {0: 0, 1: 0}
        prev_cb = [0]
        comb_marks = {}
        dn_last = {}
        out_marks = {}
        th_marks = {}
        rq_last = {}

        def emit_quant(tt):
            rb_ = tt % 2
            rq = {}
            for k in range(22):
                ub = (tt * 22 + k) % 4
                qb = (tt * 22 + k) % 2
                gi = tt * 22 + k
                if gi >= 4:
                    W("sync", "scalar", th_marks[gi - 4])
                t_lu = DQ("uhl", lambda h, k=k, tt=tt, b=ub: h.dma_start(
                    uhl[:, b * 512:(b + 1) * 512],
                    uh_sc[k * 128:(k + 1) * 128, tt * 512:(tt + 1) * 512]))
                WD("scalar", t_lu)
                W("scalar", "vector", c_sd)
                if gi >= 2:
                    W("scalar", "vector", rq_last.get(gi - 2, 0))
                c_th = S(lambda h, b=ub, qb=qb: h.activation(
                    qth2[:, qb * 512:(qb + 1) * 512], uhl[:, b * 512:(b + 1) * 512],
                    AF.Identity, bias=0.5, scale=invsd_col[:, 0:1]))
                th_marks[gi] = c_th
                W("vector", "scalar", c_th)
                if gi >= 2:
                    W("vector", "vector", rq_last.get(gi - 2, 0))
                if k == 0 and tt >= 2:
                    W("vector", "tensor", dn_guard[rb_])
                c_cl = V(lambda h, qb=qb: h.tensor_scalar(
                    qth2[:, qb * 512:(qb + 1) * 512], qth2[:, qb * 512:(qb + 1) * 512],
                    127.9, -127.9, OP.min, OP.max))
                W("scalar", "vector", c_cl)
                c_i = S(lambda h, qb=qb: h.copy(
                    qi32[:, qb * 512:(qb + 1) * 512], qth2[:, qb * 512:(qb + 1) * 512]))
                W("vector", "scalar", c_i)
                c_g = V(lambda h, qb=qb: h.tensor_tensor(
                    qgt2[:, qb * 512:(qb + 1) * 512], qi32[:, qb * 512:(qb + 1) * 512],
                    qth2[:, qb * 512:(qb + 1) * 512], OP.is_gt))
                qg_v[gi] = c_g
                W("vector", "vector", c_g)
                rq[k] = V(lambda h, k=k, qb=qb, rb_=rb_: h.scalar_tensor_tensor(
                    rh[:, (rb_ * 22 + k) * 512:(rb_ * 22 + k + 1) * 512],
                    qgt2[:, qb * 512:(qb + 1) * 512], -1.0,
                    qi32[:, qb * 512:(qb + 1) * 512], OP.mult, OP.add))
                rq_last[gi] = rq[k]
            return rq

        def emit_down(tt, rq_of):
            ob = tt % 2
            rb_ = tt % 2
            if tt >= 2:
                W("sync", "vector", comb_marks[tt - 2])
            if tt >= 2:
                WD("vector", out_marks[tt - 2])
            t_lx = DQ("xA", lambda h, tt=tt, b=ob: h.dma_start(
                xb2.rearrange("p (z pt c) -> p z pt c", z=2, pt=8)[:, b],
                x_h.rearrange("(pt p) (tt c) -> p tt pt c", p=128, tt=8)[:, tt]))
            if tt >= 1:
                W("sync", "vector", comb_marks[tt - 1])
            t_ls = DQ("cls", lambda h, tt=tt: h.dma_start(
                sld.rearrange("p (pt c) -> p pt c", pt=8),
                s_sc.rearrange("(pt p) (tt c) -> p tt pt c", p=128, tt=8)[:, tt]))
            WD("tensor", t_wdl)
            for hm in range(8):
                use_ps(hm)
            for k in range(22):
                W("tensor", "vector", rq_of[k])
                for hm in range(8):
                    c_d = T(lambda h, hm=hm, k=k, rb_=rb_: h.matmul(
                        PS[hm][:, :], wdTq[:, k * 1024 + hm * 128:k * 1024 + (hm + 1) * 128],
                        rh[:, (rb_ * 22 + k) * 512:(rb_ * 22 + k + 1) * 512],
                        start=(k == 0), stop=(k == 21)))
            for hm in range(8):
                W("scalar", "tensor", c_d)
                if tt >= 1:
                    W("scalar", "vector", comb_marks[tt - 1])
                c_gv = S(lambda h, hm=hm: h.activation(
                    gbuf[:, hm * 512:(hm + 1) * 512], PS[hm][:, :], AF.Silu,
                    scale=scd[:, hm:hm + 1]))
                guard_ps(hm, "scalar", c_gv)
                W("vector", "scalar", c_gv)
                WD("vector", t_lx)
                WD("vector", t_ls)
                W("vector", "vector", prev_cb[0])
                c_sg = V(lambda h, hm=hm: h.tensor_tensor(
                    sgbuf[:, :], sld[:, hm * 512:(hm + 1) * 512],
                    gbuf[:, hm * 512:(hm + 1) * 512], OP.mult))
                W("vector", "vector", c_sg)
                c_cb = V(lambda h, hm=hm, b=ob: h.tensor_tensor(
                    outb[:, (b * 8 + hm) * 512:(b * 8 + hm) * 512 + 512],
                    xb2[:, (b * 8 + hm) * 512:(b * 8 + hm) * 512 + 512], sgbuf[:, :], OP.add))
                prev_cb[0] = c_cb
            dn_last[tt] = c_d
            dn_guard[rb_] = c_d
            comb_marks[tt] = c_cb
            W("sync", "vector", c_cb)
            out_marks[tt] = DQ("out", lambda h, tt=tt, b=ob: h.dma_start(
                o_out.rearrange("(pt p) (tt c) -> p tt pt c", p=128, tt=8)[:, tt],
                outb.rearrange("p (z pt c) -> p z pt c", z=2, pt=8)[:, b]))

        rq_pend = emit_quant(0)
        for tt in range(8):
            nxt = emit_quant(tt + 1) if tt + 1 < 8 else None
            emit_down(tt, rq_pend)
            rq_pend = nxt
        WT("sync", "out")

        with nc.Block() as blk:
            p.emit_block(blk)
        ces.close()
    return es


def _stage_constants():
    bf = ml_dtypes.bfloat16
    c = {}
    c["triu_h"] = np.triu(np.ones((128, 128), np.float32)).astype(bf)
    c["corrA_h"] = np.concatenate([0.5 * np.triu(np.ones((128, 128), np.float32)),
                                   0.5 * np.ones((128, 128), np.float32)], axis=1).astype(bf)
    c["corrB_h"] = (0.5 * np.triu(np.ones((128, 128), np.float32))).astype(bf)
    for par in range(2):
        ea = np.zeros((32, 256), np.float32)
        eb = np.zeros((32, 256), np.float32)
        for Tt in range(2):
            for il in range(8):
                for j in range(16):
                    col = Tt * 128 + il * 16 + j
                    i = il + 8 * Tt
                    ea[16 * par + i, col] = 0.25
                    eb[16 * par + j, col] = 0.125
        c[f"EA{par}_h"] = np.tile(ea, (4, 1)).astype(bf)
        c[f"EB{par}_h"] = np.tile(eb, (4, 1)).astype(bf)
    m = np.zeros((128, 256), np.float32)
    for a in range(4):
        for f in range(32):
            for cc_ in range(2):
                if (f < 16) == (cc_ == 0):
                    for b in range(2):
                        m[32 * a + f, 128 * b + 64 * cc_:128 * b + 64 * cc_ + 64] = 0.25
    c["s1mask_h"] = m
    bov = np.zeros((128, 256), np.float32)
    for pr in range(8):
        bov[0:64, pr * 32 + 2 * pr] = 1.0
        bov[64:128, pr * 32 + 2 * pr + 1] = 1.0
    c["bo_h"] = bov.astype(bf)
    selm = np.zeros((32, 1024), np.float32)
    for pr in range(8):
        selm[2 * pr, pr * 128:pr * 128 + 64] = 1.0
        selm[2 * pr + 1, pr * 128 + 64:pr * 128 + 128] = 1.0
    c["sel_h"] = selm.astype(bf)
    pm = np.arange(128) % 32
    c["emaske_h"] = (pm < 16).astype(np.float32).reshape(128, 1)
    c["emasko_h"] = (pm >= 16).astype(np.float32).reshape(128, 1)
    c["onescol_h"] = np.ones((128, 1), np.float32)
    c["onescolb_h"] = np.ones((128, 1), np.float32).astype(bf)
    c["ones1r_h"] = np.ones((1, 128), np.float32)
    c["ident_h"] = np.eye(128, dtype=np.float32).astype(bf)
    cv = np.zeros((128, 8), np.float32)
    for j, v in enumerate([1e-5, 0.5, 4.0 / math.sqrt(32.0), 1.0 / math.sqrt(32.0), -0.5, 1.0]):
        cv[:, j] = v
    c["cvals_h"] = cv
    return c


def kernel(L0_attn_hidden_states, L0_attn_Wq, L0_attn_Wk, L0_attn_Wv, L0_attn_Wo,
           L0_attn_g_weight, L0_ffn_clip_x, L0_ffn_clip_down, L0_ffn_w_gate,
           L0_ffn_w_up, L0_ffn_w_down):
    x = np.asarray(L0_attn_hidden_states, np.float32)
    B, L, _ = x.shape
    x_fm = np.ascontiguousarray(x.reshape(B * L, D).T)

    shared = {
        "wqT_h": np.ascontiguousarray(np.asarray(L0_attn_Wq, np.float32).T),
        "wkT_h": np.ascontiguousarray(np.asarray(L0_attn_Wk, np.float32).T),
        "wvT_h": np.ascontiguousarray(np.asarray(L0_attn_Wv, np.float32).T),
        "woT_h": np.ascontiguousarray(np.asarray(L0_attn_Wo, np.float32).T),
        "gw_h": np.asarray(L0_attn_g_weight, np.float32).reshape(1, HD),
        "clipx_h": np.asarray(L0_ffn_clip_x, np.float32).reshape(1, 1),
        "clipd_h": np.asarray(L0_ffn_clip_down, np.float32).reshape(1, 1),
        "wg_nat_h": np.ascontiguousarray(np.asarray(L0_ffn_w_gate, np.float32)),
        "wu_nat_h": np.ascontiguousarray(np.asarray(L0_ffn_w_up, np.float32)),
        "wd_nat_h": np.ascontiguousarray(np.asarray(L0_ffn_w_down, np.float32)),
        "wgT_h": np.ascontiguousarray(np.asarray(L0_ffn_w_gate, np.float32).T),
        "wuT_h": np.ascontiguousarray(np.asarray(L0_ffn_w_up, np.float32).T),
        "wdT_h": np.ascontiguousarray(np.asarray(L0_ffn_w_down, np.float32).T),
    }
    shared.update(_stage_constants())

    nc = bass.Bass("TRN2", target_bir_lowering=False, debug=False)
    es = build(nc)

    in_maps = []
    for c in range(8):
        mm = dict(shared)
        mm["x_h"] = np.ascontiguousarray(x_fm[:, c * NT:(c + 1) * NT])
        in_maps.append(mm)
    res = run_bass_kernel_spmd(nc, in_maps, core_ids=list(range(8)))
    es.close()

    out_fm = np.concatenate([res.results[c]["o_out"] for c in range(8)], axis=1)
    out = np.ascontiguousarray(out_fm.T).reshape(B, L, D).astype(np.float32)
    return out


# version 18
# speedup vs baseline: 1.0384x; 1.0140x over previous
import math
import os
import numpy as np
import ml_dtypes
import concourse.bass as bass
import concourse.mybir as mybir
from concourse.alu_op_type import AluOpType as OP
from concourse.bass_utils import run_bass_kernel_spmd
from contextlib import ExitStack

F32 = mybir.dt.float32
BF16 = mybir.dt.bfloat16
I32 = mybir.dt.int32
AF = mybir.ActivationFunctionType
AX = mybir.AxisListType

D = 1024
NT = 4096
SEQ = 2048
NH = 16
FD = 16
HD = 64
CH = 256
NCHUNK = NT // CH
I_FFN = 2816
RMS_EPS = 1e-5
LN_EPS = 1e-5
QMAX = 127.0
S32 = math.sqrt(32.0)

ENGS = ("tensor", "vector", "scalar", "sync", "gpsimd")


class Prog:
    def __init__(self, nc):
        self.nc = nc
        self.items = {e: [] for e in ENGS}
        self.cnt = {e: 0 for e in ENGS}
        self.sem = {}
        self.last_wait = {}

    def alloc_sems(self, es):
        for e in ENGS:
            self.sem[e] = es.enter_context(self.nc.semaphore("sem_" + e))
        self._es = es
        self.streams = {}

    def stream(self, name, queue, depth):
        sems = [self._es.enter_context(self.nc.semaphore(f"dma_{name}_{i}"))
                for i in range(depth)]
        self.streams[name] = [queue, depth, sems, 0]
        return name

    def dma(self, name, fn):
        queue, depth, sems, k = self.streams[name]
        sem = sems[k % depth]
        self.streams[name][3] = k + 1
        self.items[queue].append((lambda h, f=fn, s=sem: f(h).then_inc(s, 16), 0))
        return (name, k)

    def wait_dma(self, eng, token):
        if token is None:
            return
        name, k = token
        queue, depth, sems, _ = self.streams[name]
        sem = sems[k % depth]
        val = 16 * (k // depth + 1)
        key = (eng, name, k % depth)
        if self.last_wait.get(key, -1) >= val:
            return
        self.last_wait[key] = val
        self.items[eng].append((lambda h, s=sem, v=val: h.wait_ge(s, v), 0))

    def wait_stream_total(self, eng, name):
        queue, depth, sems, k = self.streams[name]
        for i in range(depth):
            n_on = (k - i + depth - 1) // depth
            if n_on > 0:
                self.items[eng].append(
                    (lambda h, s=sems[i], v=16 * n_on: h.wait_ge(s, v), 0))

    def do(self, eng, fn, inc):
        self.cnt[eng] += inc
        self.items[eng].append((fn, inc))
        return self.cnt[eng]

    def raw(self, eng, fn):
        self.items[eng].append((fn, 0))

    def wait(self, eng, on_eng, val):
        if val is None or val <= 0:
            return
        assert val <= self.cnt[on_eng], f"{eng} waits {on_eng}>={val} cnt={self.cnt[on_eng]}"
        key = (eng, on_eng)
        if self.last_wait.get(key, -1) >= val:
            return
        self.last_wait[key] = val
        sem = self.sem[on_eng]
        self.items[eng].append((lambda h, s=sem, v=val: h.wait_ge(s, v), 0))

    def barrier(self):
        for name, (queue, depth, sems, k) in self.streams.items():
            if k > 0:
                self.wait_stream_total(queue, name)
        for e in ("sync", "gpsimd"):
            self.do(e, lambda h: h.nop(), 1)
        snap = dict(self.cnt)
        for e in ENGS:
            for o in ENGS:
                if e != o:
                    self.wait(e, o, snap[o])

    def emit_block(self, blk):
        for e in ENGS:
            def mk(engname):
                def _(h):
                    sem = self.sem[engname]
                    for fn, inc in self.items[engname]:
                        inst = fn(h)
                        if inc:
                            inst.then_inc(sem, inc)
                return _
            getattr(blk, e)(mk(e))


def build(nc):
    PHASELIM = int(os.environ.get("K_PHASELIM", "9"))
    es = ExitStack()
    p = Prog(nc)
    p.alloc_sems(es)
    sem_cc = es.enter_context(nc.semaphore("sem_cc"))
    cc_state = [0]

    def T(fn): return p.do("tensor", fn, 1)
    def V(fn): return p.do("vector", fn, 1)
    def S(fn): return p.do("scalar", fn, 1)
    def W(eng, on, val): p.wait(eng, on, val)
    DQ = p.dma
    WD = p.wait_dma
    WT = p.wait_stream_total
    for nm, q, d in [("consts", "gpsimd", 1), ("bounce", "gpsimd", 1),
                     ("spill_xn", "gpsimd", 2), ("spill_uh", "gpsimd", 2),
                     ("spill_s", "gpsimd", 2), ("wdsp", "gpsimd", 1),
                     ("attnw", "gpsimd", 4),
                     ("wqsp_d", "gpsimd", 2), ("wqsp_g", "gpsimd", 2), ("wqsp_u", "gpsimd", 2),
                     ("wnat", "sync", 2), ("wtr", "sync", 2),
                     ("spill_xf", "gpsimd", 4), ("xfl", "sync", 4),
                     ("xA", "sync", 2), ("xnch", "sync", 2),
                     ("wcolg", "sync", 2), ("wcolu", "sync", 2),
                     ("wdl", "sync", 1), ("uhl", "sync", 4),
                     ("cls", "sync", 2), ("out", "sync", 2)]:
        p.stream(nm, q, d)

    def din(name, shape, dt=F32):
        return nc.dram_tensor(name, shape, dt, kind="ExternalInput").ap()

    x_h = din("x_h", [D, NT])
    wqT_h = din("wqT_h", [D, 256]); wkT_h = din("wkT_h", [D, 256])
    wvT_h = din("wvT_h", [D, D]); woT_h = din("woT_h", [D, D])
    gw_h = din("gw_h", [1, HD])
    clipx_h = din("clipx_h", [1, 1]); clipd_h = din("clipd_h", [1, 1])
    wg_nat_h = din("wg_nat_h", [I_FFN, D]); wu_nat_h = din("wu_nat_h", [I_FFN, D])
    wd_nat_h = din("wd_nat_h", [D, I_FFN])
    wgT_h = din("wgT_h", [D, I_FFN]); wuT_h = din("wuT_h", [D, I_FFN])
    wdT_h = din("wdT_h", [I_FFN, D])
    triu_h = din("triu_h", [128, 128], BF16)
    corrA_h = din("corrA_h", [128, 256], BF16)
    corrB_h = din("corrB_h", [128, 128], BF16)
    EA0_h = din("EA0_h", [128, 256], BF16); EA1_h = din("EA1_h", [128, 256], BF16)
    EB0_h = din("EB0_h", [128, 256], BF16); EB1_h = din("EB1_h", [128, 256], BF16)
    s1mask_h = din("s1mask_h", [128, 256])
    bo_h = din("bo_h", [128, 256], BF16)
    sel_h = din("sel_h", [32, 1024], BF16)
    emaske_h = din("emaske_h", [128, 1]); emasko_h = din("emasko_h", [128, 1])
    onescol_h = din("onescol_h", [128, 1]); onescolb_h = din("onescolb_h", [128, 1], BF16)
    ones1r_h = din("ones1r_h", [1, 128])
    ident_h = din("ident_h", [128, 128], BF16)
    cvals_h = din("cvals_h", [128, 8])

    o_out = nc.dram_tensor("o_out", [D, NT], F32, kind="ExternalOutput").ap()

    xn_sc = nc.dram_tensor("xn_sc", [D, NT], BF16).ap()
    xf_sc = nc.dram_tensor("xf_sc", [D, NT], F32).ap()
    uh_sc = nc.dram_tensor("uh_sc", [I_FFN, NT], F32).ap()
    s_sc = nc.dram_tensor("s_sc", [D, NT], BF16).ap()
    wg_sc = nc.dram_tensor("wg_sc", [128, 8 * I_FFN], BF16).ap()
    wu_sc = nc.dram_tensor("wu_sc", [128, 8 * I_FFN], BF16).ap()
    wd_sc = nc.dram_tensor("wd_sc", [128, 22 * 1024], BF16).ap()
    mr_sc = nc.dram_tensor("mr_sc", [1, 4096], F32).ap()
    cc_t = [(nc.dram_tensor("cc1_in", [1, 64], F32), nc.dram_tensor("cc1_out", [1, 64], F32)),
            (nc.dram_tensor("cc2_in", [1, 64], F32), nc.dram_tensor("cc2_out", [1, 64], F32))]

    def sbp(name, shape, dt):
        return es.enter_context(nc.sbuf_tensor(name, shape, dt))

    triu = sbp("triu", [128, 128], BF16)
    corrA = sbp("corrA", [128, 256], BF16)
    corrB = sbp("corrB", [128, 128], BF16)
    EA = [sbp("EA0", [128, 256], BF16), sbp("EA1", [128, 256], BF16)]
    EB = [sbp("EB0", [128, 256], BF16), sbp("EB1", [128, 256], BF16)]
    s1mask = sbp("s1mask", [128, 256], F32)
    bo = sbp("bo", [128, 256], BF16)
    sel = sbp("sel", [32, 1024], BF16)
    emaske = sbp("emaske", [128, 1], F32)
    emasko = sbp("emasko", [128, 1], F32)
    onescol = sbp("onescol", [128, 1], F32)
    onescolb = sbp("onescolb", [128, 1], BF16)
    ones1r = sbp("ones1r", [1, 128], F32)
    ident = sbp("ident", [128, 128], BF16)
    gwcol = sbp("gwcol", [128, 1], F32)
    clips = sbp("clips", [1, 2], F32)
    maxst = sbp("maxst", [128, 64], F32)
    maxst2 = sbp("maxst2", [128, 192], F32)
    gmcol = sbp("gmcol", [128, 1], F32)
    rmg = sbp("rmg", [128, 22], F32)
    rmu = sbp("rmu", [128, 22], F32)
    rmd = sbp("rmd", [128, 8], F32)
    scg = sbp("scg", [128, 22], F32)
    scu = sbp("scu", [128, 22], F32)
    scd = sbp("scd", [128, 8], F32)
    invsx_col = sbp("invsx_col", [128, 1], F32)
    invsd_col = sbp("invsd_col", [128, 1], F32)
    smallrow = sbp("smallrow", [1, 128], F32)
    qth2 = sbp("qth2", [128, 2 * 512], F32)
    qi32 = sbp("qi32", [128, 2 * 512], I32)
    qgt2 = sbp("qgt2", [128, 2 * 512], F32)
    cvals = sbp("cvals", [128, 8], F32)
    _CONSTVALS = [LN_EPS, 0.5, 4.0 / S32, 1.0 / S32, -0.5, 1.0]
    for _j, _v in enumerate(_CONSTVALS):
        nc.const_aps.register(F32, float(_v), cvals[:, _j:_j + 1])

    PS = [es.enter_context(nc.psum_tensor(f"ps{i}", [128, 512], F32)) for i in range(8)]
    ps_guard = {i: [] for i in range(8)}

    def use_ps(i):
        for eng, cnt in ps_guard[i]:
            W("tensor", eng, cnt)
        ps_guard[i] = []

    def guard_ps(i, eng, cnt):
        ps_guard[i].append((eng, cnt))

    for dst, src in [(triu, triu_h), (corrA, corrA_h), (corrB, corrB_h),
                     (EA[0], EA0_h), (EA[1], EA1_h), (EB[0], EB0_h), (EB[1], EB1_h),
                     (s1mask, s1mask_h), (bo, bo_h), (sel, sel_h),
                     (emaske, emaske_h), (emasko, emasko_h),
                     (onescol, onescol_h), (onescolb, onescolb_h), (ones1r, ones1r_h),
                     (ident, ident_h)]:
        DQ("consts", lambda h, d=dst, s=src: h.dma_start(d[:, :], s))
    DQ("consts", lambda h: h.dma_start(gwcol[0:64, 0:1], gw_h.rearrange("o (d u) -> (o d) u", u=1)))
    DQ("consts", lambda h: h.dma_start(gwcol[64:128, 0:1], gw_h.rearrange("o (d u) -> (o d) u", u=1)))
    DQ("consts", lambda h: h.dma_start(clips[0:1, 0:1], clipx_h))
    DQ("consts", lambda h: h.dma_start(clips[0:1, 1:2], clipd_h))
    DQ("consts", lambda h: h.dma_start(cvals[:, :], cvals_h))

    def global_scale(stash, ncols, cc_idx, clip_slot, gmax_slot, sig_slot, s_slot, inv_slot,
                     inv_col, rm_list):
        c_f = V(lambda h, n=ncols: h.reduce_max(gmcol[:, 0:1], stash[:, 0:n], axis=AX.X))
        W("gpsimd", "vector", c_f)
        t_o = DQ("bounce", lambda h: h.dma_start(
            mr_sc[0:1, 0:128].rearrange("o (a b) -> (o b) a", a=1), gmcol[:, 0:1],
            allow_slow_non_contiguous=True))
        WD("gpsimd", t_o)
        t_i = DQ("bounce", lambda h: h.dma_start(smallrow[0:1, :], mr_sc[0:1, 0:128]))
        WD("vector", t_i)
        c_r = V(lambda h, sl=gmax_slot: h.reduce_max(smallrow[0:1, sl:sl + 1], smallrow[0:1, 0:128],
                                                     axis=AX.X))
        W("gpsimd", "vector", c_r)
        tin, tout = cc_t[cc_idx]
        t_b = DQ("bounce", lambda h, t=tin: h.dma_start(t.ap()[0:1, 0:64],
                                                        smallrow[0:1, 0:64]))
        WD("gpsimd", t_b)
        cc_state[0] += 1
        tgt = cc_state[0]
        p.raw("gpsimd", lambda h, ti=tin, to=tout: h.collective_compute(
            "AllReduce", OP.max, replica_groups=[list(range(8))],
            ins=[ti.ap().opt()], outs=[to.ap().opt()]).then_inc(sem_cc, 1))
        p.raw("gpsimd", lambda h, t=tgt: h.wait_ge(sem_cc, t))
        t_gi = DQ("bounce", lambda h, sl=gmax_slot, t=tout: h.dma_start(
            smallrow[0:1, sl:sl + 1], t.ap()[0:1, sl:sl + 1]))
        WD("scalar", t_gi)
        WD("vector", t_gi)
        c_sg = S(lambda h, a=sig_slot, cs=clip_slot: h.activation(
            smallrow[0:1, a:a + 1], clips[0:1, cs:cs + 1], AF.Sigmoid))
        W("vector", "scalar", c_sg)
        c1 = V(lambda h, a=sig_slot, g=gmax_slot, s=s_slot: h.tensor_tensor(
            smallrow[0:1, s:s + 1], smallrow[0:1, g:g + 1], smallrow[0:1, a:a + 1], OP.mult))
        W("vector", "vector", c1)
        c2 = V(lambda h, s=s_slot: h.tensor_scalar(
            smallrow[0:1, s:s + 1], smallrow[0:1, s:s + 1], 1.0 / QMAX, None, OP.mult))
        W("vector", "vector", c2)
        c3 = V(lambda h, s=s_slot, iv=inv_slot: h.reciprocal(
            smallrow[0:1, iv:iv + 1], smallrow[0:1, s:s + 1]))
        use_ps(6)
        W("tensor", "vector", c3)
        c_mm = T(lambda h, s=s_slot: h.matmul(PS[6][:, 0:2], ones1r[0:1, :],
                                              smallrow[0:1, s:s + 2], start=True, stop=True))
        W("vector", "tensor", c_mm)
        c_e = V(lambda h, ic=inv_col: h.tensor_copy(ic[:, 0:1], PS[6][:, 1:2]))
        last = c_e
        for (sc_dst, rm_src) in rm_list:
            W("vector", "vector", last)
            last = V(lambda h, d=sc_dst, r=rm_src: h.tensor_scalar(
                d[:, :], r[:, :], PS[6][:, 0:1], 1.0 / QMAX, OP.mult, OP.mult))
        guard_ps(6, "vector", last)
        return last

    wq_res = {}
    wq_prev = {}
    wq_state = [None, False]

    def pump(budget):
        it = wq_state[0]
        while budget > 0 and not wq_state[1]:
            try:
                budget -= next(it)
            except StopIteration:
                wq_state[1] = True

    def wq_gen(nat_h, wT_h, n_rows, n_k, rmcols, w_out_sc, spst, tag,
               wnat, wtr, invb, invrow, qthw, qr0w, qgtw, wqow):
        nrt = n_rows // 128
        inner = 128 * n_k
        spq = p.streams[spst][0]
        red_marks = {}
        for rt in range(nrt):
            b = rt % 2
            if rt >= 2:
                W("sync", "vector", red_marks[rt - 2])
            elif wq_prev.get("red"):
                W("sync", "vector", wq_prev["red"])
            t_ld = DQ("wnat", lambda h, rt=rt, b=b: h.dma_start(
                wnat[:, b * inner:(b + 1) * inner], nat_h[rt * 128:(rt + 1) * 128, :]))
            WD("vector", t_ld)
            red_marks[rt] = V(lambda h, rt=rt, b=b: h.reduce_max(
                rmcols[:, rt:rt + 1], wnat[:, b * inner:(b + 1) * inner],
                axis=AX.X, apply_absolute_value=True))
            yield 3.2 if n_k == 22 else 1.4
        W("gpsimd", "vector", red_marks[nrt - 1])
        if wq_prev.get("mm"):
            W("gpsimd", "tensor", wq_prev["mm"])
        t_o = DQ("bounce", lambda h, nrt=nrt: h.dma_start(
            mr_sc[0:1, 0:nrt * 128].rearrange("o (a b) -> (o b) a", a=nrt),
            rmcols[:, 0:nrt], allow_slow_non_contiguous=True))
        WD("gpsimd", t_o)
        t_i = DQ("bounce", lambda h, nrt=nrt, n_rows=n_rows: h.dma_start(
            invrow[0:1, 0:n_rows], mr_sc[0:1, 0:nrt * 128]))
        yield 4.0
        WD("vector", t_i)
        c_rec = V(lambda h, n_rows=n_rows: h.reciprocal(invrow[0:1, 0:n_rows],
                                                        invrow[0:1, 0:n_rows]))
        W("vector", "vector", c_rec)
        c_sc = V(lambda h, n_rows=n_rows: h.tensor_scalar(
            invrow[0:1, 0:n_rows], invrow[0:1, 0:n_rows], QMAX, None, OP.mult))
        nb = (n_rows + 511) // 512
        WT("tensor", "consts")
        W("tensor", "vector", c_sc)
        c_ev = None
        c_mm = None
        for bb in range(nb):
            lo, hi = bb * 512, min(n_rows, (bb + 1) * 512)
            use_ps(6)
            c_mm = T(lambda h, lo=lo, hi=hi: h.matmul(
                PS[6][:, 0:hi - lo], ones1r[0:1, :], invrow[0:1, lo:hi], start=True, stop=True))
            W("vector", "tensor", c_mm)
            c_ev = V(lambda h, lo=lo, hi=hi: h.tensor_copy(invb[:, lo:hi], PS[6][:, 0:hi - lo]))
            guard_ps(6, "vector", c_ev)
        c_invb = c_ev
        yield 2.0
        sp_marks = {}
        q_marks = {}
        for k in range(n_k):
            b = k % 2
            if k >= 2:
                W("sync", "vector", q_marks[k - 2])
            elif wq_prev.get("chain"):
                W("sync", "vector", wq_prev["chain"])
            t_ld = DQ("wtr", lambda h, k=k, b=b, n_rows=n_rows: h.dma_start(
                wtr[:, b * n_rows:(b + 1) * n_rows], wT_h[k * 128:(k + 1) * 128, :]))
            WD("vector", t_ld)
            W("vector", "vector", c_invb)
            if k >= 2:
                WD("vector", sp_marks[k - 2])
                W("vector", "vector", q_marks[k - 2])
            elif wq_prev.get("sp2"):
                WD("vector", wq_prev["sp2"][0])
                WD("vector", wq_prev["sp2"][1])
            c1 = V(lambda h, b=b, n_rows=n_rows: h.tensor_tensor(
                qthw[:, 0:n_rows], wtr[:, b * n_rows:(b + 1) * n_rows],
                invb[:, 0:n_rows], OP.mult))
            W("vector", "vector", c1)
            c2 = V(lambda h, n_rows=n_rows: h.tensor_scalar(
                qthw[:, 0:n_rows], qthw[:, 0:n_rows], 0.5, 127.9, OP.add, OP.min))
            W("vector", "vector", c2)
            c4 = V(lambda h, n_rows=n_rows: h.tensor_scalar(
                qr0w[:, 0:n_rows], qthw[:, 0:n_rows], -127.9, None, OP.max))
            W("vector", "vector", c4)
            c5 = V(lambda h, n_rows=n_rows: h.tensor_tensor(
                qgtw[:, 0:n_rows], qr0w[:, 0:n_rows], qthw[:, 0:n_rows], OP.is_gt))
            W("vector", "vector", c5)
            c6 = V(lambda h, b=b, n_rows=n_rows: h.scalar_tensor_tensor(
                wqow[:, b * n_rows:(b + 1) * n_rows], qgtw[:, 0:n_rows], -1.0,
                qr0w[:, 0:n_rows], OP.mult, OP.add))
            q_marks[k] = c6
            W(spq, "vector", c6)
            sp_marks[k] = DQ(spst, lambda h, k=k, b=b, n_rows=n_rows: h.dma_start(
                w_out_sc[:, k * n_rows:(k + 1) * n_rows],
                wqow[:, b * n_rows:(b + 1) * n_rows]))
            yield 4.9 if n_k == 22 else 12.8
        wq_prev.clear()
        wq_prev.update({"red": red_marks[nrt - 1], "chain": q_marks[n_k - 1], "mm": c_mm,
                        "sp2": {0: sp_marks[n_k - 2], 1: sp_marks[n_k - 1]}})
        wq_res[tag] = (sp_marks[n_k - 1], q_marks[n_k - 1])

    if PHASELIM < 1:
        with nc.Block() as blk:
            p.emit_block(blk)
        return es

    with nc.sbuf_tensor("xbuf", [128, 2 * 8 * 512], F32) as xbuf, \
         nc.sbuf_tensor("xsq", [128, 2 * 512], F32) as xsq, \
         nc.sbuf_tensor("xnf", [128, 4 * 512], F32) as xnf, \
         nc.sbuf_tensor("xnb", [128, 8 * 512], BF16) as xnb, \
         nc.sbuf_tensor("mrows", [1, 2 * 512], F32) as mrows, \
         nc.sbuf_tensor("rrows", [1, 2 * 512], F32) as rrows, \
         nc.sbuf_tensor("t1row", [1, 512], F32) as t1row, \
         nc.sbuf_tensor("t2row", [1, 512], F32) as t2row, \
         nc.sbuf_tensor("wnat", [128, 2 * I_FFN], F32) as wnat, \
         nc.sbuf_tensor("wtr", [128, 2 * I_FFN], F32) as wtr, \
         nc.sbuf_tensor("invb", [128, I_FFN], F32) as invb, \
         nc.sbuf_tensor("invrow", [1, I_FFN], F32) as invrow, \
         nc.sbuf_tensor("qthw", [128, I_FFN], F32) as qthw, \
         nc.sbuf_tensor("qr0w", [128, I_FFN], I32) as qr0w, \
         nc.sbuf_tensor("qgtw", [128, I_FFN], F32) as qgtw, \
         nc.sbuf_tensor("wqow", [128, 2 * I_FFN], BF16) as wqow:

        def _gen_all():
            yield from wq_gen(wd_nat_h, wdT_h, D, 22, rmd, wd_sc, "wqsp_d", "d",
                              wnat, wtr, invb, invrow, qthw, qr0w, qgtw, wqow)
            yield from wq_gen(wg_nat_h, wgT_h, I_FFN, 8, rmg, wg_sc, "wqsp_g", "g",
                              wnat, wtr, invb, invrow, qthw, qr0w, qgtw, wqow)
            yield from wq_gen(wu_nat_h, wuT_h, I_FFN, 8, rmu, wu_sc, "wqsp_u", "u",
                              wnat, wtr, invb, invrow, qthw, qr0w, qgtw, wqow)
        wq_state[0] = _gen_all()

        def xtile(b, pt):
            return xbuf[:, (b * 8 + pt) * 512:(b * 8 + pt) * 512 + 512]

        xf_w = xf_sc.rearrange("(pt p) (tt c) -> p pt tt c", p=128, tt=8)

        sqmm_marks = {}
        a_done = {}
        cast_g = {}
        spxf = {}
        prev_cmq = 0
        for tt in range(8):
            b = tt % 2
            sl2 = tt % 2
            if tt >= 2:
                WD("sync", a_done[tt - 2])
            c_ld = DQ("xA", lambda h, tt=tt, b=b: h.dma_start(
                xbuf.rearrange("p (z pt c) -> p z pt c", z=2, pt=8)[:, b],
                x_h.rearrange("(pt p) (tt c) -> p tt pt c", p=128, tt=8)[:, tt]))
            WD("scalar", c_ld)
            WT("scalar", "consts")
            W("scalar", "tensor", prev_cmq)
            use_ps(4); use_ps(5)
            WD("tensor", c_ld)
            WT("tensor", "consts")
            sq_marks = []
            for pt in range(8):
                if pt >= 2:
                    W("scalar", "tensor", sqmm_marks[pt - 2])
                sq_marks.append(S(lambda h, pt=pt, b=b: h.activation(
                    xsq[:, (pt % 2) * 512:(pt % 2 + 1) * 512], xtile(b, pt), AF.Square)))
                T(lambda h, pt=pt, b=b: h.matmul(
                    PS[4][0:1, :], onescol[:, 0:1], xtile(b, pt), start=(pt == 0), stop=(pt == 7)))
                W("tensor", "scalar", sq_marks[pt])
                c_mq = T(lambda h, pt=pt: h.matmul(
                    PS[5][0:1, :], onescol[:, 0:1], xsq[:, (pt % 2) * 512:(pt % 2 + 1) * 512],
                    start=(pt == 0), stop=(pt == 7)))
                sqmm_marks[pt] = c_mq
            pump(14.0)
            W("vector", "tensor", c_mq)
            c_m = V(lambda h, sl2=sl2: h.tensor_scalar(
                mrows[0:1, sl2 * 512:(sl2 + 1) * 512], PS[4][0:1, :], 1.0 / D, None, OP.mult))
            W("vector", "vector", c_m)
            c_mq2 = V(lambda h: h.tensor_scalar(t1row[0:1, :], PS[5][0:1, :], 1.0 / D, None, OP.mult))
            guard_ps(4, "vector", c_m); guard_ps(5, "vector", c_mq2)
            W("vector", "vector", c_mq2)
            c_m2 = V(lambda h, sl2=sl2: h.tensor_tensor(
                t2row[0:1, :], mrows[0:1, sl2 * 512:(sl2 + 1) * 512],
                mrows[0:1, sl2 * 512:(sl2 + 1) * 512], OP.mult))
            W("vector", "vector", c_m2)
            c_var = V(lambda h: h.tensor_tensor(t1row[0:1, :], t1row[0:1, :], t2row[0:1, :],
                                                OP.subtract))
            W("scalar", "vector", c_var)
            c_ln = S(lambda h: h.activation(t2row[0:1, :], t1row[0:1, :], AF.Ln, bias=LN_EPS))
            W("scalar", "scalar", c_ln)
            c_rs = S(lambda h, sl2=sl2: h.activation(
                rrows[0:1, sl2 * 512:(sl2 + 1) * 512], t2row[0:1, :], AF.Exp, scale=-0.5))
            use_ps(4); use_ps(5)
            W("tensor", "vector", c_m)
            W("tensor", "scalar", c_rs)
            T(lambda h, sl2=sl2: h.matmul(PS[4][:, :], ones1r[0:1, :],
                                          mrows[0:1, sl2 * 512:(sl2 + 1) * 512],
                                          start=True, stop=True))
            c_br = T(lambda h, sl2=sl2: h.matmul(PS[5][:, :], ones1r[0:1, :],
                                                 rrows[0:1, sl2 * 512:(sl2 + 1) * 512],
                                                 start=True, stop=True))
            pump(7.0)
            W("vector", "tensor", c_br)
            if tt >= 1:
                WD("scalar", a_done[tt - 1])
            for pt in range(8):
                ti = tt * 8 + pt
                s4 = ti % 4
                if ti >= 4:
                    W("vector", "scalar", cast_g[ti - 4])
                    WD("vector", spxf[ti - 4])
                c_s = V(lambda h, s4=s4, pt=pt, b=b: h.tensor_tensor(
                    xnf[:, s4 * 512:(s4 + 1) * 512], xtile(b, pt), PS[4][:, :],
                    OP.subtract))
                W("vector", "vector", c_s)
                c_x = V(lambda h, s4=s4: h.tensor_tensor(
                    xnf[:, s4 * 512:(s4 + 1) * 512],
                    xnf[:, s4 * 512:(s4 + 1) * 512], PS[5][:, :], OP.mult))
                W("vector", "vector", c_x)
                V(lambda h, s4=s4, ti=ti: h.reduce_max(
                    maxst[:, ti:ti + 1], xnf[:, s4 * 512:(s4 + 1) * 512],
                    axis=AX.X, apply_absolute_value=True))
                W("scalar", "vector", c_x)
                cast_g[ti] = S(lambda h, s4=s4, pt=pt: h.copy(
                    xnb[:, pt * 512:(pt + 1) * 512], xnf[:, s4 * 512:(s4 + 1) * 512]))
                W("gpsimd", "vector", c_x)
                spxf[ti] = DQ("spill_xf", lambda h, s4=s4, pt=pt, tt=tt: h.dma_start(
                    xf_w[:, pt, tt], xnf[:, s4 * 512:(s4 + 1) * 512]))
            guard_ps(4, "vector", c_x); guard_ps(5, "vector", c_x)
            prev_cmq = c_mq
            W("gpsimd", "scalar", cast_g[tt * 8 + 7])
            a_done[tt] = DQ("spill_xn", lambda h, tt=tt: h.dma_start(
                xn_sc.rearrange("(pt p) (tt c) -> p tt pt c", p=128, tt=8)[:, tt],
                xnb.rearrange("p (pt c) -> p pt c", pt=8)))

        W("vector", "vector", p.cnt["vector"])
        c_f = V(lambda h: h.reduce_max(gmcol[:, 0:1], maxst[:, 0:64], axis=AX.X))
        W("gpsimd", "vector", c_f)
        t_o1 = DQ("bounce", lambda h: h.dma_start(
            mr_sc[0:1, 0:128].rearrange("o (a b) -> (o b) a", a=1), gmcol[:, 0:1],
            allow_slow_non_contiguous=True))
        WD("gpsimd", t_o1)
        t_i1 = DQ("bounce", lambda h: h.dma_start(smallrow[0:1, :], mr_sc[0:1, 0:128]))
        pump(22.0)
        WD("vector", t_i1)
        c_r = V(lambda h: h.reduce_max(smallrow[0:1, 2:3], smallrow[0:1, 0:128], axis=AX.X))
        W("gpsimd", "vector", c_r)
        t_b1 = DQ("bounce", lambda h: h.dma_start(cc_t[0][0].ap()[0:1, 0:64],
                                                  smallrow[0:1, 0:64]))
        WD("gpsimd", t_b1)
        cc_state[0] += 1
        tgt1 = cc_state[0]
        p.raw("gpsimd", lambda h: h.collective_compute(
            "AllReduce", OP.max, replica_groups=[list(range(8))],
            ins=[cc_t[0][0].ap().opt()], outs=[cc_t[0][1].ap().opt()]).then_inc(sem_cc, 1))
        pump(1e9)
        p.raw("gpsimd", lambda h, t=tgt1: h.wait_ge(sem_cc, t))
        t_gi1 = DQ("bounce", lambda h: h.dma_start(smallrow[0:1, 2:3],
                                                   cc_t[0][1].ap()[0:1, 2:3]))
        WD("scalar", t_gi1)
        WD("vector", t_gi1)
        c_sg = S(lambda h: h.activation(smallrow[0:1, 3:4], clips[0:1, 0:1], AF.Sigmoid))
        W("vector", "scalar", c_sg)
        c1x = V(lambda h: h.tensor_tensor(smallrow[0:1, 4:5], smallrow[0:1, 2:3],
                                          smallrow[0:1, 3:4], OP.mult))
        W("vector", "vector", c1x)
        c2x = V(lambda h: h.tensor_scalar(smallrow[0:1, 4:5], smallrow[0:1, 4:5],
                                          1.0 / QMAX, None, OP.mult))
        W("vector", "vector", c2x)
        c3x = V(lambda h: h.reciprocal(smallrow[0:1, 5:6], smallrow[0:1, 4:5]))
        use_ps(6)
        W("tensor", "vector", c3x)
        c_mmx = T(lambda h: h.matmul(PS[6][:, 0:2], ones1r[0:1, :], smallrow[0:1, 4:6],
                                     start=True, stop=True))
        W("vector", "tensor", c_mmx)
        c_e = V(lambda h: h.tensor_copy(invsx_col[:, 0:1], PS[6][:, 1:2]))
        W("vector", "vector", c_e)
        cf1 = V(lambda h: h.tensor_scalar(scg[:, :], rmg[:, :], PS[6][:, 0:1], 1.0 / QMAX,
                                          OP.mult, OP.mult))
        W("vector", "vector", cf1)
        c_sx = V(lambda h: h.tensor_scalar(scu[:, :], rmu[:, :], PS[6][:, 0:1], 1.0 / QMAX,
                                           OP.mult, OP.mult))
        guard_ps(6, "vector", c_sx)
        t_wdq = wq_res["d"][0]
        p.barrier()

    if PHASELIM < 4:
        with nc.Block() as blk:
            p.emit_block(blk)
        return es
    with nc.sbuf_tensor("xq", [128, 8 * NT], BF16) as xq, \
         nc.sbuf_tensor("xnl", [128, 4 * 512], F32) as xnl, \
         nc.sbuf_tensor("slbuf", [128, 2 * 512], F32) as slbuf, \
         nc.sbuf_tensor("uhbuf", [128, 2 * 512], F32) as uhbuf, \
         nc.sbuf_tensor("wgq", [128, 8 * I_FFN], BF16) as wgq, \
         nc.sbuf_tensor("wuq", [128, 8 * I_FFN], BF16) as wuq:

        xf_r = xf_sc.rearrange("(pt p) (tt c) -> p pt tt c", p=128, tt=8)
        th_m = {}
        qg_m = {}
        bq_tt_done = {}

        def emit_bq(tt):
            for pt in range(8):
                ti = tt * 8 + pt
                sl = ti % 4
                qb = ti % 2
                if ti >= 4:
                    W("sync", "scalar", th_m[ti - 4])
                t_ld = DQ("xfl", lambda h, pt=pt, tt=tt, sl=sl: h.dma_start(
                    xnl[:, sl * 512:(sl + 1) * 512], xf_r[:, pt, tt]))
                WD("scalar", t_ld)
                if ti == 0:
                    W("scalar", "vector", c_sx)
                if ti >= 2:
                    W("scalar", "vector", qg_m[ti - 2])
                c_th = S(lambda h, sl=sl, qb=qb: h.activation(
                    qth2[:, qb * 512:(qb + 1) * 512], xnl[:, sl * 512:(sl + 1) * 512],
                    AF.Identity, bias=0.5, scale=invsx_col[:, 0:1]))
                th_m[ti] = c_th
                W("vector", "scalar", c_th)
                c_cl = V(lambda h, qb=qb: h.tensor_scalar(
                    qth2[:, qb * 512:(qb + 1) * 512], qth2[:, qb * 512:(qb + 1) * 512],
                    127.9, None, OP.min))
                W("vector", "vector", c_cl)
                c_i = V(lambda h, qb=qb: h.tensor_scalar(
                    qi32[:, qb * 512:(qb + 1) * 512], qth2[:, qb * 512:(qb + 1) * 512],
                    -127.9, None, OP.max))
                W("vector", "vector", c_i)
                c_g = V(lambda h, qb=qb: h.tensor_tensor(
                    qgt2[:, qb * 512:(qb + 1) * 512], qi32[:, qb * 512:(qb + 1) * 512],
                    qth2[:, qb * 512:(qb + 1) * 512], OP.is_gt))
                qg_m[ti] = c_g
                W("vector", "vector", c_g)
                c_q = V(lambda h, pt=pt, tt=tt, qb=qb: h.scalar_tensor_tensor(
                    xq[:, pt * NT + tt * 512:pt * NT + (tt + 1) * 512],
                    qgt2[:, qb * 512:(qb + 1) * 512], -1.0,
                    qi32[:, qb * 512:(qb + 1) * 512], OP.mult, OP.add))
                if pt == 7:
                    bq_tt_done[tt] = c_q

        uh_mx = {}
        b_sp = {}
        WT("sync", "wqsp_g")
        WT("sync", "wqsp_u")
        t_wgl = DQ("wcolg", lambda h: h.dma_start(wgq[:, :], wg_sc))
        t_wul = DQ("wcolu", lambda h: h.dma_start(wuq[:, :], wu_sc))
        WD("tensor", t_wgl)
        WD("tensor", t_wul)

        def emit_b(tt):
            W("tensor", "vector", bq_tt_done[tt])
            for m in range(22):
                idx = tt * 22 + m
                pg_ = idx % 2
                pu_ = 2 + idx % 2
                ub = idx % 2
                use_ps(pg_); use_ps(pu_)
                for k in range(8):
                    T(lambda h, k=k, tt=tt, m=m, pg_=pg_: h.matmul(
                        PS[pg_][:, :], wgq[:, k * I_FFN + m * 128:k * I_FFN + (m + 1) * 128],
                        xq[:, k * NT + tt * 512:k * NT + (tt + 1) * 512],
                        start=(k == 0), stop=(k == 7)))
                for k in range(8):
                    T(lambda h, k=k, tt=tt, m=m, pu_=pu_: h.matmul(
                        PS[pu_][:, :], wuq[:, k * I_FFN + m * 128:k * I_FFN + (m + 1) * 128],
                        xq[:, k * NT + tt * 512:k * NT + (tt + 1) * 512],
                        start=(k == 0), stop=(k == 7)))
                W("scalar", "tensor", p.cnt["tensor"])
                W("scalar", "vector", c_sx)
                if idx >= 2:
                    W("scalar", "vector", uh_mx[idx - 2])
                c_sl = S(lambda h, m=m, pg_=pg_, ub=ub: h.activation(
                    slbuf[:, ub * 512:(ub + 1) * 512], PS[pg_][:, :], AF.Silu,
                    scale=scg[:, m:m + 1]))
                guard_ps(pg_, "scalar", c_sl)
                W("vector", "scalar", c_sl)
                if idx >= 2:
                    WD("vector", b_sp[idx - 2])
                c_uh = V(lambda h, m=m, pu_=pu_, ub=ub: h.scalar_tensor_tensor(
                    uhbuf[:, ub * 512:(ub + 1) * 512], PS[pu_][:, :], scu[:, m:m + 1],
                    slbuf[:, ub * 512:(ub + 1) * 512], OP.mult, OP.mult))
                guard_ps(pu_, "vector", c_uh)
                W("vector", "vector", c_uh)
                uh_mx[idx] = V(lambda h, idx=idx, ub=ub: h.reduce_max(
                    maxst2[:, idx:idx + 1], uhbuf[:, ub * 512:(ub + 1) * 512],
                    axis=AX.X, apply_absolute_value=True))
                W("gpsimd", "vector", uh_mx[idx])
                b_sp[idx] = DQ("spill_uh", lambda h, m=m, tt=tt, ub=ub: h.dma_start(
                    uh_sc[m * 128:(m + 1) * 128, tt * 512:(tt + 1) * 512],
                    uhbuf[:, ub * 512:(ub + 1) * 512]))

        emit_bq(0)
        emit_bq(1)
        for tt in range(8):
            emit_b(tt)
            if tt + 2 < 8:
                emit_bq(tt + 2)

        if PHASELIM < 6:
            p.barrier()
            with nc.Block() as blk:
                p.emit_block(blk)
            return es
        W("vector", "vector", uh_mx[22 * 8 - 1])
        c_sd = global_scale(maxst2, 176, 1, 1, 6, 7, 8, 9, invsd_col, [(scd, rmd)])
        p.barrier()

    if PHASELIM < 7:
        with nc.Block() as blk:
            p.emit_block(blk)
        return es

    if PHASELIM < 7:
        with nc.Block() as blk:
            p.emit_block(blk)
        return es
    if True:
        aes = ExitStack()
        def asb(name, shape, dt):
            return aes.enter_context(nc.sbuf_tensor(name, shape, dt))
        wqT = asb("wqT", [128, 8 * 256], BF16)
        wkT = asb("wkT", [128, 8 * 256], BF16)
        wvT = asb("wvT", [128, 8 * 1024], BF16)
        woT = asb("woT", [128, 8 * 1024], BF16)
        S2f = asb("S2f", [128, NH * 128], F32)
        S2b = asb("S2b", [128, NH * 128], BF16)
        S1f = asb("S1f", [128, 256], F32)
        S1b = asb("S1b", [128, 256], BF16)
        S0c = asb("S0c", [128, 8], F32)
        xnch = asb("xnch", [128, 2 * 8 * CH], BF16)
        qf = [[asb(f"qf{z}_{i}", [128, CH], BF16) for i in range(2)] for z in range(2)]
        kev = [[asb(f"kev{z}_{i}", [128, CH], BF16) for i in range(2)] for z in range(2)]
        kod = [[asb(f"kod{z}_{i}", [128, CH], BF16) for i in range(2)] for z in range(2)]
        kfm = [asb(f"kfm_{i}", [128, CH], BF16) for i in range(2)]
        ktm = [[asb(f"ktm{z}_{i}", [128, 256], BF16) for i in range(2)] for z in range(2)]
        vtm = [[asb(f"vtm{z}_{i}", [128, 1024], BF16) for i in range(2)] for z in range(2)]
        kf2 = [[asb(f"kf2{z}_{i}", [128, 4096], BF16) for i in range(2)] for z in range(2)]
        qf2 = [asb(f"qf2_{z}", [128, NH * 512], BF16) for z in range(2)]
        qAe = asb("qAe", [128, 512], BF16)
        Am = [asb(f"Am_{z}", [128, NH * 384], BF16) for z in range(2)]
        ysq = asb("ysq", [128, 256], BF16)
        rstd = asb("rstd", [32, 256], F32)
        rstdb = asb("rstdb", [32, 256], BF16)
        rstdg = asb("rstdg", [128, 256], BF16)
        yrms = asb("yrms", [128, 8 * 256], BF16)
        sstg = asb("sstg", [128, 8 * 256], BF16)

        for dst, src_ in [(wqT, wqT_h), (wkT, wkT_h), (wvT, wvT_h), (woT, woT_h)]:
            DQ("attnw", lambda h, d=dst, s=src_: h.dma_start(
                d.rearrange("p (k c) -> p k c", k=8),
                s.rearrange("(k p) c -> p k c", p=128)))

        state_ready = {"vector": 0, "scalar": 0}
        kf2_guard = [0]
        s0_prev = [0]
        xn_lastuse = {}
        post_done = {0: 0, 1: 0}
        s_spill = {}
        pre_out = {}

        def emit_pre(ci):
            z = ci % 2
            blkb = ci % 2
            xoff = blkb * 8 * CH
            if ci >= 2:
                W("sync", "tensor", xn_lastuse[ci - 2])
            c_ld = DQ("xnch", lambda h, ci=ci, b=blkb: h.dma_start(
                xnch.rearrange("p (z pt c) -> p z pt c", z=2, pt=8)[:, b],
                xn_sc.rearrange("(pt p) (ci c) -> p ci pt c", p=128, ci=NCHUNK)[:, ci]))

            def xn_ap(pt, lo=0, hi=CH, xoff=xoff):
                base = xoff + pt * CH
                return xnch[:, base + lo:base + hi]

            pd_guard = post_done[z]
            use_ps(0); use_ps(1)
            WD("tensor", c_ld)
            WT("tensor", "attnw")
            W("tensor", "tensor", pd_guard)
            for mm_ in range(2):
                for k in range(8):
                    c_qmm = T(lambda h, k=k, mm_=mm_, f=xn_ap: h.matmul(
                        PS[0][:, mm_ * 256:(mm_ + 1) * 256],
                        wqT[:, k * 256 + mm_ * 128:k * 256 + (mm_ + 1) * 128],
                        f(k), start=(k == 0), stop=(k == 7)))
            for mm_ in range(2):
                for k in range(8):
                    c_kmm = T(lambda h, k=k, mm_=mm_, f=xn_ap: h.matmul(
                        PS[1][:, mm_ * 256:(mm_ + 1) * 256],
                        wkT[:, k * 256 + mm_ * 128:k * 256 + (mm_ + 1) * 128],
                        f(k), start=(k == 0), stop=(k == 7)))
            W("scalar", "tensor", c_qmm)
            c_qe = S(lambda h, z=z: h.copy(qf[z][0][:, :], PS[0][:, 0:256]))
            c_qe = S(lambda h, z=z: h.copy(qf[z][1][:, :], PS[0][:, 256:512]))
            guard_ps(0, "scalar", c_qe)
            W("vector", "tensor", c_kmm)
            for mm_ in range(2):
                V(lambda h, mm_=mm_, z=z: h.tensor_scalar(
                    kev[z][mm_][:, :], PS[1][:, mm_ * 256:(mm_ + 1) * 256], emaske[:, 0:1],
                    None, OP.mult))
                c_kp = V(lambda h, mm_=mm_, z=z: h.tensor_scalar(
                    kod[z][mm_][:, :], PS[1][:, mm_ * 256:(mm_ + 1) * 256], emasko[:, 0:1],
                    None, OP.mult))
            W("scalar", "tensor", c_kmm)
            W("scalar", "vector", c_kp)
            c_kf = S(lambda h: h.copy(kfm[0][:, :], PS[1][:, 0:256]))
            c_kf = S(lambda h: h.copy(kfm[1][:, :], PS[1][:, 256:512]))
            guard_ps(1, "scalar", c_kf)
            guard_ps(1, "vector", c_kp)

            ps_t = PS[0].bitcast(BF16)
            c_ktm = None
            for half in range(2):
                for mm_ in range(2):
                    use_ps(0)
                    W("tensor", "scalar", c_kf)
                    c_tr = T(lambda h, half=half, mm_=mm_: h.transpose(
                        ps_t[:, 0:128], kfm[mm_][:, half * 128:(half + 1) * 128], ident[:, :]))
                    W("vector", "tensor", c_tr)
                    c_ktm = V(lambda h, half=half, mm_=mm_, z=z: h.tensor_copy(
                        ktm[z][half][:, mm_ * 128:(mm_ + 1) * 128], ps_t[:, 0:128]))
                    guard_ps(0, "vector", c_ktm)

            c_vtm = None
            for half in range(2):
                use_ps(2); use_ps(3)
                for k in range(8):
                    T(lambda h, k=k, half=half, f=xn_ap: h.matmul(
                        PS[2][:, :], f(k, half * 128, half * 128 + 128),
                        wvT[:, k * 1024:k * 1024 + 512], start=(k == 0), stop=(k == 7)))
                for k in range(8):
                    c_v2 = T(lambda h, k=k, half=half, f=xn_ap: h.matmul(
                        PS[3][:, :], f(k, half * 128, half * 128 + 128),
                        wvT[:, k * 1024 + 512:(k + 1) * 1024], start=(k == 0), stop=(k == 7)))
                W("scalar", "tensor", c_v2)
                c_ve = S(lambda h, half=half, z=z: h.copy(vtm[z][half][:, 0:512], PS[2][:, :]))
                guard_ps(2, "scalar", c_ve)
                c_vtm = S(lambda h, half=half, z=z: h.copy(vtm[z][half][:, 512:1024], PS[3][:, :]))
                guard_ps(3, "scalar", c_vtm)

            c_kf2 = None
            for half in range(2):
                W("gpsimd", "vector", c_ktm)
                W("gpsimd", "tensor", pd_guard)
                a_ap = ktm[z][half].rearrange("p (hh i j) -> p hh i j", hh=NH, j=1).broadcast_to(
                    [128, NH, FD, FD])
                b_ap = ktm[z][half].rearrange("p (hh i j) -> p hh i j", hh=NH, i=1).broadcast_to(
                    [128, NH, FD, FD])
                c_kf2 = p.do("gpsimd", lambda h, half=half, a=a_ap, bb=b_ap, z=z: h.tensor_tensor(
                    kf2[z][half].rearrange("p (hh i j) -> p hh i j", hh=NH, i=FD), a, bb,
                    OP.mult), 1)

            c_qf2 = None
            for hh in range(NH):
                par = hh % 2
                pair = hh // 2
                tile = pair // 4
                rg = 32 * (pair % 4)
                pa, pb_ = (0, 1) if hh % 2 == 0 else (2, 3)
                use_ps(pa); use_ps(pb_)
                W("tensor", "scalar", c_qe)
                for Tt in range(2):
                    c_b = None
                    T(lambda h, par=par, Tt=Tt, tile=tile, rg=rg, pa=pa, z=z: h.matmul(
                        PS[pa][:, Tt * 256:(Tt + 1) * 256],
                        EA[par][rg:rg + 32, Tt * 128:(Tt + 1) * 128],
                        qf[z][tile][rg:rg + 32, :], start=True, stop=True, tile_position=(rg, 0)))
                    c_b = T(lambda h, par=par, Tt=Tt, tile=tile, rg=rg, pb_=pb_, z=z: h.matmul(
                        PS[pb_][:, Tt * 256:(Tt + 1) * 256],
                        EB[par][rg:rg + 32, Tt * 128:(Tt + 1) * 128],
                        qf[z][tile][rg:rg + 32, :], start=True, stop=True, tile_position=(rg, 0)))
                W("scalar", "tensor", c_b)
                if hh >= 1:
                    W("scalar", "vector", c_qf2)
                c_ae = S(lambda h, pa=pa: h.copy(qAe[:, :], PS[pa][:, :]))
                guard_ps(pa, "scalar", c_ae)
                W("vector", "scalar", c_ae)
                c_qf2 = V(lambda h, hh=hh, pb_=pb_, z=z: h.tensor_tensor(
                    qf2[z][:, hh * 512:(hh + 1) * 512], qAe[:, :], PS[pb_][:, :], OP.mult))
                guard_ps(pb_, "vector", c_qf2)

            c_am = None
            for hh in range(NH):
                par = hh % 2
                pair = hh // 2
                tile = pair // 4
                rg = 32 * (pair % 4)
                kpar = kev[z] if par == 0 else kod[z]
                pt_ = 0 if hh % 2 == 0 else 1
                use_ps(pt_)
                W("tensor", "vector", c_kp)
                T(lambda h, tile=tile, rg=rg, pt_=pt_, kpar=kpar, z=z: h.matmul(
                    PS[pt_][:, 0:256], kpar[tile][rg:rg + 32, 0:128],
                    qf[z][tile][rg:rg + 32, :], start=True, stop=True, tile_position=(rg, 0)))
                c_t2 = T(lambda h, tile=tile, rg=rg, pt_=pt_, kpar=kpar, z=z: h.matmul(
                    PS[pt_][:, 256:384], kpar[tile][rg:rg + 32, 128:256],
                    qf[z][tile][rg:rg + 32, 128:256], start=True, stop=True, tile_position=(rg, 0)))
                W("scalar", "tensor", c_t2)
                W("scalar", "tensor", pd_guard)
                c_pl = S(lambda h, hh=hh, pt_=pt_, z=z: h.activation(
                    Am[z][:, hh * 384:(hh + 1) * 384], PS[pt_][:, 0:384], AF.Square,
                    bias=4.0 / S32, scale=1.0 / S32))
                guard_ps(pt_, "scalar", c_pl)
                W("vector", "scalar", c_pl)
                c_m1 = V(lambda h, hh=hh, z=z: h.scalar_tensor_tensor(
                    Am[z][:, hh * 384:hh * 384 + 128], Am[z][:, hh * 384:hh * 384 + 128],
                    0.5, triu[:, :], OP.add, OP.mult))
                W("vector", "vector", c_m1)
                c_m2 = V(lambda h, hh=hh, z=z: h.tensor_scalar(
                    Am[z][:, hh * 384 + 128:hh * 384 + 256],
                    Am[z][:, hh * 384 + 128:hh * 384 + 256], 0.5, None, OP.add))
                W("vector", "vector", c_m2)
                c_am = V(lambda h, hh=hh, z=z: h.scalar_tensor_tensor(
                    Am[z][:, hh * 384 + 256:hh * 384 + 384],
                    Am[z][:, hh * 384 + 256:hh * 384 + 384],
                    0.5, triu[:, :], OP.add, OP.mult))
            xn_lastuse[ci] = p.cnt["tensor"]
            pre_out[ci] = {"am": c_am, "qf2": c_qf2, "vtm": c_vtm, "kf2": c_kf2,
                           "ktm": c_ktm, "kp": c_kp}

        def emit_post(ci):
            z = ci % 2
            pre = pre_out[ci]
            if ci % (SEQ // CH) == 0:
                if ci > 0:
                    W("vector", "tensor", p.cnt["tensor"])
                    W("vector", "scalar", state_ready["scalar"])
                    W("vector", "vector", max(state_ready["vector"], s0_prev[0]))
                for zz in (S2f, S1f, S0c, S2b, S1b):
                    c_z = V(lambda h, zz=zz: h.memset(zz[:, :], 0.0))
                state_ready["vector"] = c_z

            W("tensor", "vector", state_ready["vector"])
            W("tensor", "scalar", state_ready["scalar"])
            W("tensor", "vector", pre["am"])
            W("tensor", "vector", pre["qf2"])
            W("tensor", "scalar", pre["vtm"])
            stats_first = True
            c_st = None
            ysb_guard = 0
            for pr in range(8):
                py = 4 if pr % 2 == 0 else 5
                use_ps(py)
                tile = pr // 4
                rg = 32 * (pr % 4)
                c_yl = None
                for hi in range(2):
                    hh = 2 * pr + hi
                    eb = hi * 64
                    sbase = 128 * (pr // 4) + 64 * hi
                    T(lambda h, tile=tile, rg=rg, eb=eb, sbase=sbase, py=py, z=z: h.matmul(
                        PS[py][eb:eb + 64, 0:256], S1b[rg:rg + 32, sbase:sbase + 64],
                        qf[z][tile][rg:rg + 32, :], start=True, stop=False,
                        tile_position=(rg, eb)))
                    T(lambda h, hh=hh, eb=eb, py=py, z=z: h.matmul(
                        PS[py][eb:eb + 64, 0:256], vtm[z][0][:, hh * 64:(hh + 1) * 64],
                        Am[z][:, hh * 384:hh * 384 + 256], start=False, stop=False))
                    T(lambda h, hh=hh, eb=eb, py=py, z=z: h.matmul(
                        PS[py][eb:eb + 64, 128:256], vtm[z][1][:, hh * 64:(hh + 1) * 64],
                        Am[z][:, hh * 384 + 256:hh * 384 + 384], start=False, stop=False))
                    for Tt in range(2):
                        c_yl = T(lambda h, hh=hh, eb=eb, Tt=Tt, py=py, z=z,
                                 lst=(Tt == 1): h.matmul(
                            PS[py][eb:eb + 64, 0:256],
                            S2b[:, (2 * hh + Tt) * 64:(2 * hh + Tt + 1) * 64],
                            qf2[z][:, (2 * hh + Tt) * 256:(2 * hh + Tt + 1) * 256],
                            start=False, stop=lst))
                W("scalar", "tensor", c_yl)
                W("scalar", "vector", state_ready["vector"])
                if not stats_first:
                    W("scalar", "tensor", c_st)
                c_ysq = S(lambda h, pr=pr, py=py: h.activation(
                    ysq[:, :], PS[py][:, 0:256], AF.Square, bias=S0c[:, pr:pr + 1]))
                W("scalar", "scalar", c_ysq)
                c_ysb = S(lambda h, pr=pr, py=py: h.activation(
                    yrms[:, pr * 256:(pr + 1) * 256], PS[py][:, 0:256], AF.Identity,
                    bias=S0c[:, pr:pr + 1]))
                guard_ps(py, "scalar", c_ysb)
                ysb_guard = c_ysb
                if stats_first:
                    use_ps(6)
                W("tensor", "scalar", c_ysq)
                c_st = T(lambda h, pr=pr, s=stats_first: h.matmul(
                    PS[6][0:32, 0:256], bo[:, pr * 32:(pr + 1) * 32], ysq[:, :],
                    start=s, stop=(pr == 7)))
                stats_first = False
            W("vector", "tensor", c_st)
            c_z2 = V(lambda h: h.tensor_scalar(rstd[:, :], PS[6][0:32, 0:256], 1.0 / HD,
                                               RMS_EPS, OP.mult, OP.add))
            guard_ps(6, "vector", c_z2)
            W("scalar", "vector", c_z2)
            c_l = S(lambda h: h.activation(rstd[:, :], rstd[:, :], AF.Ln))
            W("scalar", "scalar", c_l)
            c_e = S(lambda h: h.activation(rstdb[:, :], rstd[:, :], AF.Exp, scale=-0.5))
            c_yrm = None
            for pr in range(8):
                use_ps(7)
                W("tensor", "scalar", c_e)
                W("tensor", "scalar", ysb_guard)
                c_bc = T(lambda h, pr=pr: h.matmul(
                    PS[7][:, 0:256], sel[:, pr * 128:(pr + 1) * 128], rstdb[:, :],
                    start=True, stop=True))
                W("vector", "tensor", c_bc)
                if c_yrm is not None:
                    W("vector", "vector", c_yrm)
                c_rg = V(lambda h: h.tensor_scalar(rstdg[:, :], PS[7][:, 0:256],
                                                   gwcol[:, 0:1], None, OP.mult))
                guard_ps(7, "vector", c_rg)
                W("vector", "vector", c_rg)
                W("vector", "scalar", ysb_guard)
                c_yrm = V(lambda h, pr=pr: h.tensor_tensor(
                    yrms[:, pr * 256:(pr + 1) * 256], yrms[:, pr * 256:(pr + 1) * 256],
                    rstdg[:, :], OP.mult))

            c_se = None
            for m in range(8):
                po = 6 if m % 2 == 0 else 7
                use_ps(po)
                W("tensor", "vector", c_yrm)
                for k in range(8):
                    c_o = T(lambda h, m=m, k=k, po=po: h.matmul(
                        PS[po][:, 0:256], woT[:, k * 1024 + m * 128:k * 1024 + (m + 1) * 128],
                        yrms[:, k * 256:(k + 1) * 256], start=(k == 0), stop=(k == 7)))
                W("scalar", "tensor", c_o)
                if ci >= 1:
                    WD("scalar", s_spill[ci - 1])
                c_se = S(lambda h, m=m, po=po: h.copy(
                    sstg[:, m * 256:(m + 1) * 256], PS[po][:, 0:256]))
                guard_ps(po, "scalar", c_se)
            W("gpsimd", "scalar", c_se)
            s_spill[ci] = DQ("spill_s", lambda h, ci=ci: h.dma_start(
                s_sc.rearrange("(pt p) (ci c) -> p ci pt c", p=128, ci=NCHUNK)[:, ci],
                sstg.rearrange("p (pt c) -> p pt c", pt=8)))

            if ci % (SEQ // CH) == (SEQ // CH) - 1:
                post_done[z] = p.cnt["tensor"]
                return
            c_s2cp = None
            for grp in range(4):
                pg_ = 4 if grp % 2 == 0 else 5
                use_ps(pg_)
                W("tensor", "gpsimd", pre["kf2"])
                for gi in range(4):
                    hh = grp * 4 + gi
                    for Tt in range(2):
                        for half in range(2):
                            c_u = T(lambda h, hh=hh, Tt=Tt, half=half, gi=gi, pg_=pg_,
                                    z=z: h.matmul(
                                PS[pg_][:, (gi * 2 + Tt) * 64:(gi * 2 + Tt + 1) * 64],
                                kf2[z][half][:, hh * 256 + Tt * 128:hh * 256 + (Tt + 1) * 128],
                                vtm[z][half][:, hh * 64:(hh + 1) * 64],
                                start=(half == 0), stop=(half == 1)))
                W("vector", "tensor", c_u)
                c_ad = V(lambda h, grp=grp, pg_=pg_: h.tensor_tensor(
                    S2f[:, grp * 512:(grp + 1) * 512], S2f[:, grp * 512:(grp + 1) * 512],
                    PS[pg_][:, :], OP.add))
                guard_ps(pg_, "vector", c_ad)
                W("scalar", "vector", c_ad)
                c_s2cp = S(lambda h, grp=grp: h.copy(S2b[:, grp * 512:(grp + 1) * 512],
                                                     S2f[:, grp * 512:(grp + 1) * 512]))
            use_ps(6)
            W("tensor", "vector", pre["ktm"])
            for pr in range(8):
                tile = pr // 4
                rg = 32 * (pr % 4)
                for half in range(2):
                    c_s1 = T(lambda h, pr=pr, half=half, tile=tile, rg=rg, z=z: h.matmul(
                        PS[6][rg:rg + 32, tile * 128:(tile + 1) * 128],
                        ktm[z][half][:, pr * 32:(pr + 1) * 32],
                        vtm[z][half][:, pr * 128:(pr + 1) * 128],
                        start=(half == 0), stop=(half == 1), tile_position=(0, rg)))
            W("vector", "tensor", c_s1)
            c_s1a = V(lambda h: h.tensor_tensor(S1f[:, :], S1f[:, :], PS[6][:, 0:256], OP.add))
            guard_ps(6, "vector", c_s1a)
            W("vector", "vector", c_s1a)
            c_s1b = V(lambda h: h.tensor_tensor(S1b[:, :], S1f[:, :], s1mask[:, :], OP.mult))
            use_ps(7)
            for pr in range(8):
                for half in range(2):
                    c_s0 = T(lambda h, pr=pr, half=half, z=z: h.matmul(
                        PS[7][:, (pr * 2 + half):(pr * 2 + half + 1)],
                        vtm[z][half][:, pr * 128:(pr + 1) * 128], onescolb[:, 0:1],
                        start=True, stop=True))
            W("vector", "tensor", c_s0)
            W("vector", "scalar", ysb_guard)
            W("vector", "vector", s0_prev[0])
            c_s0a = V(lambda h: h.tensor_tensor(
                S0c[:, :], S0c[:, :],
                PS[7][:, 0:16].rearrange("p (a b) -> p a b", a=8)[:, :, 0:1].rearrange(
                    "p a b -> p (a b)"), OP.add))
            W("vector", "vector", c_s0a)
            c_s0b = V(lambda h: h.tensor_tensor(
                S0c[:, :], S0c[:, :],
                PS[7][:, 0:16].rearrange("p (a b) -> p a b", a=8)[:, :, 1:2].rearrange(
                    "p a b -> p (a b)"), OP.add))
            guard_ps(7, "vector", c_s0b)
            s0_prev[0] = c_s0b
            state_ready["vector"] = c_s1b if c_s0b < c_s1b else c_s0b
            state_ready["scalar"] = c_s2cp
            post_done[z] = p.cnt["tensor"]

        emit_pre(0)
        for ci in range(NCHUNK):
            if ci + 1 < NCHUNK:
                emit_pre(ci + 1)
            emit_post(ci)
        c_slast = s_spill[NCHUNK - 1]
        p.barrier()
        aes.close()

    if True:
        ces = ExitStack()
        def csb(name, shape, dt):
            return ces.enter_context(nc.sbuf_tensor(name, shape, dt))
        wdTq = csb("wdTq", [128, 22 * 1024], BF16)
        rh = csb("rh", [128, 3 * 22 * 512], BF16)
        uhl = csb("uhl", [128, 4 * 512], F32)
        gbuf = csb("gbuf", [128, 8 * 512], BF16)
        sgbuf = csb("sgbuf", [128, 512], F32)
        xb2 = csb("xb2", [128, 2 * 8 * 512], F32)
        sld = csb("sld", [128, 8 * 512], BF16)
        outb = csb("outb", [128, 2 * 512], F32)

        WD("sync", t_wdq)
        t_wdl = DQ("wdl", lambda h: h.dma_start(wdTq[:, :], wd_sc))
        WT("sync", "spill_uh")
        WT("sync", "spill_s")
        qg_v = {}
        qg_g = {}
        dn_guard = {0: 0, 1: 0}
        prev_cb = [0]
        comb_marks = {}
        dn_last = {}
        out_marks = {}
        th_marks = {}
        rq_last = {}

        def emit_quant(tt):
            rb_ = tt % 3
            rq = {}
            for k in range(22):
                ub = (tt * 22 + k) % 4
                qb = (tt * 22 + k) % 2
                gi = tt * 22 + k
                if gi >= 4:
                    W("sync", "scalar", th_marks[gi - 4])
                t_lu = DQ("uhl", lambda h, k=k, tt=tt, b=ub: h.dma_start(
                    uhl[:, b * 512:(b + 1) * 512],
                    uh_sc[k * 128:(k + 1) * 128, tt * 512:(tt + 1) * 512]))
                WD("scalar", t_lu)
                W("scalar", "vector", c_sd)
                if gi >= 2:
                    W("scalar", "vector", rq_last.get(gi - 2, 0))
                c_th = S(lambda h, b=ub, qb=qb: h.activation(
                    qth2[:, qb * 512:(qb + 1) * 512], uhl[:, b * 512:(b + 1) * 512],
                    AF.Identity, bias=0.5, scale=invsd_col[:, 0:1]))
                th_marks[gi] = c_th
                W("vector", "scalar", c_th)
                if gi >= 2:
                    W("vector", "vector", rq_last.get(gi - 2, 0))
                if k == 0 and tt >= 3:
                    W("vector", "tensor", dn_last[tt - 3])
                c_cl = V(lambda h, qb=qb: h.tensor_scalar(
                    qth2[:, qb * 512:(qb + 1) * 512], qth2[:, qb * 512:(qb + 1) * 512],
                    127.9, -127.9, OP.min, OP.max))
                W("scalar", "vector", c_cl)
                c_i = S(lambda h, qb=qb: h.copy(
                    qi32[:, qb * 512:(qb + 1) * 512], qth2[:, qb * 512:(qb + 1) * 512]))
                W("vector", "scalar", c_i)
                c_g = V(lambda h, qb=qb: h.tensor_tensor(
                    qgt2[:, qb * 512:(qb + 1) * 512], qi32[:, qb * 512:(qb + 1) * 512],
                    qth2[:, qb * 512:(qb + 1) * 512], OP.is_gt))
                qg_v[gi] = c_g
                W("vector", "vector", c_g)
                rq[k] = V(lambda h, k=k, qb=qb, rb_=rb_: h.scalar_tensor_tensor(
                    rh[:, (rb_ * 22 + k) * 512:(rb_ * 22 + k + 1) * 512],
                    qgt2[:, qb * 512:(qb + 1) * 512], -1.0,
                    qi32[:, qb * 512:(qb + 1) * 512], OP.mult, OP.add))
                rq_last[gi] = rq[k]
            return rq

        def emit_down(tt, rq_of):
            ob = tt % 2
            rb_ = tt % 3
            if tt >= 2:
                W("sync", "vector", comb_marks[tt - 2])
            t_lx = DQ("xA", lambda h, tt=tt, b=ob: h.dma_start(
                xb2.rearrange("p (z pt c) -> p z pt c", z=2, pt=8)[:, b],
                x_h.rearrange("(pt p) (tt c) -> p tt pt c", p=128, tt=8)[:, tt]))
            if tt >= 1:
                W("sync", "vector", comb_marks[tt - 1])
            t_ls = DQ("cls", lambda h, tt=tt: h.dma_start(
                sld.rearrange("p (pt c) -> p pt c", pt=8),
                s_sc.rearrange("(pt p) (tt c) -> p tt pt c", p=128, tt=8)[:, tt]))
            WD("tensor", t_wdl)
            for hm in range(8):
                use_ps(hm)
            for k in range(22):
                W("tensor", "vector", rq_of[k])
                for hm in range(8):
                    c_d = T(lambda h, hm=hm, k=k, rb_=rb_: h.matmul(
                        PS[hm][:, :], wdTq[:, k * 1024 + hm * 128:k * 1024 + (hm + 1) * 128],
                        rh[:, (rb_ * 22 + k) * 512:(rb_ * 22 + k + 1) * 512],
                        start=(k == 0), stop=(k == 21)))
            for hm in range(8):
                W("scalar", "tensor", c_d)
                if tt >= 1:
                    W("scalar", "vector", comb_marks[tt - 1])
                c_gv = S(lambda h, hm=hm: h.activation(
                    gbuf[:, hm * 512:(hm + 1) * 512], PS[hm][:, :], AF.Silu,
                    scale=scd[:, hm:hm + 1]))
                guard_ps(hm, "scalar", c_gv)
                W("vector", "scalar", c_gv)
                WD("vector", t_lx)
                WD("vector", t_ls)
                W("vector", "vector", prev_cb[0])
                c_sg = V(lambda h, hm=hm: h.tensor_tensor(
                    sgbuf[:, :], sld[:, hm * 512:(hm + 1) * 512],
                    gbuf[:, hm * 512:(hm + 1) * 512], OP.mult))
                W("vector", "vector", c_sg)
                gi2 = tt * 8 + hm
                if gi2 >= 2:
                    WD("vector", out_marks[gi2 - 2])
                c_cb = V(lambda h, hm=hm, b=ob, s2=gi2 % 2: h.tensor_tensor(
                    outb[:, s2 * 512:(s2 + 1) * 512],
                    xb2[:, (b * 8 + hm) * 512:(b * 8 + hm) * 512 + 512], sgbuf[:, :], OP.add))
                prev_cb[0] = c_cb
                W("sync", "vector", c_cb)
                out_marks[gi2] = DQ("out", lambda h, tt=tt, hm=hm, s2=gi2 % 2: h.dma_start(
                    o_out.rearrange("(pt p) (tt c) -> p pt tt c", p=128, tt=8)[:, hm, tt],
                    outb[:, s2 * 512:(s2 + 1) * 512]))
            dn_last[tt] = c_d
            dn_guard[rb_] = c_d
            comb_marks[tt] = c_cb

        rq_pend = emit_quant(0)
        for tt in range(8):
            nxt = emit_quant(tt + 1) if tt + 1 < 8 else None
            emit_down(tt, rq_pend)
            rq_pend = nxt
        WT("sync", "out")

        with nc.Block() as blk:
            p.emit_block(blk)
        ces.close()
    return es


def _stage_constants():
    bf = ml_dtypes.bfloat16
    c = {}
    c["triu_h"] = np.triu(np.ones((128, 128), np.float32)).astype(bf)
    c["corrA_h"] = np.concatenate([0.5 * np.triu(np.ones((128, 128), np.float32)),
                                   0.5 * np.ones((128, 128), np.float32)], axis=1).astype(bf)
    c["corrB_h"] = (0.5 * np.triu(np.ones((128, 128), np.float32))).astype(bf)
    for par in range(2):
        ea = np.zeros((32, 256), np.float32)
        eb = np.zeros((32, 256), np.float32)
        for Tt in range(2):
            for il in range(8):
                for j in range(16):
                    col = Tt * 128 + il * 16 + j
                    i = il + 8 * Tt
                    ea[16 * par + i, col] = 0.25
                    eb[16 * par + j, col] = 0.125
        c[f"EA{par}_h"] = np.tile(ea, (4, 1)).astype(bf)
        c[f"EB{par}_h"] = np.tile(eb, (4, 1)).astype(bf)
    m = np.zeros((128, 256), np.float32)
    for a in range(4):
        for f in range(32):
            for cc_ in range(2):
                if (f < 16) == (cc_ == 0):
                    for b in range(2):
                        m[32 * a + f, 128 * b + 64 * cc_:128 * b + 64 * cc_ + 64] = 0.25
    c["s1mask_h"] = m
    bov = np.zeros((128, 256), np.float32)
    for pr in range(8):
        bov[0:64, pr * 32 + 2 * pr] = 1.0
        bov[64:128, pr * 32 + 2 * pr + 1] = 1.0
    c["bo_h"] = bov.astype(bf)
    selm = np.zeros((32, 1024), np.float32)
    for pr in range(8):
        selm[2 * pr, pr * 128:pr * 128 + 64] = 1.0
        selm[2 * pr + 1, pr * 128 + 64:pr * 128 + 128] = 1.0
    c["sel_h"] = selm.astype(bf)
    pm = np.arange(128) % 32
    c["emaske_h"] = (pm < 16).astype(np.float32).reshape(128, 1)
    c["emasko_h"] = (pm >= 16).astype(np.float32).reshape(128, 1)
    c["onescol_h"] = np.ones((128, 1), np.float32)
    c["onescolb_h"] = np.ones((128, 1), np.float32).astype(bf)
    c["ones1r_h"] = np.ones((1, 128), np.float32)
    c["ident_h"] = np.eye(128, dtype=np.float32).astype(bf)
    cv = np.zeros((128, 8), np.float32)
    for j, v in enumerate([1e-5, 0.5, 4.0 / math.sqrt(32.0), 1.0 / math.sqrt(32.0), -0.5, 1.0]):
        cv[:, j] = v
    c["cvals_h"] = cv
    return c


def kernel(L0_attn_hidden_states, L0_attn_Wq, L0_attn_Wk, L0_attn_Wv, L0_attn_Wo,
           L0_attn_g_weight, L0_ffn_clip_x, L0_ffn_clip_down, L0_ffn_w_gate,
           L0_ffn_w_up, L0_ffn_w_down):
    x = np.asarray(L0_attn_hidden_states, np.float32)
    B, L, _ = x.shape
    x_fm = np.ascontiguousarray(x.reshape(B * L, D).T)

    shared = {
        "wqT_h": np.ascontiguousarray(np.asarray(L0_attn_Wq, np.float32).T),
        "wkT_h": np.ascontiguousarray(np.asarray(L0_attn_Wk, np.float32).T),
        "wvT_h": np.ascontiguousarray(np.asarray(L0_attn_Wv, np.float32).T),
        "woT_h": np.ascontiguousarray(np.asarray(L0_attn_Wo, np.float32).T),
        "gw_h": np.asarray(L0_attn_g_weight, np.float32).reshape(1, HD),
        "clipx_h": np.asarray(L0_ffn_clip_x, np.float32).reshape(1, 1),
        "clipd_h": np.asarray(L0_ffn_clip_down, np.float32).reshape(1, 1),
        "wg_nat_h": np.ascontiguousarray(np.asarray(L0_ffn_w_gate, np.float32)),
        "wu_nat_h": np.ascontiguousarray(np.asarray(L0_ffn_w_up, np.float32)),
        "wd_nat_h": np.ascontiguousarray(np.asarray(L0_ffn_w_down, np.float32)),
        "wgT_h": np.ascontiguousarray(np.asarray(L0_ffn_w_gate, np.float32).T),
        "wuT_h": np.ascontiguousarray(np.asarray(L0_ffn_w_up, np.float32).T),
        "wdT_h": np.ascontiguousarray(np.asarray(L0_ffn_w_down, np.float32).T),
    }
    shared.update(_stage_constants())

    nc = bass.Bass("TRN2", target_bir_lowering=False, debug=False)
    es = build(nc)

    in_maps = []
    for c in range(8):
        mm = dict(shared)
        mm["x_h"] = np.ascontiguousarray(x_fm[:, c * NT:(c + 1) * NT])
        in_maps.append(mm)
    res = run_bass_kernel_spmd(nc, in_maps, core_ids=list(range(8)))
    es.close()

    out_fm = np.concatenate([res.results[c]["o_out"] for c in range(8)], axis=1)
    out = np.ascontiguousarray(out_fm.T).reshape(B, L, D).astype(np.float32)
    return out
